# version 12
# baseline (speedup 1.0000x reference)
import numpy as np
from contextlib import ExitStack
import concourse.bass as bass
import concourse.mybir as mybir

F32 = mybir.dt.float32
BF16 = mybir.dt.bfloat16
AF = mybir.ActivationFunctionType
ALU = mybir.AluOpType
AX = mybir.AxisListType

ENGINES = ("sync", "tensor", "vector", "scalar", "gpsimd")


def _overlap(t1, t2):
    if t1 == "*" or t2 == "*":
        return True
    r1, r2 = isinstance(t1, tuple), isinstance(t2, tuple)
    if r1 and r2:
        return t1[0] < t2[1] and t2[0] < t1[1]
    if r1 or r2:
        return False
    return t1 == t2


def _contains(t1, t2):
    if t1 == "*":
        return True
    if isinstance(t1, tuple) and isinstance(t2, tuple):
        return t1[0] <= t2[0] and t2[1] <= t1[1]
    return t1 == t2


class Op:
    __slots__ = ("idx", "eng", "fn", "deps", "dma", "tag",
                 "marked", "inc_idx", "dma_idx", "waits")

    def __init__(self, idx, eng, fn, dma, tag):
        self.idx = idx
        self.eng = eng
        self.fn = fn
        self.dma = dma
        self.tag = tag
        self.deps = set()
        self.marked = False
        self.inc_idx = None
        self.dma_idx = None
        self.waits = []


class _NameState:
    __slots__ = ("w", "r")

    def __init__(self):
        self.w = {}
        self.r = {}


class Rec:
    def __init__(self, nc):
        self.nc = nc
        self.ops = []
        self.names = {}

    def _access(self, o, name, tag, iswrite):
        st = self.names.setdefault(name, _NameState())
        if iswrite:
            for t2, wop in st.w.items():
                if _overlap(tag, t2):
                    o.deps.add(wop)
            for t2, rl in st.r.items():
                if _overlap(tag, t2):
                    for rr in rl:
                        o.deps.add(rr)
            for t2 in [t for t in st.w if _contains(tag, t) and t != tag]:
                del st.w[t2]
            for t2 in [t for t in st.r if _contains(tag, t)]:
                del st.r[t2]
            st.w[tag] = o.idx
        else:
            for t2, wop in st.w.items():
                if _overlap(tag, t2):
                    o.deps.add(wop)
            st.r.setdefault(tag, []).append(o.idx)

    def op(self, eng, fn, reads=(), writes=(), dma=False, tag="", deps=()):
        o = Op(len(self.ops), eng, fn, dma, tag)
        for n, t in reads:
            self._access(o, n, t, False)
        for n, t in writes:
            self._access(o, n, t, True)
        for d in deps:
            o.deps.add(d.idx)
        o.deps.discard(o.idx)
        self.ops.append(o)
        return o

    def barrier(self, tag="barrier"):
        last = {}
        for o in self.ops:
            last[o.eng] = o
        outs = []
        for e in ENGINES:
            if e not in last:
                continue
            o = Op(len(self.ops), e, None, False, tag)
            for e2, lo in last.items():
                o.deps.add(lo.idx)
            self.ops.append(o)
            outs.append(o)
        return outs

    def build(self, es):
        nc = self.nc
        esem = {e: es.enter_context(nc.semaphore(f"S{e}")) for e in ENGINES}
        dsem = {e: es.enter_context(nc.semaphore(f"D{e}")) for e in ENGINES}
        for o in self.ops:
            for d in o.deps:
                a = self.ops[d]
                if a.dma:
                    a.marked = True
                elif a.eng == "tensor" and o.eng == "tensor":
                    pass
                else:
                    a.marked = True
        cnt = {e: 0 for e in ENGINES}
        dcnt = {e: 0 for e in ENGINES}
        for o in self.ops:
            if o.dma:
                o.dma_idx = dcnt[o.eng]
                dcnt[o.eng] += 1
            elif o.marked:
                cnt[o.eng] += 1
                o.inc_idx = cnt[o.eng]
        ep_of = {}
        ep_end = {}
        cur_ep = {e: 0 for e in ENGINES}
        cur_cnt = {e: 0 for e in ENGINES}
        open_ep = {e: False for e in ENGINES}
        fence = {}
        for o in self.ops:
            for d in o.deps:
                a = self.ops[d]
                if a.dma and a.idx in ep_of:
                    q2 = a.eng
                    if open_ep[q2] and ep_of[a.idx] == (q2, cur_ep[q2]):
                        open_ep[q2] = False
            if o.dma:
                q = o.eng
                if not open_ep[q]:
                    cur_ep[q] += 1
                    open_ep[q] = True
                    if cur_cnt[q] > 0:
                        fence[o.idx] = 16 * cur_cnt[q]
                ep_of[o.idx] = (q, cur_ep[q])
                cur_cnt[q] += 1
                ep_end[(q, cur_ep[q])] = cur_cnt[q]
        wm_after = [None] * len(self.ops)
        last_on_engine = {}
        import os as _os
        _noprune = bool(_os.environ.get("REC_NOPRUNE"))
        for o in self.ops:
            wm = {} if _noprune else dict(last_on_engine.get(o.eng) or {})
            waits = {}
            if o.idx in fence:
                k = ("D", o.eng)
                v = fence[o.idx]
                if wm.get(k, 0) < v:
                    waits[k] = v
            for d in sorted(o.deps):
                a = self.ops[d]
                if a.dma:
                    k = ("D", a.eng)
                    v = 16 * ep_end[ep_of[a.idx]]
                elif a.eng == "tensor" and o.eng == "tensor":
                    continue
                else:
                    k = ("S", a.eng)
                    v = a.inc_idx
                if wm.get(k, 0) < v:
                    waits[k] = max(waits.get(k, 0), v)
            for d in sorted(o.deps):
                aw = wm_after[self.ops[d].idx]
                if aw:
                    for k2, v2 in aw.items():
                        if v2 > wm.get(k2, 0):
                            wm[k2] = v2
            for k, v in waits.items():
                if v > wm.get(k, 0):
                    wm[k] = v
            o.waits = [(k, v) for k, v in waits.items()]
            wm_after[o.idx] = wm
            last_on_engine[o.eng] = wm
        self._esem = esem
        self._dsem = dsem
        self.final_counts = dict(cnt)
        self.final_dma = {e: 16 * dcnt[e] for e in ENGINES}

    def check(self):
        streams = {e: [o for o in self.ops if o.eng == e] for e in ENGINES}
        pos = {e: 0 for e in ENGINES}
        sv = {}
        progress = True
        while progress:
            progress = False
            for e in ENGINES:
                while pos[e] < len(streams[e]):
                    o = streams[e][pos[e]]
                    if all(sv.get(k, 0) >= v for k, v in o.waits):
                        if o.dma:
                            sv[("D", e)] = sv.get(("D", e), 0) + 16
                        elif o.marked:
                            sv[("S", e)] = sv.get(("S", e), 0) + 1
                        pos[e] += 1
                        progress = True
                    else:
                        break
        stuck = {e: pos[e] < len(streams[e]) for e in ENGINES}
        if any(stuck.values()):
            print("DEADLOCK at build-check:")
            for e in ENGINES:
                if stuck[e]:
                    o = streams[e][pos[e]]
                    bad = [(k, v, sv.get(k, 0)) for k, v in o.waits if sv.get(k, 0) < v]
                    print(f"  {e}[{pos[e]}] op#{o.idx} tag={o.tag!r} dma={o.dma} waits: {bad}")
                    print(f"    deps: {sorted(o.deps)[-8:]}")
            return False
        return True

    def emit(self, block):
        esem, dsem = self._esem, self._dsem
        streams = {e: [o for o in self.ops if o.eng == e] for e in ENGINES}

        def mk(e):
            def body(engobj):
                for o in streams[e]:
                    for (kind, eng2), v in o.waits:
                        engobj.wait_ge(dsem[eng2] if kind == "D" else esem[eng2], v)
                    if o.fn is None:
                        continue
                    inst = o.fn(engobj)
                    if o.dma:
                        inst.then_inc(dsem[e], 16)
                    elif o.marked:
                        inst.then_inc(esem[e], 1)
            return body

        for e in ENGINES:
            if streams[e]:
                getattr(block, e)(mk(e))

    def stats(self):
        from collections import Counter
        c = Counter(o.eng for o in self.ops)
        m = Counter(o.eng for o in self.ops if o.marked)
        nw = sum(len(o.waits) for o in self.ops)
        return dict(total=len(self.ops), per_engine=dict(c), marked=dict(m), waits=nw)




class SB:
    __slots__ = ("ar", "off", "cols", "dt", "esz")

    def __init__(self, ar, off, cols, dt):
        self.ar = ar
        self.off = off
        self.cols = cols
        self.dt = dt
        self.esz = 2 if dt == BF16 else 4

    def __getitem__(self, idx):
        ps, cs = idx
        h = self.ar.typed[self.dt]
        base = self.off // self.esz
        a, b, stp = cs.indices(self.cols)
        assert stp == 1
        return h[ps, base + a: base + b]

    def ap3(self, ps, pattern, **kw):
        return self[ps, 0:self.cols].rearrange(pattern, **kw)

    def k(self, a=0, b=None):
        if b is None:
            b = self.cols
        lo = self.off + a * self.esz
        hi = self.off + b * self.esz
        if self.ar.is_psum:
            lo = (lo // 2048) * 2048
            hi = ((hi + 2047) // 2048) * 2048
        return (self.ar.name, (lo, hi))

    @property
    def K(self):
        return self.k()


class Arena:
    def __init__(self, nc, es, name, byts, space="sbuf", base_partition=0):
        self.name = name
        self.nbytes = byts
        self.is_psum = (space == "psum")
        if space == "sbuf":
            t = es.enter_context(nc.sbuf_tensor(name, [128, byts // 4], F32))
        else:
            t = es.enter_context(nc.psum_tensor(name, [128, byts // 4], F32))
        self.typed = {F32: t, BF16: t.bitcast(BF16)}
        self.off = 0
        self.marks = []

    def alloc(self, cols, dt=F32):
        esz = 2 if dt == BF16 else 4
        off = (self.off + 3) & ~3
        nb = cols * esz
        assert off + nb <= self.nbytes, f"{self.name} overflow: {off}+{nb} > {self.nbytes}"
        self.off = off + nb
        return SB(self, off, cols, dt)

    def at(self, off, cols, dt=F32):
        return SB(self, off, cols, dt)

    def mark(self):
        self.marks.append(self.off)

    def release(self):
        self.off = self.marks.pop()


class Ctx:
    def __init__(self, nc, rec, es, sbuf_bytes=212824):
        self.nc = nc
        self.rec = rec
        self.es = es
        self.A = Arena(nc, es, "AR", sbuf_bytes, "sbuf")
        self.P = Arena(nc, es, "PS", 16 * 1024, "psum")
        self._uid = 0

    def sb(self, cols, dt=F32):
        return self.A.alloc(cols, dt)

    def pbank(self, i, dt=F32, banks=1):
        return self.P.at(i * 2048, banks * 2048 // (2 if dt == BF16 else 4), dt)

    def setup_constants(self):
        r = self.rec
        self.ident = self.sb(128, BF16)
        self.tri = self.sb(128, BF16)
        self.ones_col = self.sb(1, BF16)
        self.ones_colf = self.sb(1, F32)
        self.eps_col = self.sb(1, F32)
        self.ones_row = self.sb(128, BF16)

        def aps(sbv, dt):
            return sbv[:, 0:128]

        r.op("gpsimd", lambda g: g.memset(self.ident[:, 0:128], 0.0), writes=[self.ident.K])
        r.op("gpsimd", lambda g: g.affine_select(
            out=self.ident[:, 0:128], in_=self.ident[:, 0:128], compare_op=ALU.not_equal,
            fill=1.0, base=0, pattern=[[-1, 128]], channel_multiplier=1),
            reads=[self.ident.K], writes=[self.ident.K])
        r.op("gpsimd", lambda g: g.memset(self.tri[:, 0:128], 0.0), writes=[self.tri.K])
        r.op("gpsimd", lambda g: g.affine_select(
            out=self.tri[:, 0:128], in_=self.tri[:, 0:128], compare_op=ALU.is_gt,
            fill=1.0, base=0, pattern=[[-1, 128]], channel_multiplier=1),
            reads=[self.tri.K], writes=[self.tri.K])
        r.op("gpsimd", lambda g: g.memset(self.ones_col[:, 0:1], 1.0), writes=[self.ones_col.K])
        r.op("gpsimd", lambda g: g.memset(self.ones_colf[:, 0:1], 1.0), writes=[self.ones_colf.K])
        r.op("gpsimd", lambda g: g.memset(self.eps_col[:, 0:1], 1e-5), writes=[self.eps_col.K])
        r.op("gpsimd", lambda g: g.memset(self.ones_row[:, 0:128], 1.0), writes=[self.ones_row.K])


def emit_ln_ufm(ctx, R, ufm, T=2048, D=1024, tiles=None, col_base=0, tile_cb=None):
    r = ctx.rec
    nt = T // 128
    nd = D // 128
    A = ctx.A
    A.mark()
    stats = ctx.sb(8 * nt, F32)
    sq = [ctx.sb(D, F32), ctx.sb(D, F32)]
    utm = [ctx.sb(D, BF16), ctx.sb(D, BF16)]
    pp = [ctx.pbank(6, BF16), ctx.pbank(7, BF16)]
    for i in (tiles if tiles is not None else range(nt)):
        Rk = R.k(i * D, (i + 1) * D)
        Rs = R[:, i * D:(i + 1) * D]
        il = i - col_base
        c0 = 8 * il
        sqt = sq[il % 2]
        utt = utm[il % 2]
        s_sum = stats[:, c0:c0 + 1]
        s_mean = stats[:, c0 + 2:c0 + 3]
        s_var = stats[:, c0 + 3:c0 + 4]
        s_rstd = stats[:, c0 + 5:c0 + 6]
        sk = stats.k(c0, c0 + 8)
        r.op("scalar", lambda s, Rs=Rs, sqt=sqt, o=s_sum: s.activation(
            sqt[:, 0:D], Rs, AF.Copy, bias=0.0, scale=1.0, accum_out=o),
            reads=[Rk], writes=[sqt.K, sk])
        r.op("scalar", lambda s, Rs=Rs, sqt=sqt, o=stats[:, c0 + 1:c0 + 2]: s.activation(
            sqt[:, 0:D], Rs, AF.Square, bias=0.0, scale=1.0, accum_out=o),
            reads=[Rk], writes=[sqt.K, sk])
        r.op("vector", lambda v, i_=s_sum, o=s_mean: v.tensor_scalar(o, i_, 1.0 / D, None, ALU.mult),
             reads=[sk], writes=[sk])
        r.op("vector", lambda v, m=s_mean, o=s_var: v.tensor_tensor(o, m, m, ALU.mult),
             reads=[sk], writes=[sk])
        r.op("vector", lambda v, a=stats[:, c0 + 1:c0 + 2], b=s_var: v.tensor_scalar(
            b, a, 1.0 / D, b, ALU.mult, ALU.subtract),
            reads=[sk], writes=[sk])
        r.op("scalar", lambda s, a=s_var, o=s_rstd: s.activation(
            o, a, AF.Ln, bias=ctx.eps_col[:, 0:1], scale=1.0),
            reads=[sk, ctx.eps_col.K], writes=[sk])
        r.op("scalar", lambda s, o=s_rstd: s.activation(o, o, AF.Exp, bias=0.0, scale=-0.5),
             reads=[sk], writes=[sk])
        r.op("vector", lambda v, Rs=Rs, m=s_mean, rs=s_rstd, utt=utt: v.tensor_scalar(
            utt[:, 0:D], Rs, m, rs, ALU.subtract, ALU.mult),
            reads=[Rk, sk], writes=[utt.K])
        for c in range(nd):
            p_ = pp[c % 2]
            r.op("tensor", lambda t, c=c, p_=p_, utt=utt: t.transpose(
                p_[:, 0:128], utt[:, c * 128:(c + 1) * 128], ctx.ident[:, 0:128]),
                reads=[utt.K, ctx.ident.K], writes=[p_.K])
            if c % 2 == 0:
                r.op("vector", lambda v, c=c, il=il, p_=p_: v.tensor_copy(
                    ufm[:, c * T + il * 128: c * T + (il + 1) * 128], p_[:, 0:128]),
                    reads=[p_.K], writes=[ufm.k(c * T + il * 128, c * T + (il + 1) * 128)])
            else:
                r.op("scalar", lambda s, c=c, il=il, p_=p_: s.copy(
                    ufm[:, c * T + il * 128: c * T + (il + 1) * 128], p_[:, 0:128]),
                    reads=[p_.K], writes=[ufm.k(c * T + il * 128, c * T + (il + 1) * 128)])
        if tile_cb is not None:
            tile_cb(i)
    A.release()


def load_bias_cols(ctx, b_dram, n_heads, cols_out):
    r = ctx.rec
    A = ctx.A
    A.mark()
    row = ctx.sb(n_heads * 64, F32)
    pp = ctx.pbank(6)
    r.op("sync", lambda s: s.dma_start(out=row[0:1, :], in_=b_dram.rearrange("h o e -> o (h e)")),
         dma=True, writes=[row.K])
    for p in range(n_heads // 2):
        r.op("tensor", lambda t, p=p: t.transpose(
            pp[:, 0:1], row[0:1, p * 128:(p + 1) * 128], ctx.ones_colf[0:1, 0:1]),
            reads=[row.K, ctx.ones_colf.K], writes=[pp.k(0, 1)])
        r.op("vector", lambda v, p=p: v.tensor_copy(cols_out[:, p:p + 1], pp[:, 0:1]),
             reads=[pp.k(0, 1)], writes=[cols_out.k(p, p + 1)])
    A.release()


def emit_l1_attn(ctx, R, w, T=2048, D=1024, H=16, DH=64):
    r = ctx.rec
    nt, nd, npair = T // 128, D // 128, H // 2
    A = ctx.A
    A.mark()

    ufm = ctx.sb(nd * T, BF16)
    emit_ln_ufm(ctx, R, ufm, T, D)

    kfm = [ctx.sb(T, BF16) for _ in range(npair)]
    vtm = ctx.sb(nt * 1040, BF16)
    for i_ in range(nt):
        r.op("gpsimd", lambda g, i_=i_: g.memset(
            vtm[:, i_ * 1040: (i_ + 1) * 1040].rearrange("p (h x) -> p h x", x=65)[:, :, 64:65], 1.0),
            writes=[vtm.k(i_ * 1040, (i_ + 1) * 1040)])
    warn = [ctx.sb(nd * 128, BF16) for _ in range(npair)]
    bq_cols = ctx.sb(npair, F32)
    bk_cols = ctx.sb(npair, F32)
    bo_row = ctx.sb(D, BF16)
    woT = ctx.sb(nd * 1024, BF16)
    A.mark()
    bv_row = ctx.sb(D, BF16)

    def load_w(nm, p, hh):
        h = 2 * p + hh
        src = w[nm][h].rearrange("(c p) e -> p c e", p=128)
        dst = warn[p].ap3(slice(None), "p (c x) -> p c x", x=128)[:, :, hh * 64:(hh + 1) * 64]
        r.op("gpsimd", lambda g: g.dma_start(out=dst, in_=src), dma=True,
             writes=[warn[p].K])

    for p in range(npair):
        load_w("wk", p, 0)
        load_w("wk", p, 1)
    wv_sb = woT
    for h in range(H):
        src = w["wv"][h].rearrange("(c p) e -> p c e", p=128)
        dst = wv_sb.ap3(slice(None), "p (c h x) -> p c h x", h=H, x=64)[:, :, h, :]
        r.op("gpsimd", lambda g, s_=src, d_=dst: g.dma_start(out=d_, in_=s_), dma=True,
             writes=[wv_sb.K])
    load_bias_cols(ctx, w["bq"], H, bq_cols)
    load_bias_cols(ctx, w["bk"], H, bk_cols)
    r.op("gpsimd", lambda g: g.dma_start(out=bv_row[0:1, :], in_=w["bv"].rearrange("h o e -> o (h e)")),
         dma=True, writes=[bv_row.K])
    r.op("gpsimd", lambda g: g.dma_start(out=bo_row[0:1, :], in_=w["bo"].rearrange("(o d) -> o d", o=1)),
         dma=True, writes=[bo_row.K])

    pk = [ctx.pbank(0), ctx.pbank(1)]
    for p in range(npair):
        for icol in range(T // 512):
            pp = pk[icol % 2]
            for c in range(nd):
                r.op("tensor", lambda t, p=p, icol=icol, c=c, pp=pp: t.matmul(
                    pp[:, 0:512], warn[p][:, c * 128:(c + 1) * 128],
                    ufm[:, c * T + icol * 512: c * T + icol * 512 + 512],
                    start=(c == 0), stop=(c == nd - 1)),
                    reads=[warn[p].K, ufm.K], writes=[pp.K])
            r.op("vector", lambda v, p=p, icol=icol, pp=pp: v.tensor_scalar(
                kfm[p][:, icol * 512:(icol + 1) * 512], pp[:, 0:512],
                bk_cols[:, p:p + 1], None, ALU.add),
                reads=[pp.K, bk_cols.k(p, p + 1)], writes=[kfm[p].k(icol * 512, (icol + 1) * 512)])
    for i in range(nt):
        for half in range(2):
            pp = pk[half % 2]
            r.op("tensor", lambda t, half=half, pp=pp: t.matmul(
                pp[:, 0:512], ctx.ones_row[0:1, 0:128], bv_row[0:1, half * 512:(half + 1) * 512],
                start=True, stop=False),
                reads=[ctx.ones_row.K, bv_row.K], writes=[pp.K])
            for c in range(nd):
                r.op("tensor", lambda t, i=i, half=half, c=c, pp=pp: t.matmul(
                    pp[:, 0:512], ufm[:, c * T + i * 128: c * T + (i + 1) * 128],
                    wv_sb[:, c * 1024 + half * 512: c * 1024 + half * 512 + 512],
                    start=False, stop=(c == nd - 1)),
                    reads=[ufm.K, wv_sb.K], writes=[pp.K])
            dsts = vtm[:, i * 1040 + half * 520: i * 1040 + half * 520 + 520].rearrange(
                "p (h x) -> p h x", x=65)[:, :, 0:64]
            r.op("vector", lambda v, i=i, half=half, pp=pp, dsts=dsts: v.tensor_copy(
                dsts, pp[:, 0:512].rearrange("p (h x) -> p h x", x=64)),
                reads=[pp.K], writes=[vtm.k(i * 1040 + half * 520, i * 1040 + half * 520 + 520)])
    A.release()
    for p in range(npair):
        load_w("wq", p, 0)
        load_w("wq", p, 1)
    r.op("gpsimd", lambda g: g.dma_start(
        out=woT.ap3(slice(None), "p (r d) -> p r d", d=1024),
        in_=w["wot"].rearrange("(r p) d -> p r d", p=128)),
        dma=True, writes=[woT.K])

    NIQ = 512
    qsb = ctx.sb(NIQ, BF16)
    _pA, _pB, _pC = ctx.sb(NIQ, BF16), ctx.sb(NIQ, BF16), ctx.sb(NIQ, BF16)
    ptb = [[_pA, _pB], [_pC, _pC]]
    att = ctx.sb(npair * NIQ, BF16)
    zbb = ctx.A.at(qsb.off, NIQ, BF16)
    zrow = ctx.A.at(_pC.off, NIQ, BF16)
    POev = [ctx.pbank(6), ctx.pbank(7)]
    PS = [[ctx.pbank(0), ctx.pbank(1)], [ctx.pbank(2), ctx.pbank(3)]]
    PO = [ctx.pbank(4), ctx.pbank(5)]
    PQ = [ctx.pbank(0), ctx.pbank(1)]
    PAux = ctx.pbank(6)
    POut = [ctx.pbank(6), ctx.pbank(7)]

    def st_mm(p, hh, j):
        ps = PS[hh][j % 2]
        r.op("tensor", lambda t, p=p, hh=hh, j=j, ps=ps: t.matmul(
            ps[:, 0:NIQ], kfm[p][hh * 64:(hh + 1) * 64, j * 128:(j + 1) * 128],
            qsb[hh * 64:(hh + 1) * 64, 0:NIQ],
            start=True, stop=True, tile_position=(hh * 64, 0)),
            reads=[kfm[p].K, qsb.K], writes=[ps.K])

    for iq in range(T // NIQ):
        for p in range(npair):
            pq = PQ[0]
            for c in range(nd):
                r.op("tensor", lambda t, p=p, c=c, pq=pq, iq=iq: t.matmul(
                    pq[:, 0:NIQ], warn[p][:, c * 128:(c + 1) * 128],
                    ufm[:, c * T + iq * NIQ: c * T + iq * NIQ + NIQ],
                    start=(c == 0), stop=(c == nd - 1)),
                    reads=[warn[p].K, ufm.K], writes=[pq.K])
            r.op("vector", lambda v, p=p, pq=pq: v.tensor_scalar(
                qsb[:, 0:NIQ], pq[:, 0:NIQ], bq_cols[:, p:p + 1], None, ALU.add),
                reads=[pq.K, bq_cols.k(p, p + 1)], writes=[qsb.K])
            st_mm(p, 0, 0)
            st_mm(p, 1, 0)
            for j in range(nt):
                for hh in range(2):
                    ps = PS[hh][j % 2]
                    pt = ptb[hh][j % 2]
                    r.op("scalar", lambda s, ps=ps, pt=pt: s.activation(
                        pt[:, 0:NIQ], ps[:, 0:NIQ], AF.Exp, bias=0.0, scale=0.125),
                        reads=[ps.K], writes=[pt.K])
                if j + 1 < nt:
                    st_mm(p, 0, j + 1)
                    st_mm(p, 1, j + 1)
                for hh in range(2):
                    h = 2 * p + hh
                    po = PO[hh]
                    pt = ptb[hh][j % 2]
                    r.op("tensor", lambda t, h=h, j=j, po=po, pt=pt: t.matmul(
                        po[0:65, 0:NIQ], vtm[:, j * 1040 + h * 65: j * 1040 + (h + 1) * 65],
                        pt[:, 0:NIQ], start=(j == 0), stop=(j == nt - 1)),
                        reads=[vtm.K, pt.K], writes=[po.K])
            for hh in range(2):
                po = PO[hh]
                poev = POev[hh]
                r.op("vector", lambda v, po=po, poev=poev: v.tensor_copy(
                    poev[0:65, 0:NIQ], po[0:65, 0:NIQ]),
                    reads=[po.K], writes=[poev.K])
                def _zrow(s, poev=poev):
                    with ctx.nc.allow_low_precision(reason="Z row staged bf16"):
                        return s.copy(zrow[0:1, 0:NIQ], poev[64:65, 0:NIQ])
                r.op("scalar", _zrow, reads=[poev.K], writes=[zrow.K])
                r.op("tensor", lambda t, po=po: t.matmul(
                    po[0:64, 0:NIQ], ctx.ones_row[0:1, 0:64], zrow[0:1, 0:NIQ],
                    start=True, stop=True),
                    reads=[ctx.ones_row.K, zrow.K], writes=[po.K])
                def _recip(v, po=po):
                    with ctx.nc.allow_low_precision(reason="1/Z stored bf16"):
                        return v.reciprocal(zbb[0:64, 0:NIQ], po[0:64, 0:NIQ])
                r.op("vector", _recip, reads=[po.K], writes=[zbb.K])
                r.op("vector", lambda v, p=p, hh=hh, poev=poev: v.tensor_tensor(
                    att[hh * 64:(hh + 1) * 64, p * NIQ:(p + 1) * NIQ], poev[0:64, 0:NIQ],
                    zbb[0:64, 0:NIQ], ALU.mult),
                    reads=[poev.K, zbb.K],
                    writes=[att.k(p * NIQ, (p + 1) * NIQ)])
        for tt in range(NIQ // 128):
            i = iq * (NIQ // 128) + tt
            for dh in range(2):
                PP = POut[dh]
                r.op("tensor", lambda t, dh=dh, PP=PP: t.matmul(
                    PP[:, 0:512], ctx.ones_row[0:1, 0:128], bo_row[0:1, dh * 512:(dh + 1) * 512],
                    start=True, stop=False),
                    reads=[ctx.ones_row.K, bo_row.K], writes=[PP.K])
                for p in range(npair):
                    r.op("tensor", lambda t, tt=tt, dh=dh, p=p, PP=PP: t.matmul(
                        PP[:, 0:512], att[:, p * NIQ + tt * 128: p * NIQ + (tt + 1) * 128],
                        woT[:, p * 1024 + dh * 512: p * 1024 + dh * 512 + 512],
                        start=False, stop=(p == npair - 1)),
                        reads=[att.K, woT.K], writes=[PP.K])
                r.op("vector", lambda v, i=i, dh=dh, PP=PP: v.tensor_tensor(
                    R[:, i * 1024 + dh * 512: i * 1024 + dh * 512 + 512],
                    R[:, i * 1024 + dh * 512: i * 1024 + dh * 512 + 512], PP[:, 0:512], ALU.add),
                    reads=[PP.K, R.k(i * 1024 + dh * 512, i * 1024 + dh * 512 + 512)],
                    writes=[R.k(i * 1024 + dh * 512, i * 1024 + dh * 512 + 512)])
    A.release()


def build_standalone(emit_fn, wspecs, nseq=2, T=2048, D=1024):
    nc = bass.Bass("TRN2", target_bir_lowering=False)
    x = nc.dram_tensor("x", [nseq, T, D], F32, kind="ExternalInput").ap()
    y = nc.dram_tensor("y", [nseq, T, D], F32, kind="ExternalOutput").ap()
    waps = {n: nc.dram_tensor(n, list(sh), F32, kind="ExternalInput").ap()
            for n, sh in wspecs.items()}
    es = ExitStack()
    with es:
        rec = Rec(nc)
        ctx = Ctx(nc, rec, es)
        ctx.setup_constants()
        nt = T // 128
        R = ctx.sb(nt * D, F32)
        for s in range(nseq):
            for i in range(nt):
                rec.op("sync", lambda sy, s=s, i=i: sy.dma_start(
                    out=R[:, i * D:(i + 1) * D], in_=x[s, i * 128:(i + 1) * 128, :]),
                    dma=True, writes=[R.k(i * D, (i + 1) * D)])
            emit_fn(ctx, R, waps)
            for i in range(nt):
                rec.op("sync", lambda sy, s=s, i=i: sy.dma_start(
                    out=y[s, i * 128:(i + 1) * 128, :], in_=R[:, i * D:(i + 1) * D]),
                    dma=True, reads=[R.k(i * D, (i + 1) * D)])
            rec.barrier()
        rec.build(es)
        assert rec.check(), "schedulability check failed"
        with nc.Block() as block:
            rec.emit(block)
    return nc, rec


def emit_l0_ffn(ctx, R, w, T=2048, D=1024, H=8, DH=64, E=5):
    r = ctx.rec
    nt, nd, npair = T // 128, D // 128, H // 2
    A = ctx.A
    A.mark()
    ufm = ctx.sb(nd * T, BF16)

    wq_sb = [ctx.sb(nd * 128, BF16) for _ in range(npair)]
    wk_sb = [ctx.sb(nd * 128, BF16) for _ in range(npair)]
    wsd = ctx.sb(nd * 80, BF16)
    wv_sb = ctx.sb(nd * E * 64, BF16)
    wo_sb = ctx.sb(E * 1024, BF16)
    for p in range(npair):
        for nm, buf in (("Wq", wq_sb), ("Wk", wk_sb)):
            src = w[nm][:, p * 128:(p + 1) * 128].rearrange("(c p) e -> p c e", p=128)
            dst = buf[p].ap3(slice(None), "p (c x) -> p c x", x=128)
            r.op("gpsimd", lambda g, s_=src, d_=dst: g.dma_start(out=d_, in_=s_), dma=True,
                 writes=[buf[p].K])
    for nm, off in (("Ws", 0), ("Wd", 40)):
        src = w[nm].rearrange("(c p) e -> p c e", p=128)
        dst = wsd.ap3(slice(None), "p (c x) -> p c x", x=80)[:, :, off:off + 40]
        r.op("gpsimd", lambda g, s_=src, d_=dst: g.dma_start(out=d_, in_=s_), dma=True,
             writes=[wsd.K])
    for e in range(E):
        src = w["Wv"][e].rearrange("(c p) x -> p c x", p=128)
        dst = wv_sb.ap3(slice(None), "p (c e x) -> p c e x", e=E, x=64)[:, :, e, :]
        r.op("gpsimd", lambda g, s_=src, d_=dst: g.dma_start(out=d_, in_=s_), dma=True,
             writes=[wv_sb.K])
    for e in range(E):
        r.op("gpsimd", lambda g, e=e: g.dma_start(out=wo_sb[0:64, e * 1024:(e + 1) * 1024],
                                                  in_=w["Wo"][e]), dma=True, writes=[wo_sb.K])

    kfm = [ctx.sb(T, BF16) for _ in range(npair)]
    vtm = ctx.sb(nt * H * 65, BF16)
    for i_ in range(nt):
        r.op("gpsimd", lambda g, i_=i_: g.memset(
            vtm[:, i_ * 520: (i_ + 1) * 520].rearrange("p (h x) -> p h x", x=65)[:, :, 64:65], 1.0),
            writes=[vtm.k(i_ * 520, (i_ + 1) * 520)])
    maskd = ctx.sb(nt * H * E, F32)
    P0 = ctx.pbank(0)
    P1 = ctx.pbank(1)

    A.mark()
    gl = ctx.sb(80, F32)
    m1h = ctx.sb(H, F32)
    m2h = ctx.sb(H, F32)
    glm = ctx.sb(40, F32)
    sgm = ctx.sb(40, F32)
    maskv = ctx.sb(40, F32)
    xvt = ctx.sb(E * 64, BF16)
    vmt = ctx.sb(512, BF16)
    PG = ctx.pbank(2)
    PX = ctx.pbank(3)
    def gate_vmix_tile(i):
        for c in range(nd):
            r.op("tensor", lambda t, i=i, c=c: t.matmul(
                PG[:, 0:80], ufm[:, c * T + i * 128: c * T + (i + 1) * 128],
                wsd[:, c * 80:(c + 1) * 80], start=(c == 0), stop=(c == nd - 1)),
                reads=[ufm.K, wsd.K], writes=[PG.K])
        r.op("vector", lambda v: v.tensor_copy(gl[:, 0:80], PG[:, 0:80]),
             reads=[PG.K], writes=[gl.K])
        for c in range(nd):
            r.op("tensor", lambda t, i=i, c=c: t.matmul(
                PX[:, 0:E * 64], ufm[:, c * T + i * 128: c * T + (i + 1) * 128],
                wv_sb[:, c * E * 64:(c + 1) * E * 64], start=(c == 0), stop=(c == nd - 1)),
                reads=[ufm.K, wv_sb.K], writes=[PX.K])
        r.op("vector", lambda v: v.tensor_copy(xvt[:, 0:E * 64], PX[:, 0:E * 64]),
             reads=[PX.K], writes=[xvt.K])
        for goff, is_v in ((0, True), (40, False)):
            g3 = gl[:, goff:goff + 40].rearrange("p (h e) -> p h e", e=E)
            glk = gl.k(goff, goff + 40)
            r.op("vector", lambda v, g3=g3: v.reduce_max(m1h[:, 0:H], g3, axis=AX.X),
                 reads=[glk], writes=[m1h.K])
            m1b = m1h[:, 0:H].rearrange("p h -> p h ()").broadcast_to([128, H, E])
            r.op("vector", lambda v, g3=g3, m1b=m1b: v.tensor_tensor(
                glm[:, 0:40].rearrange("p (h e) -> p h e", e=E), g3, m1b, ALU.is_ge),
                reads=[glk, m1h.K], writes=[glm.K])
            r.op("vector", lambda v, goff=goff: v.scalar_tensor_tensor(
                glm[:, 0:40], glm[:, 0:40], -1e30, gl[:, goff:goff + 40], ALU.mult, ALU.add),
                reads=[glm.K, glk], writes=[glm.K])
            r.op("vector", lambda v: v.reduce_max(m2h[:, 0:H], glm[:, 0:40].rearrange("p (h e) -> p h e", e=E), axis=AX.X),
                 reads=[glm.K], writes=[m2h.K])
            m2b = m2h[:, 0:H].rearrange("p h -> p h ()").broadcast_to([128, H, E])
            if is_v:
                r.op("vector", lambda v, g3=g3, m2b=m2b: v.tensor_tensor(
                    maskv[:, 0:40].rearrange("p (h e) -> p h e", e=E), g3, m2b, ALU.is_ge),
                    reads=[glk, m2h.K], writes=[maskv.K])
            else:
                dk = maskd.k(i * 40, (i + 1) * 40)
                r.op("vector", lambda v, g3=g3, m2b=m2b, i=i: v.tensor_tensor(
                    maskd[:, i * 40:(i + 1) * 40].rearrange("p (h e) -> p h e", e=E), g3, m2b, ALU.is_ge),
                    reads=[glk, m2h.K], writes=[dk])
        r.op("scalar", lambda s: s.activation(sgm[:, 0:40], gl[:, 0:40], AF.Exp, bias=0.0, scale=-1.0),
             reads=[gl.K], writes=[sgm.K])
        r.op("vector", lambda v: v.tensor_scalar(sgm[:, 0:40], sgm[:, 0:40], 1.0, None, ALU.add),
             reads=[sgm.K], writes=[sgm.K])
        r.op("vector", lambda v: v.reciprocal(sgm[:, 0:40], sgm[:, 0:40]),
             reads=[sgm.K], writes=[sgm.K])
        r.op("vector", lambda v: v.tensor_tensor(sgm[:, 0:40], sgm[:, 0:40], maskv[:, 0:40], ALU.mult),
             reads=[sgm.K, maskv.K], writes=[sgm.K])
        v3d = vtm[:, i * 520: (i + 1) * 520].rearrange("p (h x) -> p h x", x=65)[:, :, 0:64]
        vk_all = vtm.k(i * 520, (i + 1) * 520)
        for e in range(E):
            xv_b = xvt[:, e * 64:(e + 1) * 64].rearrange("p (o x) -> p o x", o=1).broadcast_to([128, H, 64])
            sg_b = sgm[:, 0:40].rearrange("p (h e) -> p h e", e=E)[:, :, e:e + 1].broadcast_to([128, H, 64])
            if e == 0:
                r.op("vector", lambda v, xv_b=xv_b, sg_b=sg_b, v3d=v3d: v.tensor_tensor(
                    v3d, xv_b, sg_b, ALU.mult),
                    reads=[xvt.K, sgm.K], writes=[vk_all])
            else:
                r.op("vector", lambda v, xv_b=xv_b, sg_b=sg_b: v.tensor_tensor(
                    vmt[:, 0:512].rearrange("p (h x) -> p h x", x=64), xv_b, sg_b, ALU.mult),
                    reads=[xvt.K, sgm.K], writes=[vmt.K])
                r.op("vector", lambda v, v3d=v3d: v.tensor_tensor(
                    v3d, v3d, vmt[:, 0:512].rearrange("p (h x) -> p h x", x=64), ALU.add),
                    reads=[vmt.K, vk_all], writes=[vk_all])

    emit_ln_ufm(ctx, R, ufm, T, D, tile_cb=gate_vmix_tile)
    for p in range(npair):
        for icol in range(T // 512):
            pp = P0 if icol % 2 == 0 else P1
            for c in range(nd):
                r.op("tensor", lambda t, p=p, icol=icol, c=c, pp=pp: t.matmul(
                    pp[:, 0:512], wk_sb[p][:, c * 128:(c + 1) * 128],
                    ufm[:, c * T + icol * 512: c * T + icol * 512 + 512],
                    start=(c == 0), stop=(c == nd - 1)),
                    reads=[wk_sb[p].K, ufm.K], writes=[pp.K])
            r.op("scalar", lambda s, p=p, icol=icol, pp=pp: s.activation(
                kfm[p][:, icol * 512:(icol + 1) * 512], pp[:, 0:512], AF.Copy, bias=0.0, scale=1.0),
                reads=[pp.K], writes=[kfm[p].k(icol * 512, (icol + 1) * 512)])

    A.release()

    NIQ = 512
    A.mark()
    qsb = ctx.sb(NIQ, BF16)
    ptb = [[ctx.sb(NIQ, BF16) for _ in range(2)] for _ in range(2)]
    attm = ctx.sb((NIQ // 128) * 512, BF16)
    oev = ctx.sb(NIQ, BF16)
    zcol = ctx.sb(1, F32)
    zt = ctx.sb(E * 64, F32)
    ztb = ctx.sb(E * 64, BF16)
    zts = ctx.sb(E * 128, BF16)
    PS = [[ctx.pbank(0), ctx.pbank(1)], [ctx.pbank(2), ctx.pbank(3)]]
    PO = [ctx.pbank(4), ctx.pbank(5)]
    PQ = ctx.pbank(0)
    PT = ctx.pbank(6, BF16)
    POut = [ctx.pbank(4), ctx.pbank(5)]

    def st_mm(p, hh, j):
        ps = PS[hh][j % 2]
        r.op("tensor", lambda t, p=p, hh=hh, j=j, ps=ps: t.matmul(
            ps[:, 0:NIQ], kfm[p][hh * 64:(hh + 1) * 64, j * 128:(j + 1) * 128],
            qsb[hh * 64:(hh + 1) * 64, 0:NIQ],
            start=True, stop=True, tile_position=(hh * 64, 0)),
            reads=[kfm[p].K, qsb.K], writes=[ps.K])

    for iq in range(T // NIQ):
        for p in range(npair):
            for c in range(nd):
                r.op("tensor", lambda t, p=p, c=c, iq=iq: t.matmul(
                    PQ[:, 0:NIQ], wq_sb[p][:, c * 128:(c + 1) * 128],
                    ufm[:, c * T + iq * NIQ: c * T + iq * NIQ + NIQ],
                    start=(c == 0), stop=(c == nd - 1)),
                    reads=[wq_sb[p].K, ufm.K], writes=[PQ.K])
            r.op("scalar", lambda s: s.activation(qsb[:, 0:NIQ], PQ[:, 0:NIQ], AF.Copy, bias=0.0, scale=1.0),
                 reads=[PQ.K], writes=[qsb.K])
            st_mm(p, 0, 0)
            st_mm(p, 1, 0)
            for j in range(nt):
                for hh in range(2):
                    ps = PS[hh][j % 2]
                    pt = ptb[hh][j % 2]
                    r.op("scalar", lambda s, ps=ps, pt=pt: s.activation(
                        pt[:, 0:NIQ], ps[:, 0:NIQ], AF.Exp, bias=0.0, scale=0.125),
                        reads=[ps.K], writes=[pt.K])
                if j + 1 < nt:
                    st_mm(p, 0, j + 1)
                    st_mm(p, 1, j + 1)
                for hh in range(2):
                    h = 2 * p + hh
                    po = PO[hh]
                    pt = ptb[hh][j % 2]
                    r.op("tensor", lambda t, h=h, j=j, po=po, pt=pt: t.matmul(
                        po[0:65, 0:NIQ], vtm[:, j * 520 + h * 65: j * 520 + (h + 1) * 65],
                        pt[:, 0:NIQ], start=(j == 0), stop=(j == nt - 1)),
                        reads=[vtm.K, pt.K], writes=[po.K])
            for hh in range(2):
                h = 2 * p + hh
                po = PO[hh]
                r.op("scalar", lambda s, po=po: s.activation(oev[0:65, 0:NIQ], po[0:65, 0:NIQ],
                                                             AF.Copy, bias=0.0, scale=1.0),
                     reads=[po.K], writes=[oev.K])
                for ic in range(NIQ // 128):
                    r.op("tensor", lambda t, ic=ic: t.transpose(
                        PT[:, 0:65], oev[0:65, ic * 128:(ic + 1) * 128], ctx.ident[0:65, 0:65]),
                        reads=[oev.K, ctx.ident.K], writes=[PT.k(0, 128)])
                    r.op("vector", lambda v: v.reciprocal(zcol[:, 0:1], PT[:, 64:65]),
                         reads=[PT.k(0, 128)], writes=[zcol.K])
                    ak = attm.k(ic * 512 + h * 64, ic * 512 + (h + 1) * 64)
                    r.op("vector", lambda v, h=h, ic=ic: v.tensor_scalar(
                        attm[:, ic * 512 + h * 64: ic * 512 + (h + 1) * 64],
                        PT[:, 0:64], zcol[:, 0:1], None, ALU.mult),
                        reads=[PT.k(0, 128), zcol.K], writes=[ak])
        for ic in range(NIQ // 128):
            i = iq * (NIQ // 128) + ic
            for e in range(E):
                for h in range(H):
                    zsl = zt[:, e * 64:(e + 1) * 64]
                    zk = zt.k(e * 64, (e + 1) * 64)
                    msl = maskd[:, i * 40 + h * E + e: i * 40 + h * E + e + 1]
                    asl = attm[:, ic * 512 + h * 64: ic * 512 + (h + 1) * 64]
                    if h == 0:
                        r.op("vector", lambda v, zsl=zsl, msl=msl, asl=asl: v.tensor_scalar(
                            zsl, asl, msl, None, ALU.mult),
                            reads=[attm.K, maskd.K], writes=[zk])
                    else:
                        r.op("vector", lambda v, zsl=zsl, msl=msl, asl=asl: v.scalar_tensor_tensor(
                            zsl, asl, msl, zsl, ALU.mult, ALU.add),
                            reads=[attm.K, maskd.K, zk], writes=[zk])
            r.op("vector", lambda v: v.tensor_copy(ztb[:, 0:E * 64], zt[:, 0:E * 64]),
                 reads=[zt.K], writes=[ztb.K])
            for e in range(E):
                r.op("tensor", lambda t, e=e: t.transpose(
                    PT[0:64, e * 128:(e + 1) * 128], ztb[:, e * 64:(e + 1) * 64], ctx.ident[:, 0:128]),
                    reads=[ztb.K, ctx.ident.K], writes=[PT.k(e * 128, (e + 1) * 128)])
                r.op("vector", lambda v, e=e: v.tensor_copy(
                    zts[0:64, e * 128:(e + 1) * 128], PT[0:64, e * 128:(e + 1) * 128]),
                    reads=[PT.k(e * 128, (e + 1) * 128)], writes=[zts.k(e * 128, (e + 1) * 128)])
            for dh in range(2):
                PP = POut[dh]
                for e in range(E):
                    r.op("tensor", lambda t, e=e, dh=dh, PP=PP: t.matmul(
                        PP[:, 0:512], zts[0:64, e * 128:(e + 1) * 128],
                        wo_sb[0:64, e * 1024 + dh * 512: e * 1024 + dh * 512 + 512],
                        start=(e == 0), stop=(e == E - 1)),
                        reads=[zts.K, wo_sb.K], writes=[PP.K])
                r.op("vector", lambda v, i=i, dh=dh, PP=PP: v.tensor_tensor(
                    R[:, i * 1024 + dh * 512: i * 1024 + dh * 512 + 512],
                    R[:, i * 1024 + dh * 512: i * 1024 + dh * 512 + 512], PP[:, 0:512], ALU.add),
                    reads=[PP.K, R.k(i * 1024 + dh * 512, i * 1024 + dh * 512 + 512)],
                    writes=[R.k(i * 1024 + dh * 512, i * 1024 + dh * 512 + 512)])
    A.release()
    A.release()


def _build_transposed(ctx, dst, src_ap_fn, n_in_tiles, cols_in, stages, PTa, PTb,
                      dst_col_fn):
    r = ctx.rec
    np_ = cols_in // 128
    if not isinstance(stages, (list, tuple)):
        stages = [stages, stages]
    for it in range(n_in_tiles):
        stage = stages[it % len(stages)]
        r.op("gpsimd", lambda g, it=it, stage=stage: g.dma_start(
            out=stage[:, 0:cols_in], in_=src_ap_fn(it)),
            dma=True, writes=[stage.k(0, cols_in)])
        for pc in range(np_):
            PT = PTa if pc % 2 == 0 else PTb
            r.op("tensor", lambda t, pc=pc, PT=PT, stage=stage: t.transpose(
                PT[:, 0:128], stage[:, pc * 128:(pc + 1) * 128], ctx.ident[:, 0:128]),
                reads=[stage.k(0, cols_in), ctx.ident.K], writes=[PT.k(0, 128)])
            c0 = dst_col_fn(it, pc)
            r.op("vector", lambda v, c0=c0, PT=PT: v.tensor_copy(
                dst[:, c0:c0 + 128], PT[:, 0:128]),
                reads=[PT.k(0, 128)], writes=[dst.k(c0, c0 + 128)])


def emit_l1_ffn(ctx, R, w, T=2048, D=1024, E=16, HD=512, S=2048, phases=(1, 1, 1), gate_steps=99):
    r = ctx.rec
    nt, nd = T // 128, D // 128
    do_gate, do_shared, do_moe = phases
    A = ctx.A
    A.mark()
    ufm = ctx.sb(nd * T, BF16)
    moeacc = ctx.sb(nt * D, F32)
    disp = ctx.sb(nt * E, F32)

    PTa, PTb = ctx.pbank(6, BF16), ctx.pbank(7, BF16)
    P0, P1 = ctx.pbank(0), ctx.pbank(1)

    A.mark()
    gwT = ctx.sb(nd * E, BF16)
    r.op("gpsimd", lambda g: g.dma_start(
        out=gwT.ap3(slice(None), "p (c e) -> p c e", e=E),
        in_=w["gate_wt"].rearrange("(c p) e -> p c e", p=128)), dma=True,
        writes=[gwT.K])
    gl = ctx.sb(E, F32)
    glm = ctx.sb(E, F32)
    eg = ctx.sb(E, F32)
    msk = ctx.sb(E, F32)
    m1 = ctx.sb(1, F32)
    nm1 = ctx.sb(1, F32)
    m2 = ctx.sb(1, F32)
    e2 = ctx.sb(1, F32)

    def gate_tile(i):
        if not do_gate:
            return
        for c in range(nd):
            r.op("tensor", lambda t, i=i, c=c: t.matmul(
                P0[:, 0:E], ufm[:, c * T + i * 128: c * T + (i + 1) * 128],
                gwT[:, c * E:(c + 1) * E], start=(c == 0), stop=(c == nd - 1)),
                reads=[ufm.k(c * T + i * 128, c * T + (i + 1) * 128), gwT.K], writes=[P0.K])
        r.op("vector", lambda v: v.tensor_copy(gl[:, 0:E], P0[:, 0:E]),
             reads=[P0.K], writes=[gl.K])
        r.op("vector", lambda v: v.reduce_max(m1[:, 0:1], gl[:, 0:E], axis=AX.X),
             reads=[gl.K], writes=[m1.K])
        r.op("vector", lambda v: v.tensor_scalar(glm[:, 0:E], gl[:, 0:E], m1[:, 0:1], -1e30, ALU.is_ge, ALU.mult),
             reads=[gl.K, m1.K], writes=[glm.K])
        r.op("vector", lambda v: v.tensor_tensor(glm[:, 0:E], glm[:, 0:E], gl[:, 0:E], ALU.add),
             reads=[glm.K, gl.K], writes=[glm.K])
        r.op("vector", lambda v: v.reduce_max(m2[:, 0:1], glm[:, 0:E], axis=AX.X),
             reads=[glm.K], writes=[m2.K])
        r.op("vector", lambda v: v.tensor_scalar(msk[:, 0:E], gl[:, 0:E], m2[:, 0:1], None, ALU.is_ge),
             reads=[gl.K, m2.K], writes=[msk.K])
        r.op("vector", lambda v: v.tensor_scalar(nm1[:, 0:1], m1[:, 0:1], -1.0, None, ALU.mult),
             reads=[m1.K], writes=[nm1.K])
        r.op("scalar", lambda s: s.activation(eg[:, 0:E], gl[:, 0:E], AF.Exp, bias=nm1[:, 0:1], scale=1.0),
             reads=[gl.K, nm1.K], writes=[eg.K])
        r.op("scalar", lambda s: s.activation(e2[:, 0:1], m2[:, 0:1], AF.Exp, bias=nm1[:, 0:1], scale=1.0),
             reads=[m2.K, nm1.K], writes=[e2.K])
        r.op("vector", lambda v: v.tensor_scalar(e2[:, 0:1], e2[:, 0:1], 1.0, None, ALU.add),
             reads=[e2.K], writes=[e2.K])
        r.op("vector", lambda v: v.reciprocal(e2[:, 0:1], e2[:, 0:1]),
             reads=[e2.K], writes=[e2.K])
        r.op("vector", lambda v: v.tensor_tensor(eg[:, 0:E], eg[:, 0:E], msk[:, 0:E], ALU.mult),
             reads=[eg.K, msk.K], writes=[eg.K])
        r.op("vector", lambda v, i=i: v.tensor_scalar(disp[:, i * E:(i + 1) * E], eg[:, 0:E], e2[:, 0:1], None, ALU.mult),
             reads=[eg.K, e2.K], writes=[disp.k(i * E, (i + 1) * E)])

    emit_ln_ufm(ctx, R, ufm, T, D, tile_cb=gate_tile)
    A.release()

    if not do_gate:
        pass
    A.mark()
    SLAB = 512
    sfc3T = ctx.sb((SLAB // 128) * D, BF16)
    sfc1T = ctx.sb(8 * SLAB, BF16)
    sfc2T = ctx.sb(8 * SLAB, BF16)
    msb2 = [ctx.sb((SLAB // 128) * 512, BF16) for _ in range(2)]
    msil = ctx.sb(512, BF16)
    PG1, PG2 = ctx.pbank(2), ctx.pbank(3)
    PSH = [ctx.pbank(4), ctx.pbank(5)]
    for sl in (range(S // SLAB) if do_shared else []):
        for nm, dstb in (("sfc1t", sfc1T), ("sfc2t", sfc2T)):
            r.op("gpsimd", lambda g, nm=nm, sl=sl, dstb=dstb: g.dma_start(
                out=dstb.ap3(slice(None), "p (c s) -> p c s", s=SLAB),
                in_=w[nm][:, sl * SLAB:(sl + 1) * SLAB].rearrange("(c p) s -> p c s", p=128)),
                dma=True, writes=[dstb.K])
        r.op("gpsimd", lambda g, sl=sl: g.dma_start(
            out=sfc3T.ap3(slice(None), "p (sc d) -> p sc d", d=D),
            in_=w["sfc3t"][sl * SLAB:(sl + 1) * SLAB, :].rearrange("(sc p) d -> p sc d", p=128)),
            dma=True, writes=[sfc3T.K])
        for tb in range(T // 512):
            msb = msb2[tb % 2]
            for sc in range(SLAB // 128):
                for c in range(nd):
                    r.op("tensor", lambda t, sl=sl, tb=tb, sc=sc, c=c: t.matmul(
                        PG1[:, 0:512], sfc1T[:, c * SLAB + sc * 128: c * SLAB + (sc + 1) * 128],
                        ufm[:, c * T + tb * 512: c * T + tb * 512 + 512],
                        start=(c == 0), stop=(c == nd - 1)),
                        reads=[sfc1T.K, ufm.K], writes=[PG1.K])
                for c in range(nd):
                    r.op("tensor", lambda t, sl=sl, tb=tb, sc=sc, c=c: t.matmul(
                        PG2[:, 0:512], sfc2T[:, c * SLAB + sc * 128: c * SLAB + (sc + 1) * 128],
                        ufm[:, c * T + tb * 512: c * T + tb * 512 + 512],
                        start=(c == 0), stop=(c == nd - 1)),
                        reads=[sfc2T.K, ufm.K], writes=[PG2.K])
                r.op("scalar", lambda s: s.activation(msil[:, 0:512], PG1[:, 0:512], AF.Silu, bias=0.0, scale=1.0),
                     reads=[PG1.K], writes=[msil.K])
                r.op("vector", lambda v, sc=sc, msb=msb: v.tensor_tensor(
                    msb[:, sc * 512:(sc + 1) * 512], msil[:, 0:512], PG2[:, 0:512], ALU.mult),
                    reads=[msil.K, PG2.K], writes=[msb.k(sc * 512, (sc + 1) * 512)])
            for tt in range(4):
                i = tb * 4 + tt
                for dh in range(2):
                    PP = PSH[dh]
                    for sc in range(SLAB // 128):
                        r.op("tensor", lambda t, tt=tt, dh=dh, sc=sc, PP=PP, msb=msb: t.matmul(
                            PP[:, 0:512], msb[:, sc * 512 + tt * 128: sc * 512 + tt * 128 + 128],
                            sfc3T[:, sc * 1024 + dh * 512: sc * 1024 + dh * 512 + 512],
                            start=(sc == 0), stop=(sc == SLAB // 128 - 1)),
                            reads=[msb.k(sc * 512 + tt * 128, sc * 512 + tt * 128 + 128), sfc3T.K],
                            writes=[PP.K])
                    ck = moeacc.k(i * D + dh * 512, i * D + dh * 512 + 512)
                    if sl == 0:
                        r.op("vector", lambda v, i=i, dh=dh, PP=PP: v.tensor_copy(
                            moeacc[:, i * D + dh * 512: i * D + dh * 512 + 512], PP[:, 0:512]),
                            reads=[PP.K], writes=[ck])
                    else:
                        r.op("vector", lambda v, i=i, dh=dh, PP=PP: v.tensor_tensor(
                            moeacc[:, i * D + dh * 512: i * D + dh * 512 + 512],
                            moeacc[:, i * D + dh * 512: i * D + dh * 512 + 512], PP[:, 0:512], ALU.add),
                            reads=[PP.K, ck], writes=[ck])
    A.release()

    A.mark()
    w1T2 = [ctx.sb(8 * 512, BF16) for _ in range(2)]
    w2T2 = [ctx.sb(4 * 1024, BF16) for _ in range(2)]
    hw2 = [ctx.sb(4 * 512, BF16), ctx.sb(4 * 512, BF16)]
    PH = [ctx.pbank(2), ctx.pbank(3)]
    PM = [ctx.pbank(4), ctx.pbank(5)]

    def build_we(e):
        b1, b2 = w1T2[e % 2], w2T2[e % 2]
        r.op("gpsimd", lambda g, e=e, b1=b1: g.dma_start(
            out=b1.ap3(slice(None), "p (c h) -> p c h", h=HD),
            in_=w["w1t"][e].rearrange("(c p) h -> p c h", p=128)),
            dma=True, writes=[b1.K])
        r.op("gpsimd", lambda g, e=e, b2=b2: g.dma_start(
            out=b2.ap3(slice(None), "p (hc d) -> p hc d", d=D),
            in_=w["w2t"][e].rearrange("(hc p) d -> p hc d", p=128)),
            dma=True, writes=[b2.K])

    if do_moe:
        build_we(0)
    for e in (range(E) if do_moe else []):
        w1T = w1T2[e % 2]
        w2T = w2T2[e % 2]
        if e + 1 < E:
            build_we(e + 1)
        for tb in range(T // 512):
            hw = hw2[tb % 2]
            for hc in range(4):
                PHx = PH[hc % 2]
                for c in range(nd):
                    r.op("tensor", lambda t, e=e, tb=tb, hc=hc, c=c, PHx=PHx, w1T=w1T: t.matmul(
                        PHx[:, 0:512], w1T[:, c * 512 + hc * 128: c * 512 + (hc + 1) * 128],
                        ufm[:, c * T + tb * 512: c * T + tb * 512 + 512],
                        start=(c == 0), stop=(c == nd - 1)),
                        reads=[w1T.K, ufm.K], writes=[PHx.K])
                r.op("scalar", lambda s, hc=hc, PHx=PHx, hw=hw: s.activation(
                    hw[:, hc * 512:(hc + 1) * 512], PHx[:, 0:512], AF.Silu, bias=0.0, scale=1.0),
                    reads=[PHx.K], writes=[hw.k(hc * 512, (hc + 1) * 512)])
            for tt in range(4):
                i = tb * 4 + tt
                for dh in range(2):
                    PMx = PM[dh]
                    for hc in range(4):
                        r.op("tensor", lambda t, tt=tt, dh=dh, hc=hc, PMx=PMx, hw=hw, w2T=w2T: t.matmul(
                            PMx[:, 0:512], hw[:, hc * 512 + tt * 128: hc * 512 + tt * 128 + 128],
                            w2T[:, hc * 1024 + dh * 512: hc * 1024 + dh * 512 + 512],
                            start=(hc == 0), stop=(hc == 3)),
                            reads=[hw.k(hc * 512 + tt * 128, hc * 512 + tt * 128 + 128), w2T.K],
                            writes=[PMx.K])
                    ck = moeacc.k(i * D + dh * 512, i * D + dh * 512 + 512)
                    r.op("vector", lambda v, i=i, e=e, dh=dh, PMx=PMx: v.scalar_tensor_tensor(
                        moeacc[:, i * D + dh * 512: i * D + dh * 512 + 512],
                        PMx[:, 0:512], disp[:, i * E + e: i * E + e + 1],
                        moeacc[:, i * D + dh * 512: i * D + dh * 512 + 512],
                        ALU.mult, ALU.add),
                        reads=[PMx.K, disp.K, ck], writes=[ck])
    A.release()
    if not (do_shared or do_moe):
        for i in range(nt):
            r.op("gpsimd", lambda g, i=i: g.memset(moeacc[:, i * D:(i + 1) * D], 0.0),
                 writes=[moeacc.k(i * D, (i + 1) * D)])
    elif not do_shared:
        for i in range(nt):
            r.op("gpsimd", lambda g, i=i: g.memset(moeacc[:, i * D:(i + 1) * D], 0.0),
                 writes=[moeacc.k(i * D, (i + 1) * D)])
    for i in range(nt):
        for dh in range(2):
            rk = R.k(i * D + dh * 512, i * D + dh * 512 + 512)
            r.op("vector", lambda v, i=i, dh=dh: v.tensor_tensor(
                R[:, i * D + dh * 512: i * D + dh * 512 + 512],
                R[:, i * D + dh * 512: i * D + dh * 512 + 512],
                moeacc[:, i * D + dh * 512: i * D + dh * 512 + 512], ALU.add),
                reads=[moeacc.K, rk], writes=[rk])
    A.release()


def emit_l0_attn(ctx, R, w, T=2048, D=1024, H=4, K=256, V=256, M=64):
    r = ctx.rec
    nt, nd = T // 128, D // 128
    A = ctx.A
    A.mark()
    tri_f = ctx.sb(128, F32)
    onesf = ctx.sb(128, F32)
    r.op("gpsimd", lambda g: g.memset(tri_f[:, 0:128], 0.0), writes=[tri_f.K])
    r.op("gpsimd", lambda g: g.affine_select(
        out=tri_f[:, 0:128], in_=tri_f[:, 0:128], compare_op=ALU.is_gt,
        fill=1.0, base=0, pattern=[[-1, 128]], channel_multiplier=1),
        reads=[tri_f.K], writes=[tri_f.K])
    r.op("gpsimd", lambda g: g.memset(onesf[:, 0:128], 1.0), writes=[onesf.K])

    wq_sb = ctx.sb(nd * 1024, BF16)
    wk_sb = ctx.sb(nd * 1024, BF16)
    wv_sb = ctx.sb(nd * 1024, BF16)
    wf_sb = ctx.sb(nd * 256, BF16)
    wo_sb = ctx.sb(nd * 1024, BF16)
    for nm, buf, cols in (("Wq", wq_sb, 1024), ("Wk", wk_sb, 1024), ("Wv", wv_sb, 1024),
                          ("Wf", wf_sb, 256), ("Wo", wo_sb, 1024)):
        for c in range(nd):
            r.op("gpsimd", lambda g, nm=nm, c=c, buf=buf, cols=cols: g.dma_start(
                out=buf[:, c * cols:(c + 1) * cols], in_=w[nm][c * 128:(c + 1) * 128, :]),
                dma=True, writes=[buf.k(c * cols, (c + 1) * cols)])
    gn_row = ctx.sb(256, F32)
    G_sb = ctx.sb(256, F32)
    PB = ctx.pbank(4)
    r.op("sync", lambda s: s.dma_start(out=gn_row[0:1, 0:256],
                                       in_=w["g_norm_w"].rearrange("(o v) -> o v", o=1)),
         dma=True, writes=[gn_row.K])
    r.op("tensor", lambda t: t.matmul(PB[:, 0:256], onesf[0:1, 0:128], gn_row[0:1, 0:256],
                                      start=True, stop=True),
         reads=[onesf.K, gn_row.K], writes=[PB.K])
    r.op("vector", lambda v: v.tensor_copy(G_sb[:, 0:256], PB[:, 0:256]),
         reads=[PB.K], writes=[G_sb.K])

    hk_f = [ctx.sb(512, F32) for _ in range(2)]
    hkb = [ctx.sb(512, BF16) for _ in range(2)]
    hv_f = [ctx.sb(1024, F32) for _ in range(2)]
    hvb = [ctx.sb(1024, BF16) for _ in range(2)]
    for b_ in hk_f + hkb + hv_f + hvb:
        r.op("gpsimd", lambda g, b_=b_: g.memset(b_[:, 0:b_.cols], 0.0), writes=[b_.K])

    GT = 256
    ng = T // GT
    ntg = GT // 128
    ufm_g = ctx.sb(nd * GT, BF16)
    qfm = ctx.sb(8 * GT, BF16)
    kfm2 = ctx.sb(8 * GT, BF16)
    ofm_g = ufm_g
    A_s2 = [ctx.sb(256, F32) for _ in range(2)]
    Am2 = [ctx.sb(256, F32) for _ in range(2)]
    stil2 = [ctx.sb(256, BF16) for _ in range(2)]
    acr2 = [ctx.sb(256, F32) for _ in range(2)]
    acb2 = [ctx.sb(256, F32) for _ in range(2)]
    actc2 = [ctx.sb(2, F32) for _ in range(2)]
    ktm2 = [ctx.sb(8 * 128, BF16) for _ in range(2)]
    Sm2 = [ctx.sb(128, BF16) for _ in range(2)]
    wmt2 = [ctx.sb(128, BF16) for _ in range(2)]
    ptT2 = [ctx.sb(128, BF16) for _ in range(2)]
    stT2 = [ctx.sb(128, BF16) for _ in range(2)]
    lg2 = [ctx.sb(64, F32) for _ in range(2)]
    expo2 = [ctx.sb(64, F32) for _ in range(2)]
    ptil2 = [ctx.sb(64, BF16) for _ in range(2)]
    o_sb2 = [ctx.sb(256, F32) for _ in range(2)]
    obf2 = [ctx.sb(256, BF16) for _ in range(2)]
    zs2 = [ctx.sb(4, F32) for _ in range(2)]
    ztmp2 = [ctx.sb(256, F32) for _ in range(2)]
    osq2 = [ctx.sb(256, F32) for _ in range(2)]
    vtm_c2 = [ctx.sb(1024, BF16) for _ in range(2)]
    f8c2 = [ctx.sb(256, F32) for _ in range(2)]
    P0, P1 = ctx.pbank(0), ctx.pbank(1)
    PLOG2 = [ctx.pbank(0), ctx.pbank(1)]
    PSTW2 = [ctx.pbank(2), ctx.pbank(3)]
    POx2 = [ctx.pbank(4), ctx.pbank(5)]
    PC2 = [ctx.pbank(2), ctx.pbank(3)]
    PB2 = [ctx.pbank(4), ctx.pbank(5)]
    PTa, PTb = ctx.pbank(6, BF16), ctx.pbank(7, BF16)
    POP = [ctx.pbank(2), ctx.pbank(3)]

    def do_head(gi, il, cur, nxt, h, vtm_c, f8c, A_s, Am, stil, acr, acb_sb, actc, ktm, PC, PB):
        Sm = Sm2[h % 2]
        wmt = wmt2[h % 2]
        ptT = ptT2[h % 2]
        stT = stT2[h % 2]
        lg = lg2[h % 2]
        expo = expo2[h % 2]
        ptil = ptil2[h % 2]
        o_sb = o_sb2[h % 2]
        obf = obf2[h % 2]
        zs = zs2[h % 2]
        osq = osq2[h % 2]
        PLOG = PLOG2[h % 2]
        PST = PSTW2[h % 2]
        PW = PST
        POx = POx2[h % 2]
        PD = PLOG
        hm = stil[:, h * 64:(h + 1) * 64]
        hmk = stil.k(h * 64, (h + 1) * 64)
        for kc in range(2):
            oc = 2 * h + kc
            r.op("tensor", lambda t, oc=oc, il=il, kc=kc, cur=cur: t.matmul(
                PLOG[:, 0:64], qfm[:, oc * GT + il * 128: oc * GT + (il + 1) * 128],
                hkb[cur][:, oc * 64:(oc + 1) * 64],
                start=(kc == 0), stop=False),
                reads=[qfm.K, hkb[cur].K], writes=[PLOG.K])
        for kc in range(2):
            oc = 2 * h + kc
            r.op("tensor", lambda t, oc=oc, il=il, kc=kc: t.matmul(
                PST[:, 0:128], kfm2[:, oc * GT + il * 128: oc * GT + (il + 1) * 128],
                qfm[:, oc * GT + il * 128: oc * GT + (il + 1) * 128],
                start=(kc == 0), stop=(kc == 1)),
                reads=[kfm2.K, qfm.K], writes=[PST.K])
        r.op("vector", lambda v: v.tensor_tensor(Sm[:, 0:128], PST[:, 0:128], ctx.tri[:, 0:128], ALU.mult),
             reads=[PST.K, ctx.tri.K], writes=[Sm.K])
        r.op("tensor", lambda t, hm=hm: t.matmul(
            PLOG[:, 0:64], Sm[:, 0:128], hm, start=False, stop=True),
            reads=[Sm.K, hmk], writes=[PLOG.K])
        yield
        r.op("vector", lambda v, h=h: v.tensor_tensor(
            lg[:, 0:64], PLOG[:, 0:64], A_s[:, h * 64:(h + 1) * 64], ALU.mult),
            reads=[PLOG.K, A_s.K], writes=[lg.K])
        r.op("vector", lambda v: v.reduce_max(zs[:, 0:1], lg[:, 0:64], axis=AX.X, negate=True),
             reads=[lg.K], writes=[zs.K])
        r.op("vector", lambda v: v.tensor_scalar(zs[:, 1:2], zs[:, 0:1], 0.0625, None, ALU.mult),
             reads=[zs.K], writes=[zs.K])
        r.op("scalar", lambda s: s.activation(expo[:, 0:64], lg[:, 0:64], AF.Exp,
                                              bias=zs[:, 1:2], scale=0.0625, accum_out=zs[:, 2:3]),
             reads=[lg.K, zs.K], writes=[expo.K, zs.K])
        r.op("vector", lambda v: v.reciprocal(zs[:, 2:3], zs[:, 2:3]),
             reads=[zs.K], writes=[zs.K])
        r.op("vector", lambda v: v.tensor_scalar(expo[:, 0:64], expo[:, 0:64], zs[:, 2:3], None, ALU.mult),
             reads=[expo.K, zs.K], writes=[expo.K])
        r.op("vector", lambda v, h=h: v.tensor_tensor(
            ptil[:, 0:64], expo[:, 0:64], A_s[:, h * 64:(h + 1) * 64], ALU.mult),
            reads=[expo.K, A_s.K], writes=[ptil.K])
        yield
        r.op("tensor", lambda t: t.transpose(PTa[0:64, 0:128], ptil[:, 0:64], ctx.ident[:, 0:128]),
             reads=[ptil.K, ctx.ident.K], writes=[PTa.k(0, 128)])
        r.op("vector", lambda v: v.tensor_copy(ptT[0:64, 0:128], PTa[0:64, 0:128]),
             reads=[PTa.k(0, 128)], writes=[ptT.K])
        r.op("tensor", lambda t, hm=hm: t.transpose(PTb[0:64, 0:128], hm, ctx.ident[:, 0:128]),
             reads=[hmk, ctx.ident.K], writes=[PTb.k(0, 128)])
        r.op("vector", lambda v: v.tensor_copy(stT[0:64, 0:128], PTb[0:64, 0:128]),
             reads=[PTb.k(0, 128)], writes=[stT.K])
        yield
        r.op("tensor", lambda t, h=h, cur=cur: t.matmul(
            POx[:, 0:256], ptT[0:64, 0:128], hvb[cur][0:64, h * 256:(h + 1) * 256],
            start=True, stop=False),
            reads=[ptT.K, hvb[cur].K], writes=[POx.K])
        r.op("tensor", lambda t: t.matmul(
            PW[:, 0:128], stT[0:64, 0:128], ptT[0:64, 0:128], start=True, stop=True),
            reads=[stT.K, ptT.K], writes=[PW.K])
        r.op("vector", lambda v: v.tensor_tensor(wmt[:, 0:128], PW[:, 0:128], ctx.tri[:, 0:128], ALU.mult),
             reads=[PW.K, ctx.tri.K], writes=[wmt.K])
        r.op("tensor", lambda t, il=il, h=h: t.matmul(
            POx[:, 0:256], wmt[:, 0:128],
            vtm_c[:, h * 256:(h + 1) * 256],
            start=False, stop=True),
            reads=[wmt.K, vtm_c.K], writes=[POx.K])
        yield
        for kc in range(2):
            oc = 2 * h + kc
            r.op("tensor", lambda t, oc=oc, kc=kc, hm=hm: t.matmul(
                PD[:, 0:64], ktm[:, oc * 128:(oc + 1) * 128], hm, start=True, stop=True),
                reads=[ktm.K, hmk], writes=[PD.K])
            r.op("vector", lambda v, oc=oc, cur=cur, nxt=nxt: v.tensor_tensor(
                hk_f[nxt][:, oc * 64:(oc + 1) * 64], hk_f[cur][:, oc * 64:(oc + 1) * 64],
                PD[:, 0:64], ALU.add),
                reads=[hk_f[cur].K, PD.K], writes=[hk_f[nxt].k(oc * 64, (oc + 1) * 64)])
            r.op("vector", lambda v, oc=oc, h=h, nxt=nxt: v.tensor_tensor(
                hk_f[nxt][:, oc * 64:(oc + 1) * 64], hk_f[nxt][:, oc * 64:(oc + 1) * 64],
                acb_sb[:, h * 64:(h + 1) * 64], ALU.mult),
                reads=[hk_f[nxt].k(oc * 64, (oc + 1) * 64), acb_sb.K],
                writes=[hk_f[nxt].k(oc * 64, (oc + 1) * 64)])
            r.op("vector", lambda v, oc=oc, nxt=nxt: v.tensor_copy(
                hkb[nxt][:, oc * 64:(oc + 1) * 64], hk_f[nxt][:, oc * 64:(oc + 1) * 64]),
                reads=[hk_f[nxt].k(oc * 64, (oc + 1) * 64)], writes=[hkb[nxt].k(oc * 64, (oc + 1) * 64)])
        r.op("tensor", lambda t, il=il, h=h, hm=hm: t.matmul(
            PD[0:64, 64:320], hm, vtm_c[:, h * 256:(h + 1) * 256],
            start=True, stop=True),
            reads=[hmk, vtm_c.K], writes=[PD.K])
        r.op("vector", lambda v, h=h, cur=cur, nxt=nxt: v.tensor_tensor(
            hv_f[nxt][0:64, h * 256:(h + 1) * 256], hv_f[cur][0:64, h * 256:(h + 1) * 256],
            PD[0:64, 64:320], ALU.add),
            reads=[hv_f[cur].K, PD.K], writes=[hv_f[nxt].k(h * 256, (h + 1) * 256)])
        r.op("vector", lambda v, h=h, nxt=nxt: v.tensor_scalar(
            hv_f[nxt][0:64, h * 256:(h + 1) * 256], hv_f[nxt][0:64, h * 256:(h + 1) * 256],
            actc[(h % 2) * 64:(h % 2) * 64 + 64, h // 2: h // 2 + 1], None, ALU.mult),
            reads=[hv_f[nxt].k(h * 256, (h + 1) * 256), actc.K],
            writes=[hv_f[nxt].k(h * 256, (h + 1) * 256)])
        r.op("vector", lambda v, h=h, nxt=nxt: v.tensor_copy(
            hvb[nxt][0:64, h * 256:(h + 1) * 256], hv_f[nxt][0:64, h * 256:(h + 1) * 256]),
            reads=[hv_f[nxt].k(h * 256, (h + 1) * 256)], writes=[hvb[nxt].k(h * 256, (h + 1) * 256)])
        r.op("vector", lambda v: v.tensor_copy(o_sb[:, 0:256], POx[:, 0:256]),
             reads=[POx.K], writes=[o_sb.K])
        yield
        r.op("vector", lambda v: v.tensor_tensor(osq[:, 0:256], o_sb[:, 0:256], o_sb[:, 0:256], ALU.mult),
             reads=[o_sb.K], writes=[osq.K])
        r.op("vector", lambda v: v.reduce_sum(zs[:, 3:4], osq[:, 0:256], axis=AX.X),
             reads=[osq.K], writes=[zs.K])
        r.op("vector", lambda v: v.tensor_scalar(zs[:, 3:4], zs[:, 3:4], 1.0 / 256, 1e-5, ALU.mult, ALU.add),
             reads=[zs.K], writes=[zs.K])
        r.op("scalar", lambda s: s.activation(zs[:, 3:4], zs[:, 3:4], AF.Ln, bias=0.0, scale=1.0),
             reads=[zs.K], writes=[zs.K])
        r.op("scalar", lambda s: s.activation(zs[:, 3:4], zs[:, 3:4], AF.Exp, bias=0.0, scale=-0.5),
             reads=[zs.K], writes=[zs.K])
        r.op("vector", lambda v: v.tensor_scalar(o_sb[:, 0:256], o_sb[:, 0:256], zs[:, 3:4], None, ALU.mult),
             reads=[o_sb.K, zs.K], writes=[o_sb.K])
        r.op("vector", lambda v: v.tensor_tensor(obf[:, 0:256], o_sb[:, 0:256], G_sb[:, 0:256], ALU.mult),
             reads=[o_sb.K, G_sb.K], writes=[obf.K])
        for vc in range(2):
            oc = 2 * h + vc
            PT = PTa if vc % 2 == 0 else PTb
            r.op("tensor", lambda t, vc=vc, PT=PT: t.transpose(
                PT[:, 0:128], obf[:, vc * 128:(vc + 1) * 128], ctx.ident[:, 0:128]),
                reads=[obf.K, ctx.ident.K], writes=[PT.k(0, 128)])
            r.op("vector", lambda v, oc=oc, il=il, PT=PT: v.tensor_copy(
                ofm_g[:, oc * GT + il * 128: oc * GT + (il + 1) * 128], PT[:, 0:128]),
                reads=[PT.k(0, 128)], writes=[ofm_g.k(oc * GT + il * 128, oc * GT + (il + 1) * 128)])

    def do_chunk(gi, il, cur):
        nxt = 1 - cur
        vtm_c = vtm_c2[il % 2]
        f8c = f8c2[il % 2]
        A_s = A_s2[il % 2]
        Am = Am2[il % 2]
        ef = ztmp2[il % 2]
        stil = stil2[il % 2]
        acr = acr2[il % 2]
        acb_sb = acb2[il % 2]
        actc = actc2[il % 2]
        ktm = ktm2[il % 2]
        ztmp = ztmp2[il % 2]
        PC = PC2[il % 2]
        PB = PB2[il % 2]
        for half in range(2):
            pp = P0 if half % 2 == 0 else P1
            for c in range(nd):
                r.op("tensor", lambda t, il=il, half=half, c=c, pp=pp: t.matmul(
                    pp[:, 0:512], ufm_g[:, c * GT + il * 128: c * GT + (il + 1) * 128],
                    wv_sb[:, c * 1024 + half * 512: c * 1024 + half * 512 + 512],
                    start=(c == 0), stop=(c == nd - 1)),
                    reads=[ufm_g.k(c * GT + il * 128, c * GT + (il + 1) * 128), wv_sb.K], writes=[pp.K])
            r.op("vector", lambda v, half=half, pp=pp: v.tensor_copy(
                vtm_c[:, half * 512: half * 512 + 512], pp[:, 0:512]),
                reads=[pp.K], writes=[vtm_c.k(half * 512, half * 512 + 512)])
        ppf = P0
        for c in range(nd):
            r.op("tensor", lambda t, il=il, c=c, ppf=ppf: t.matmul(
                ppf[:, 0:256], ufm_g[:, c * GT + il * 128: c * GT + (il + 1) * 128],
                wf_sb[:, c * 256:(c + 1) * 256],
                start=(c == 0), stop=(c == nd - 1)),
                reads=[ufm_g.k(c * GT + il * 128, c * GT + (il + 1) * 128), wf_sb.K], writes=[ppf.K])
        r.op("scalar", lambda s, ppf=ppf: s.activation(ztmp[:, 0:256], ppf[:, 0:256], AF.Abs, bias=0.0, scale=1.0),
             reads=[ppf.K], writes=[ztmp.K])
        r.op("scalar", lambda s: s.activation(ztmp[:, 0:256], ztmp[:, 0:256], AF.Exp, bias=0.0, scale=-1.0),
             reads=[ztmp.K], writes=[ztmp.K])
        r.op("scalar", lambda s: s.activation(ztmp[:, 0:256], ztmp[:, 0:256], AF.Ln,
                                              bias=ctx.ones_colf[:, 0:1], scale=1.0),
             reads=[ztmp.K, ctx.ones_colf.K], writes=[ztmp.K])
        r.op("vector", lambda v, ppf=ppf: v.scalar_tensor_tensor(
            f8c[:, 0:256], ppf[:, 0:256], 0.0, ztmp[:, 0:256],
            ALU.min, ALU.subtract),
            reads=[ppf.K, ztmp.K], writes=[f8c.K])
        fs = f8c[:, 0:256]
        fk = f8c.K
        r.op("tensor", lambda t, fs=fs: t.matmul(PC[:, 0:256], tri_f[:, 0:128], fs,
                                                 start=True, stop=True),
             reads=[tri_f.K, fk], writes=[PC.K])
        r.op("tensor", lambda t, fs=fs: t.matmul(PC[0:1, 256:512], onesf[:, 0:1], fs,
                                                 start=True, stop=True),
             reads=[onesf.K, fk], writes=[PC.K])
        r.op("scalar", lambda s: s.activation(A_s[:, 0:256], PC[:, 0:256], AF.Exp, bias=0.0, scale=0.125),
             reads=[PC.K], writes=[A_s.K])
        r.op("scalar", lambda s: s.activation(Am[:, 0:256], PC[:, 0:256], AF.Exp, bias=0.0, scale=-0.125),
             reads=[PC.K], writes=[Am.K])
        r.op("scalar", lambda s: s.activation(acr[0:1, 0:256], PC[0:1, 256:512], AF.Exp, bias=0.0, scale=0.125),
             reads=[PC.K], writes=[acr.K])
        r.op("scalar", lambda s, fs=fs: s.activation(ef[:, 0:256], fs, AF.Exp, bias=0.0, scale=0.125),
             reads=[fk], writes=[ef.K])
        r.op("vector", lambda v: v.tensor_tensor(ef[:, 0:256], ef[:, 0:256], Am[:, 0:256], ALU.mult),
             reads=[ef.K, Am.K], writes=[ef.K])
        r.op("vector", lambda v: v.tensor_tensor(stil[:, 0:256], Am[:, 0:256], ef[:, 0:256], ALU.subtract),
             reads=[Am.K, ef.K], writes=[stil.K])
        r.op("tensor", lambda t: t.matmul(PB[:, 0:256], onesf[0:1, 0:128], acr[0:1, 0:256],
                                          start=True, stop=True),
             reads=[onesf.K, acr.K], writes=[PB.K])
        r.op("vector", lambda v: v.tensor_copy(acb_sb[:, 0:256], PB[:, 0:256]),
             reads=[PB.K], writes=[acb_sb.K])
        for pc in range(2):
            r.op("tensor", lambda t, pc=pc: t.transpose(
                PB[:, 384 + pc: 385 + pc], acr[0:1, pc * 128:(pc + 1) * 128], onesf[0:1, 0:1]),
                reads=[acr.K, onesf.K], writes=[PB.K])
            r.op("vector", lambda v, pc=pc: v.tensor_copy(actc[:, pc:pc + 1], PB[:, 384 + pc: 385 + pc]),
                 reads=[PB.K], writes=[actc.K])
        for oc in range(8):
            PT = PTa if oc % 2 == 0 else PTb
            r.op("tensor", lambda t, oc=oc, il=il, PT=PT: t.transpose(
                PT[:, 0:128], kfm2[:, oc * GT + il * 128: oc * GT + (il + 1) * 128], ctx.ident[:, 0:128]),
                reads=[kfm2.K, ctx.ident.K], writes=[PT.k(0, 128)])
            r.op("vector", lambda v, oc=oc, PT=PT: v.tensor_copy(
                ktm[:, oc * 128:(oc + 1) * 128], PT[:, 0:128]),
                reads=[PT.k(0, 128)], writes=[ktm.k(oc * 128, (oc + 1) * 128)])
        yield
        for pair_ in ((0, 1), (2, 3)):
            gens = [do_head(gi, il, cur, nxt, h, vtm_c, f8c, A_s, Am, stil, acr, acb_sb, actc, ktm, PC, PB)
                    for h in pair_]
            while gens:
                nxt_gens = []
                for g_ in gens:
                    try:
                        next(g_)
                        nxt_gens.append(g_)
                    except StopIteration:
                        pass
                gens = nxt_gens

    cur = 0
    for gi in range(ng):
        tiles = [gi * ntg + k_ for k_ in range(ntg)]
        emit_ln_ufm(ctx, R, ufm_g, GT, D, tiles=tiles, col_base=gi * ntg)
        for nm_, wsb, dst in (("q", wq_sb, qfm), ("k", wk_sb, kfm2)):
            for oc in range(8):
                pp = P0 if oc % 2 == 0 else P1
                for c in range(nd):
                    r.op("tensor", lambda t, oc=oc, c=c, pp=pp, wsb=wsb, gi=gi: t.matmul(
                        pp[:, 0:GT], wsb[:, c * 1024 + oc * 128: c * 1024 + (oc + 1) * 128],
                        ufm_g[:, c * GT: c * GT + GT],
                        start=(c == 0), stop=(c == nd - 1)),
                        reads=[wsb.K, ufm_g.K], writes=[pp.K])
                r.op("scalar", lambda s, oc=oc, pp=pp, dst=dst: s.activation(
                    dst[:, oc * GT:(oc + 1) * GT], pp[:, 0:GT], AF.Silu, bias=0.0, scale=1.0),
                    reads=[pp.K], writes=[dst.k(oc * GT, (oc + 1) * GT)])
        prev_g = None
        for il in range(ntg):
            g_ = do_chunk(gi, il, cur)
            next(g_)
            if prev_g is not None:
                for _ in prev_g:
                    pass
            prev_g = g_
            cur = 1 - cur
        for _ in prev_g:
            pass
        for il in range(ntg):
            i = gi * ntg + il
            for dh in range(2):
                PP = POP[dh]
                for oc in range(8):
                    r.op("tensor", lambda t, il=il, dh=dh, oc=oc, PP=PP: t.matmul(
                        PP[:, 0:512], ofm_g[:, oc * GT + il * 128: oc * GT + (il + 1) * 128],
                        wo_sb[:, oc * 1024 + dh * 512: oc * 1024 + dh * 512 + 512],
                        start=(oc == 0), stop=(oc == 7)),
                        reads=[ofm_g.K, wo_sb.K], writes=[PP.K])
                rk = R.k(i * D + dh * 512, i * D + dh * 512 + 512)
                r.op("vector", lambda v, i=i, dh=dh, PP=PP: v.tensor_tensor(
                    R[:, i * D + dh * 512: i * D + dh * 512 + 512],
                    R[:, i * D + dh * 512: i * D + dh * 512 + 512], PP[:, 0:512], ALU.add),
                    reads=[PP.K, rk], writes=[rk])
    A.release()


from concourse.bass_utils import run_bass_kernel_spmd

_N_CORES = 8

_L0_ATTN_KEYS = ["L0_attn_Wq", "L0_attn_Wk", "L0_attn_Wv", "L0_attn_Wf",
                 "L0_attn_g_norm_w", "L0_attn_Wo"]
_L0_FFN_KEYS = ["L0_ffn_Wq", "L0_ffn_Wk", "L0_ffn_Ws", "L0_ffn_Wd",
                "L0_ffn_Wv", "L0_ffn_Wo"]
_L1_ATTN_KEYS = ["L1_attn_wq", "L1_attn_bq", "L1_attn_wk", "L1_attn_bk",
                 "L1_attn_wv", "L1_attn_bv", "L1_attn_wo", "L1_attn_bo"]
_L1_FFN_KEYS = ["L1_ffn_gate_w", "L1_ffn_w1", "L1_ffn_w2",
                "L1_ffn_sfc1", "L1_ffn_sfc2", "L1_ffn_sfc3"]

_WSHAPES = {
    "L0_attn_Wq": [1024, 1024], "L0_attn_Wk": [1024, 1024], "L0_attn_Wv": [1024, 1024],
    "L0_attn_Wf": [1024, 256], "L0_attn_g_norm_w": [256], "L0_attn_Wo": [1024, 1024],
    "L0_ffn_Wq": [1024, 512], "L0_ffn_Wk": [1024, 512], "L0_ffn_Ws": [1024, 40],
    "L0_ffn_Wd": [1024, 40], "L0_ffn_Wv": [5, 1024, 64], "L0_ffn_Wo": [5, 64, 1024],
    "L1_attn_wq": [16, 1024, 64], "L1_attn_bq": [16, 1, 64], "L1_attn_wk": [16, 1024, 64],
    "L1_attn_bk": [16, 1, 64], "L1_attn_wv": [16, 1024, 64], "L1_attn_bv": [16, 1, 64],
    "L1_attn_bo": [1024],
    "wot": [1024, 1024],
    "gate_wt": [1024, 16], "w1t": [16, 1024, 512], "w2t": [16, 512, 1024],
    "sfc1t": [1024, 2048], "sfc2t": [1024, 2048], "sfc3t": [2048, 1024],
}
_ALL_KEYS = _L0_ATTN_KEYS + _L0_FFN_KEYS + _L1_ATTN_KEYS + _L1_FFN_KEYS


def build_fused(nseq=2, T=2048, D=1024):
    nc = bass.Bass("TRN2", target_bir_lowering=False)
    x = nc.dram_tensor("x", [nseq, T, D], F32, kind="ExternalInput").ap()
    y = nc.dram_tensor("y", [nseq, T, D], F32, kind="ExternalOutput").ap()
    waps = {n: nc.dram_tensor(n, list(sh), F32, kind="ExternalInput").ap()
            for n, sh in _WSHAPES.items()}
    short = lambda keys, pre: {k[len(pre):]: waps[k] for k in keys if k in waps}
    w0a = short(_L0_ATTN_KEYS, "L0_attn_")
    w0f = short(_L0_FFN_KEYS, "L0_ffn_")
    w1a = short(_L1_ATTN_KEYS, "L1_attn_")
    w1a["wot"] = waps["wot"]
    w1f = {"gate_wt": waps["gate_wt"], "w1t": waps["w1t"], "w2t": waps["w2t"],
           "sfc1t": waps["sfc1t"], "sfc2t": waps["sfc2t"], "sfc3t": waps["sfc3t"]}
    es = ExitStack()
    with es:
        rec = Rec(nc)
        ctx = Ctx(nc, rec, es)
        ctx.setup_constants()
        nt = T // 128
        R = ctx.sb(nt * D, F32)
        for s in range(nseq):
            for i in range(nt):
                rec.op("sync", lambda sy, s=s, i=i: sy.dma_start(
                    out=R[:, i * D:(i + 1) * D], in_=x[s, i * 128:(i + 1) * 128, :]),
                    dma=True, writes=[R.k(i * D, (i + 1) * D)])
            emit_l0_attn(ctx, R, w0a)
            emit_l0_ffn(ctx, R, w0f)
            emit_l1_attn(ctx, R, w1a)
            emit_l1_ffn(ctx, R, w1f)
            for i in range(nt):
                rec.op("sync", lambda sy, s=s, i=i: sy.dma_start(
                    out=y[s, i * 128:(i + 1) * 128, :], in_=R[:, i * D:(i + 1) * D]),
                    dma=True, reads=[R.k(i * D, (i + 1) * D)])
        rec.barrier()
        rec.build(es)
        assert rec.check(), "schedulability check failed"
        with nc.Block() as block:
            rec.emit(block)
    return nc, rec


_fused_cache = {}


def kernel(**inputs):
    x = np.ascontiguousarray(inputs["L0_attn_x"], dtype=np.float32)
    B = x.shape[0]
    per = B // _N_CORES
    if per not in _fused_cache:
        _fused_cache[per] = build_fused(nseq=per)
    nc, rec = _fused_cache[per]
    wmap = {k: np.ascontiguousarray(inputs[k], dtype=np.float32) for k in _ALL_KEYS
            if k in _WSHAPES}
    wmap["wot"] = np.ascontiguousarray(np.float32(inputs["L1_attn_wo"]).T)
    wmap["gate_wt"] = np.ascontiguousarray(np.float32(inputs["L1_ffn_gate_w"]).T)
    wmap["w1t"] = np.ascontiguousarray(np.float32(inputs["L1_ffn_w1"]).transpose(0, 2, 1))
    wmap["w2t"] = np.ascontiguousarray(np.float32(inputs["L1_ffn_w2"]).transpose(0, 2, 1))
    wmap["sfc1t"] = np.ascontiguousarray(np.float32(inputs["L1_ffn_sfc1"]).T)
    wmap["sfc2t"] = np.ascontiguousarray(np.float32(inputs["L1_ffn_sfc2"]).T)
    wmap["sfc3t"] = np.ascontiguousarray(np.float32(inputs["L1_ffn_sfc3"]).T)
    in_maps = [{"x": np.ascontiguousarray(x[c * per:(c + 1) * per]), **wmap}
               for c in range(_N_CORES)]
    res = run_bass_kernel_spmd(nc, in_maps, core_ids=list(range(_N_CORES)))
    return np.concatenate([np.asarray(res.results[c]["y"], dtype=np.float32)
                           for c in range(_N_CORES)], axis=0)


# version 13
# speedup vs baseline: 1.0074x; 1.0074x over previous
import numpy as np
from contextlib import ExitStack
import concourse.bass as bass
import concourse.mybir as mybir

F32 = mybir.dt.float32
BF16 = mybir.dt.bfloat16
AF = mybir.ActivationFunctionType
ALU = mybir.AluOpType
AX = mybir.AxisListType

ENGINES = ("sync", "tensor", "vector", "scalar", "gpsimd")


def _overlap(t1, t2):
    if t1 == "*" or t2 == "*":
        return True
    r1, r2 = isinstance(t1, tuple), isinstance(t2, tuple)
    if r1 and r2:
        return t1[0] < t2[1] and t2[0] < t1[1]
    if r1 or r2:
        return False
    return t1 == t2


def _contains(t1, t2):
    if t1 == "*":
        return True
    if isinstance(t1, tuple) and isinstance(t2, tuple):
        return t1[0] <= t2[0] and t2[1] <= t1[1]
    return t1 == t2


class Op:
    __slots__ = ("idx", "eng", "fn", "deps", "dma", "tag",
                 "marked", "inc_idx", "dma_idx", "waits")

    def __init__(self, idx, eng, fn, dma, tag):
        self.idx = idx
        self.eng = eng
        self.fn = fn
        self.dma = dma
        self.tag = tag
        self.deps = set()
        self.marked = False
        self.inc_idx = None
        self.dma_idx = None
        self.waits = []


class _NameState:
    __slots__ = ("w", "r")

    def __init__(self):
        self.w = {}
        self.r = {}


class Rec:
    def __init__(self, nc):
        self.nc = nc
        self.ops = []
        self.names = {}

    def _access(self, o, name, tag, iswrite):
        st = self.names.setdefault(name, _NameState())
        if iswrite:
            for t2, wop in st.w.items():
                if _overlap(tag, t2):
                    o.deps.add(wop)
            for t2, rl in st.r.items():
                if _overlap(tag, t2):
                    for rr in rl:
                        o.deps.add(rr)
            for t2 in [t for t in st.w if _contains(tag, t) and t != tag]:
                del st.w[t2]
            for t2 in [t for t in st.r if _contains(tag, t)]:
                del st.r[t2]
            st.w[tag] = o.idx
        else:
            for t2, wop in st.w.items():
                if _overlap(tag, t2):
                    o.deps.add(wop)
            st.r.setdefault(tag, []).append(o.idx)

    def op(self, eng, fn, reads=(), writes=(), dma=False, tag="", deps=()):
        o = Op(len(self.ops), eng, fn, dma, tag)
        for n, t in reads:
            self._access(o, n, t, False)
        for n, t in writes:
            self._access(o, n, t, True)
        for d in deps:
            o.deps.add(d.idx)
        o.deps.discard(o.idx)
        self.ops.append(o)
        return o

    def barrier(self, tag="barrier"):
        last = {}
        for o in self.ops:
            last[o.eng] = o
        outs = []
        for e in ENGINES:
            if e not in last:
                continue
            o = Op(len(self.ops), e, None, False, tag)
            for e2, lo in last.items():
                o.deps.add(lo.idx)
            self.ops.append(o)
            outs.append(o)
        return outs

    def build(self, es):
        nc = self.nc
        esem = {e: es.enter_context(nc.semaphore(f"S{e}")) for e in ENGINES}
        dsem = {e: es.enter_context(nc.semaphore(f"D{e}")) for e in ENGINES}
        for o in self.ops:
            for d in o.deps:
                a = self.ops[d]
                if a.dma:
                    a.marked = True
                elif a.eng == "tensor" and o.eng == "tensor":
                    pass
                else:
                    a.marked = True
        cnt = {e: 0 for e in ENGINES}
        dcnt = {e: 0 for e in ENGINES}
        for o in self.ops:
            if o.dma:
                o.dma_idx = dcnt[o.eng]
                dcnt[o.eng] += 1
            elif o.marked:
                cnt[o.eng] += 1
                o.inc_idx = cnt[o.eng]
        ep_of = {}
        ep_end = {}
        cur_ep = {e: 0 for e in ENGINES}
        cur_cnt = {e: 0 for e in ENGINES}
        open_ep = {e: False for e in ENGINES}
        fence = {}
        for o in self.ops:
            for d in o.deps:
                a = self.ops[d]
                if a.dma and a.idx in ep_of:
                    q2 = a.eng
                    if open_ep[q2] and ep_of[a.idx] == (q2, cur_ep[q2]):
                        open_ep[q2] = False
            if o.dma:
                q = o.eng
                if not open_ep[q]:
                    cur_ep[q] += 1
                    open_ep[q] = True
                    if cur_cnt[q] > 0:
                        fence[o.idx] = 16 * cur_cnt[q]
                ep_of[o.idx] = (q, cur_ep[q])
                cur_cnt[q] += 1
                ep_end[(q, cur_ep[q])] = cur_cnt[q]
        wm_after = [None] * len(self.ops)
        last_on_engine = {}
        import os as _os
        _noprune = bool(_os.environ.get("REC_NOPRUNE"))
        for o in self.ops:
            wm = {} if _noprune else dict(last_on_engine.get(o.eng) or {})
            waits = {}
            if o.idx in fence:
                k = ("D", o.eng)
                v = fence[o.idx]
                if wm.get(k, 0) < v:
                    waits[k] = v
            for d in sorted(o.deps):
                a = self.ops[d]
                if a.dma:
                    k = ("D", a.eng)
                    v = 16 * ep_end[ep_of[a.idx]]
                elif a.eng == "tensor" and o.eng == "tensor":
                    continue
                else:
                    k = ("S", a.eng)
                    v = a.inc_idx
                if wm.get(k, 0) < v:
                    waits[k] = max(waits.get(k, 0), v)
            for d in sorted(o.deps):
                aw = wm_after[self.ops[d].idx]
                if aw:
                    for k2, v2 in aw.items():
                        if v2 > wm.get(k2, 0):
                            wm[k2] = v2
            for k, v in waits.items():
                if v > wm.get(k, 0):
                    wm[k] = v
            o.waits = [(k, v) for k, v in waits.items()]
            wm_after[o.idx] = wm
            last_on_engine[o.eng] = wm
        self._esem = esem
        self._dsem = dsem
        self.final_counts = dict(cnt)
        self.final_dma = {e: 16 * dcnt[e] for e in ENGINES}

    def check(self):
        streams = {e: [o for o in self.ops if o.eng == e] for e in ENGINES}
        pos = {e: 0 for e in ENGINES}
        sv = {}
        progress = True
        while progress:
            progress = False
            for e in ENGINES:
                while pos[e] < len(streams[e]):
                    o = streams[e][pos[e]]
                    if all(sv.get(k, 0) >= v for k, v in o.waits):
                        if o.dma:
                            sv[("D", e)] = sv.get(("D", e), 0) + 16
                        elif o.marked:
                            sv[("S", e)] = sv.get(("S", e), 0) + 1
                        pos[e] += 1
                        progress = True
                    else:
                        break
        stuck = {e: pos[e] < len(streams[e]) for e in ENGINES}
        if any(stuck.values()):
            print("DEADLOCK at build-check:")
            for e in ENGINES:
                if stuck[e]:
                    o = streams[e][pos[e]]
                    bad = [(k, v, sv.get(k, 0)) for k, v in o.waits if sv.get(k, 0) < v]
                    print(f"  {e}[{pos[e]}] op#{o.idx} tag={o.tag!r} dma={o.dma} waits: {bad}")
                    print(f"    deps: {sorted(o.deps)[-8:]}")
            return False
        return True

    def emit(self, block):
        esem, dsem = self._esem, self._dsem
        streams = {e: [o for o in self.ops if o.eng == e] for e in ENGINES}

        def mk(e):
            def body(engobj):
                for o in streams[e]:
                    for (kind, eng2), v in o.waits:
                        engobj.wait_ge(dsem[eng2] if kind == "D" else esem[eng2], v)
                    if o.fn is None:
                        continue
                    inst = o.fn(engobj)
                    if o.dma:
                        inst.then_inc(dsem[e], 16)
                    elif o.marked:
                        inst.then_inc(esem[e], 1)
            return body

        for e in ENGINES:
            if streams[e]:
                getattr(block, e)(mk(e))

    def stats(self):
        from collections import Counter
        c = Counter(o.eng for o in self.ops)
        m = Counter(o.eng for o in self.ops if o.marked)
        nw = sum(len(o.waits) for o in self.ops)
        return dict(total=len(self.ops), per_engine=dict(c), marked=dict(m), waits=nw)




class SB:
    __slots__ = ("ar", "off", "cols", "dt", "esz")

    def __init__(self, ar, off, cols, dt):
        self.ar = ar
        self.off = off
        self.cols = cols
        self.dt = dt
        self.esz = 2 if dt == BF16 else 4

    def __getitem__(self, idx):
        ps, cs = idx
        h = self.ar.typed[self.dt]
        base = self.off // self.esz
        a, b, stp = cs.indices(self.cols)
        assert stp == 1
        return h[ps, base + a: base + b]

    def ap3(self, ps, pattern, **kw):
        return self[ps, 0:self.cols].rearrange(pattern, **kw)

    def k(self, a=0, b=None):
        if b is None:
            b = self.cols
        lo = self.off + a * self.esz
        hi = self.off + b * self.esz
        if self.ar.is_psum:
            lo = (lo // 2048) * 2048
            hi = ((hi + 2047) // 2048) * 2048
        return (self.ar.name, (lo, hi))

    @property
    def K(self):
        return self.k()


class Arena:
    def __init__(self, nc, es, name, byts, space="sbuf", base_partition=0):
        self.name = name
        self.nbytes = byts
        self.is_psum = (space == "psum")
        if space == "sbuf":
            t = es.enter_context(nc.sbuf_tensor(name, [128, byts // 4], F32))
        else:
            t = es.enter_context(nc.psum_tensor(name, [128, byts // 4], F32))
        self.typed = {F32: t, BF16: t.bitcast(BF16)}
        self.off = 0
        self.marks = []

    def alloc(self, cols, dt=F32):
        esz = 2 if dt == BF16 else 4
        off = (self.off + 3) & ~3
        nb = cols * esz
        assert off + nb <= self.nbytes, f"{self.name} overflow: {off}+{nb} > {self.nbytes}"
        self.off = off + nb
        return SB(self, off, cols, dt)

    def at(self, off, cols, dt=F32):
        return SB(self, off, cols, dt)

    def mark(self):
        self.marks.append(self.off)

    def release(self):
        self.off = self.marks.pop()


class Ctx:
    def __init__(self, nc, rec, es, sbuf_bytes=212824):
        self.nc = nc
        self.rec = rec
        self.es = es
        self.A = Arena(nc, es, "AR", sbuf_bytes, "sbuf")
        self.P = Arena(nc, es, "PS", 16 * 1024, "psum")
        self._uid = 0

    def sb(self, cols, dt=F32):
        return self.A.alloc(cols, dt)

    def pbank(self, i, dt=F32, banks=1):
        return self.P.at(i * 2048, banks * 2048 // (2 if dt == BF16 else 4), dt)

    def setup_constants(self):
        r = self.rec
        self.ident = self.sb(128, BF16)
        self.tri = self.sb(128, BF16)
        self.ones_col = self.sb(1, BF16)
        self.ones_colf = self.sb(1, F32)
        self.eps_col = self.sb(1, F32)
        self.ones_row = self.sb(128, BF16)

        def aps(sbv, dt):
            return sbv[:, 0:128]

        r.op("gpsimd", lambda g: g.memset(self.ident[:, 0:128], 0.0), writes=[self.ident.K])
        r.op("gpsimd", lambda g: g.affine_select(
            out=self.ident[:, 0:128], in_=self.ident[:, 0:128], compare_op=ALU.not_equal,
            fill=1.0, base=0, pattern=[[-1, 128]], channel_multiplier=1),
            reads=[self.ident.K], writes=[self.ident.K])
        r.op("gpsimd", lambda g: g.memset(self.tri[:, 0:128], 0.0), writes=[self.tri.K])
        r.op("gpsimd", lambda g: g.affine_select(
            out=self.tri[:, 0:128], in_=self.tri[:, 0:128], compare_op=ALU.is_gt,
            fill=1.0, base=0, pattern=[[-1, 128]], channel_multiplier=1),
            reads=[self.tri.K], writes=[self.tri.K])
        r.op("gpsimd", lambda g: g.memset(self.ones_col[:, 0:1], 1.0), writes=[self.ones_col.K])
        r.op("gpsimd", lambda g: g.memset(self.ones_colf[:, 0:1], 1.0), writes=[self.ones_colf.K])
        r.op("gpsimd", lambda g: g.memset(self.eps_col[:, 0:1], 1e-5), writes=[self.eps_col.K])
        r.op("gpsimd", lambda g: g.memset(self.ones_row[:, 0:128], 1.0), writes=[self.ones_row.K])


def emit_ln_ufm(ctx, R, ufm, T=2048, D=1024, tiles=None, col_base=0, tile_cb=None):
    r = ctx.rec
    nt = T // 128
    nd = D // 128
    A = ctx.A
    A.mark()
    stats = ctx.sb(8 * nt, F32)
    sq = [ctx.sb(D, F32), ctx.sb(D, F32)]
    utm = [ctx.sb(D, BF16), ctx.sb(D, BF16)]
    pp = [ctx.pbank(6, BF16), ctx.pbank(7, BF16)]
    for i in (tiles if tiles is not None else range(nt)):
        Rk = R.k(i * D, (i + 1) * D)
        Rs = R[:, i * D:(i + 1) * D]
        il = i - col_base
        c0 = 8 * il
        sqt = sq[il % 2]
        utt = utm[il % 2]
        s_sum = stats[:, c0:c0 + 1]
        s_mean = stats[:, c0 + 2:c0 + 3]
        s_var = stats[:, c0 + 3:c0 + 4]
        s_rstd = stats[:, c0 + 5:c0 + 6]
        sk = stats.k(c0, c0 + 8)
        r.op("scalar", lambda s, Rs=Rs, sqt=sqt, o=s_sum: s.activation(
            sqt[:, 0:D], Rs, AF.Copy, bias=0.0, scale=1.0, accum_out=o),
            reads=[Rk], writes=[sqt.K, sk])
        r.op("scalar", lambda s, Rs=Rs, sqt=sqt, o=stats[:, c0 + 1:c0 + 2]: s.activation(
            sqt[:, 0:D], Rs, AF.Square, bias=0.0, scale=1.0, accum_out=o),
            reads=[Rk], writes=[sqt.K, sk])
        r.op("vector", lambda v, i_=s_sum, o=s_mean: v.tensor_scalar(o, i_, 1.0 / D, None, ALU.mult),
             reads=[sk], writes=[sk])
        r.op("vector", lambda v, m=s_mean, o=s_var: v.tensor_tensor(o, m, m, ALU.mult),
             reads=[sk], writes=[sk])
        r.op("vector", lambda v, a=stats[:, c0 + 1:c0 + 2], b=s_var: v.tensor_scalar(
            b, a, 1.0 / D, b, ALU.mult, ALU.subtract),
            reads=[sk], writes=[sk])
        r.op("scalar", lambda s, a=s_var, o=s_rstd: s.activation(
            o, a, AF.Ln, bias=ctx.eps_col[:, 0:1], scale=1.0),
            reads=[sk, ctx.eps_col.K], writes=[sk])
        r.op("scalar", lambda s, o=s_rstd: s.activation(o, o, AF.Exp, bias=0.0, scale=-0.5),
             reads=[sk], writes=[sk])
        r.op("vector", lambda v, Rs=Rs, m=s_mean, rs=s_rstd, utt=utt: v.tensor_scalar(
            utt[:, 0:D], Rs, m, rs, ALU.subtract, ALU.mult),
            reads=[Rk, sk], writes=[utt.K])
        for c in range(nd):
            p_ = pp[c % 2]
            r.op("tensor", lambda t, c=c, p_=p_, utt=utt: t.transpose(
                p_[:, 0:128], utt[:, c * 128:(c + 1) * 128], ctx.ident[:, 0:128]),
                reads=[utt.K, ctx.ident.K], writes=[p_.K])
            if c % 2 == 0:
                r.op("vector", lambda v, c=c, il=il, p_=p_: v.tensor_copy(
                    ufm[:, c * T + il * 128: c * T + (il + 1) * 128], p_[:, 0:128]),
                    reads=[p_.K], writes=[ufm.k(c * T + il * 128, c * T + (il + 1) * 128)])
            else:
                r.op("scalar", lambda s, c=c, il=il, p_=p_: s.copy(
                    ufm[:, c * T + il * 128: c * T + (il + 1) * 128], p_[:, 0:128]),
                    reads=[p_.K], writes=[ufm.k(c * T + il * 128, c * T + (il + 1) * 128)])
        if tile_cb is not None:
            tile_cb(i)
    A.release()


def load_bias_cols(ctx, b_dram, n_heads, cols_out):
    r = ctx.rec
    A = ctx.A
    A.mark()
    row = ctx.sb(n_heads * 64, F32)
    pp = ctx.pbank(6)
    r.op("sync", lambda s: s.dma_start(out=row[0:1, :], in_=b_dram.rearrange("h o e -> o (h e)")),
         dma=True, writes=[row.K])
    for p in range(n_heads // 2):
        r.op("tensor", lambda t, p=p: t.transpose(
            pp[:, 0:1], row[0:1, p * 128:(p + 1) * 128], ctx.ones_colf[0:1, 0:1]),
            reads=[row.K, ctx.ones_colf.K], writes=[pp.k(0, 1)])
        r.op("vector", lambda v, p=p: v.tensor_copy(cols_out[:, p:p + 1], pp[:, 0:1]),
             reads=[pp.k(0, 1)], writes=[cols_out.k(p, p + 1)])
    A.release()


def emit_l1_attn(ctx, R, w, T=2048, D=1024, H=16, DH=64):
    r = ctx.rec
    nt, nd, npair = T // 128, D // 128, H // 2
    A = ctx.A
    A.mark()

    ufm = ctx.sb(nd * T, BF16)
    emit_ln_ufm(ctx, R, ufm, T, D)

    kfm = [ctx.sb(T, BF16) for _ in range(npair)]
    vtm = ctx.sb(nt * 1040, BF16)
    for i_ in range(nt):
        r.op("gpsimd", lambda g, i_=i_: g.memset(
            vtm[:, i_ * 1040: (i_ + 1) * 1040].rearrange("p (h x) -> p h x", x=65)[:, :, 64:65], 1.0),
            writes=[vtm.k(i_ * 1040, (i_ + 1) * 1040)])
    warn = [ctx.sb(nd * 128, BF16) for _ in range(npair)]
    bq_cols = ctx.sb(npair, F32)
    bk_cols = ctx.sb(npair, F32)
    bo_row = ctx.sb(D, BF16)
    woT = ctx.sb(nd * 1024, BF16)
    A.mark()
    bv_row = ctx.sb(D, BF16)

    def load_w(nm, p, hh):
        h = 2 * p + hh
        src = w[nm][h].rearrange("(c p) e -> p c e", p=128)
        dst = warn[p].ap3(slice(None), "p (c x) -> p c x", x=128)[:, :, hh * 64:(hh + 1) * 64]
        r.op("gpsimd", lambda g: g.dma_start(out=dst, in_=src), dma=True,
             writes=[warn[p].K])

    for p in range(npair):
        load_w("wk", p, 0)
        load_w("wk", p, 1)
    wv_sb = woT
    for h in range(H):
        src = w["wv"][h].rearrange("(c p) e -> p c e", p=128)
        dst = wv_sb.ap3(slice(None), "p (c h x) -> p c h x", h=H, x=64)[:, :, h, :]
        r.op("gpsimd", lambda g, s_=src, d_=dst: g.dma_start(out=d_, in_=s_), dma=True,
             writes=[wv_sb.K])
    load_bias_cols(ctx, w["bq"], H, bq_cols)
    load_bias_cols(ctx, w["bk"], H, bk_cols)
    r.op("gpsimd", lambda g: g.dma_start(out=bv_row[0:1, :], in_=w["bv"].rearrange("h o e -> o (h e)")),
         dma=True, writes=[bv_row.K])
    r.op("gpsimd", lambda g: g.dma_start(out=bo_row[0:1, :], in_=w["bo"].rearrange("(o d) -> o d", o=1)),
         dma=True, writes=[bo_row.K])

    pk = [ctx.pbank(0), ctx.pbank(1)]
    for p in range(npair):
        for icol in range(T // 512):
            pp = pk[icol % 2]
            for c in range(nd):
                r.op("tensor", lambda t, p=p, icol=icol, c=c, pp=pp: t.matmul(
                    pp[:, 0:512], warn[p][:, c * 128:(c + 1) * 128],
                    ufm[:, c * T + icol * 512: c * T + icol * 512 + 512],
                    start=(c == 0), stop=(c == nd - 1)),
                    reads=[warn[p].K, ufm.K], writes=[pp.K])
            r.op("vector", lambda v, p=p, icol=icol, pp=pp: v.tensor_scalar(
                kfm[p][:, icol * 512:(icol + 1) * 512], pp[:, 0:512],
                bk_cols[:, p:p + 1], None, ALU.add),
                reads=[pp.K, bk_cols.k(p, p + 1)], writes=[kfm[p].k(icol * 512, (icol + 1) * 512)])
    for i in range(nt):
        for half in range(2):
            pp = pk[half % 2]
            r.op("tensor", lambda t, half=half, pp=pp: t.matmul(
                pp[:, 0:512], ctx.ones_row[0:1, 0:128], bv_row[0:1, half * 512:(half + 1) * 512],
                start=True, stop=False),
                reads=[ctx.ones_row.K, bv_row.K], writes=[pp.K])
            for c in range(nd):
                r.op("tensor", lambda t, i=i, half=half, c=c, pp=pp: t.matmul(
                    pp[:, 0:512], ufm[:, c * T + i * 128: c * T + (i + 1) * 128],
                    wv_sb[:, c * 1024 + half * 512: c * 1024 + half * 512 + 512],
                    start=False, stop=(c == nd - 1)),
                    reads=[ufm.K, wv_sb.K], writes=[pp.K])
            dsts = vtm[:, i * 1040 + half * 520: i * 1040 + half * 520 + 520].rearrange(
                "p (h x) -> p h x", x=65)[:, :, 0:64]
            r.op("vector", lambda v, i=i, half=half, pp=pp, dsts=dsts: v.tensor_copy(
                dsts, pp[:, 0:512].rearrange("p (h x) -> p h x", x=64)),
                reads=[pp.K], writes=[vtm.k(i * 1040 + half * 520, i * 1040 + half * 520 + 520)])
    A.release()
    for p in range(npair):
        load_w("wq", p, 0)
        load_w("wq", p, 1)
    r.op("gpsimd", lambda g: g.dma_start(
        out=woT.ap3(slice(None), "p (r d) -> p r d", d=1024),
        in_=w["wot"].rearrange("(r p) d -> p r d", p=128)),
        dma=True, writes=[woT.K])

    NIQ = 512
    qsb = ctx.sb(NIQ, BF16)
    _pA, _pB, _pC = ctx.sb(NIQ, BF16), ctx.sb(NIQ, BF16), ctx.sb(NIQ, BF16)
    ptb = [[_pA, _pB], [_pC, _pC]]
    att = ctx.sb(npair * NIQ, BF16)
    zbb = ctx.A.at(qsb.off, NIQ, BF16)
    zrow = ctx.A.at(_pC.off, NIQ, BF16)
    PS = [[ctx.pbank(0), ctx.pbank(1)], [ctx.pbank(2), ctx.pbank(3)]]
    PO = [ctx.pbank(4), ctx.pbank(5)]
    PQ = [ctx.pbank(0), ctx.pbank(1)]
    PAux = ctx.pbank(6)
    POut = [ctx.pbank(6), ctx.pbank(7)]

    def st_mm(p, hh, j):
        ps = PS[hh][j % 2]
        r.op("tensor", lambda t, p=p, hh=hh, j=j, ps=ps: t.matmul(
            ps[:, 0:NIQ], kfm[p][hh * 64:(hh + 1) * 64, j * 128:(j + 1) * 128],
            qsb[hh * 64:(hh + 1) * 64, 0:NIQ],
            start=True, stop=True, tile_position=(hh * 64, 0)),
            reads=[kfm[p].K, qsb.K], writes=[ps.K])

    for iq in range(T // NIQ):
        for p in range(npair):
            pq = PQ[0]
            for c in range(nd):
                r.op("tensor", lambda t, p=p, c=c, pq=pq, iq=iq: t.matmul(
                    pq[:, 0:NIQ], warn[p][:, c * 128:(c + 1) * 128],
                    ufm[:, c * T + iq * NIQ: c * T + iq * NIQ + NIQ],
                    start=(c == 0), stop=(c == nd - 1)),
                    reads=[warn[p].K, ufm.K], writes=[pq.K])
            r.op("vector", lambda v, p=p, pq=pq: v.tensor_scalar(
                qsb[:, 0:NIQ], pq[:, 0:NIQ], bq_cols[:, p:p + 1], None, ALU.add),
                reads=[pq.K, bq_cols.k(p, p + 1)], writes=[qsb.K])
            st_mm(p, 0, 0)
            st_mm(p, 1, 0)
            for j in range(nt):
                for hh in range(2):
                    ps = PS[hh][j % 2]
                    pt = ptb[hh][j % 2]
                    r.op("scalar", lambda s, ps=ps, pt=pt: s.activation(
                        pt[:, 0:NIQ], ps[:, 0:NIQ], AF.Exp, bias=0.0, scale=0.125),
                        reads=[ps.K], writes=[pt.K])
                if j + 1 < nt:
                    st_mm(p, 0, j + 1)
                    st_mm(p, 1, j + 1)
                for hh in range(2):
                    h = 2 * p + hh
                    po = PO[hh]
                    pt = ptb[hh][j % 2]
                    r.op("tensor", lambda t, h=h, j=j, po=po, pt=pt: t.matmul(
                        po[0:65, 0:NIQ], vtm[:, j * 1040 + h * 65: j * 1040 + (h + 1) * 65],
                        pt[:, 0:NIQ], start=(j == 0), stop=(j == nt - 1)),
                        reads=[vtm.K, pt.K], writes=[po.K])
            for hh in range(2):
                po = PO[hh]
                def _zrow(s, po=po):
                    with ctx.nc.allow_low_precision(reason="Z row staged bf16"):
                        return s.copy(zrow[0:1, 0:NIQ], po[64:65, 0:NIQ])
                r.op("scalar", _zrow, reads=[po.K], writes=[zrow.K])
                r.op("tensor", lambda t: t.matmul(PAux[0:64, 0:NIQ], ctx.ones_row[0:1, 0:64],
                                                  zrow[0:1, 0:NIQ], start=True, stop=True),
                     reads=[ctx.ones_row.K, zrow.K], writes=[PAux.K])
                def _recip(v):
                    with ctx.nc.allow_low_precision(reason="1/Z stored bf16"):
                        return v.reciprocal(zbb[0:64, 0:NIQ], PAux[0:64, 0:NIQ])
                r.op("vector", _recip, reads=[PAux.K], writes=[zbb.K])
                r.op("vector", lambda v, p=p, hh=hh, po=po: v.tensor_tensor(
                    att[hh * 64:(hh + 1) * 64, p * NIQ:(p + 1) * NIQ], po[0:64, 0:NIQ],
                    zbb[0:64, 0:NIQ], ALU.mult),
                    reads=[po.K, zbb.K],
                    writes=[att.k(p * NIQ, (p + 1) * NIQ)])
        for tt in range(NIQ // 128):
            i = iq * (NIQ // 128) + tt
            for dh in range(2):
                PP = POut[dh]
                r.op("tensor", lambda t, dh=dh, PP=PP: t.matmul(
                    PP[:, 0:512], ctx.ones_row[0:1, 0:128], bo_row[0:1, dh * 512:(dh + 1) * 512],
                    start=True, stop=False),
                    reads=[ctx.ones_row.K, bo_row.K], writes=[PP.K])
                for p in range(npair):
                    r.op("tensor", lambda t, tt=tt, dh=dh, p=p, PP=PP: t.matmul(
                        PP[:, 0:512], att[:, p * NIQ + tt * 128: p * NIQ + (tt + 1) * 128],
                        woT[:, p * 1024 + dh * 512: p * 1024 + dh * 512 + 512],
                        start=False, stop=(p == npair - 1)),
                        reads=[att.K, woT.K], writes=[PP.K])
                r.op("vector", lambda v, i=i, dh=dh, PP=PP: v.tensor_tensor(
                    R[:, i * 1024 + dh * 512: i * 1024 + dh * 512 + 512],
                    R[:, i * 1024 + dh * 512: i * 1024 + dh * 512 + 512], PP[:, 0:512], ALU.add),
                    reads=[PP.K, R.k(i * 1024 + dh * 512, i * 1024 + dh * 512 + 512)],
                    writes=[R.k(i * 1024 + dh * 512, i * 1024 + dh * 512 + 512)])
    A.release()


def build_standalone(emit_fn, wspecs, nseq=2, T=2048, D=1024):
    nc = bass.Bass("TRN2", target_bir_lowering=False)
    x = nc.dram_tensor("x", [nseq, T, D], F32, kind="ExternalInput").ap()
    y = nc.dram_tensor("y", [nseq, T, D], F32, kind="ExternalOutput").ap()
    waps = {n: nc.dram_tensor(n, list(sh), F32, kind="ExternalInput").ap()
            for n, sh in wspecs.items()}
    es = ExitStack()
    with es:
        rec = Rec(nc)
        ctx = Ctx(nc, rec, es)
        ctx.setup_constants()
        nt = T // 128
        R = ctx.sb(nt * D, F32)
        for s in range(nseq):
            for i in range(nt):
                rec.op("sync", lambda sy, s=s, i=i: sy.dma_start(
                    out=R[:, i * D:(i + 1) * D], in_=x[s, i * 128:(i + 1) * 128, :]),
                    dma=True, writes=[R.k(i * D, (i + 1) * D)])
            emit_fn(ctx, R, waps)
            for i in range(nt):
                rec.op("sync", lambda sy, s=s, i=i: sy.dma_start(
                    out=y[s, i * 128:(i + 1) * 128, :], in_=R[:, i * D:(i + 1) * D]),
                    dma=True, reads=[R.k(i * D, (i + 1) * D)])
            rec.barrier()
        rec.build(es)
        assert rec.check(), "schedulability check failed"
        with nc.Block() as block:
            rec.emit(block)
    return nc, rec


def emit_l0_ffn(ctx, R, w, T=2048, D=1024, H=8, DH=64, E=5):
    r = ctx.rec
    nt, nd, npair = T // 128, D // 128, H // 2
    A = ctx.A
    A.mark()
    ufm = ctx.sb(nd * T, BF16)

    wq_sb = [ctx.sb(nd * 128, BF16) for _ in range(npair)]
    wk_sb = [ctx.sb(nd * 128, BF16) for _ in range(npair)]
    wsd = ctx.sb(nd * 80, BF16)
    wv_sb = ctx.sb(nd * E * 64, BF16)
    wo_sb = ctx.sb(E * 1024, BF16)
    for p in range(npair):
        for nm, buf in (("Wq", wq_sb), ("Wk", wk_sb)):
            src = w[nm][:, p * 128:(p + 1) * 128].rearrange("(c p) e -> p c e", p=128)
            dst = buf[p].ap3(slice(None), "p (c x) -> p c x", x=128)
            r.op("gpsimd", lambda g, s_=src, d_=dst: g.dma_start(out=d_, in_=s_), dma=True,
                 writes=[buf[p].K])
    for nm, off in (("Ws", 0), ("Wd", 40)):
        src = w[nm].rearrange("(c p) e -> p c e", p=128)
        dst = wsd.ap3(slice(None), "p (c x) -> p c x", x=80)[:, :, off:off + 40]
        r.op("gpsimd", lambda g, s_=src, d_=dst: g.dma_start(out=d_, in_=s_), dma=True,
             writes=[wsd.K])
    for e in range(E):
        src = w["Wv"][e].rearrange("(c p) x -> p c x", p=128)
        dst = wv_sb.ap3(slice(None), "p (c e x) -> p c e x", e=E, x=64)[:, :, e, :]
        r.op("gpsimd", lambda g, s_=src, d_=dst: g.dma_start(out=d_, in_=s_), dma=True,
             writes=[wv_sb.K])
    for e in range(E):
        r.op("gpsimd", lambda g, e=e: g.dma_start(out=wo_sb[0:64, e * 1024:(e + 1) * 1024],
                                                  in_=w["Wo"][e]), dma=True, writes=[wo_sb.K])

    kfm = [ctx.sb(T, BF16) for _ in range(npair)]
    vtm = ctx.sb(nt * H * 65, BF16)
    for i_ in range(nt):
        r.op("gpsimd", lambda g, i_=i_: g.memset(
            vtm[:, i_ * 520: (i_ + 1) * 520].rearrange("p (h x) -> p h x", x=65)[:, :, 64:65], 1.0),
            writes=[vtm.k(i_ * 520, (i_ + 1) * 520)])
    maskd = ctx.sb(nt * H * E, F32)
    P0 = ctx.pbank(0)
    P1 = ctx.pbank(1)

    A.mark()
    gl = ctx.sb(80, F32)
    m1h = ctx.sb(H, F32)
    m2h = ctx.sb(H, F32)
    glm = ctx.sb(40, F32)
    sgm = ctx.sb(40, F32)
    maskv = ctx.sb(40, F32)
    xvt = ctx.sb(E * 64, BF16)
    vmt = ctx.sb(512, BF16)
    PG = ctx.pbank(2)
    PX = ctx.pbank(3)
    def gate_vmix_tile(i):
        for c in range(nd):
            r.op("tensor", lambda t, i=i, c=c: t.matmul(
                PG[:, 0:80], ufm[:, c * T + i * 128: c * T + (i + 1) * 128],
                wsd[:, c * 80:(c + 1) * 80], start=(c == 0), stop=(c == nd - 1)),
                reads=[ufm.K, wsd.K], writes=[PG.K])
        r.op("vector", lambda v: v.tensor_copy(gl[:, 0:80], PG[:, 0:80]),
             reads=[PG.K], writes=[gl.K])
        for c in range(nd):
            r.op("tensor", lambda t, i=i, c=c: t.matmul(
                PX[:, 0:E * 64], ufm[:, c * T + i * 128: c * T + (i + 1) * 128],
                wv_sb[:, c * E * 64:(c + 1) * E * 64], start=(c == 0), stop=(c == nd - 1)),
                reads=[ufm.K, wv_sb.K], writes=[PX.K])
        r.op("vector", lambda v: v.tensor_copy(xvt[:, 0:E * 64], PX[:, 0:E * 64]),
             reads=[PX.K], writes=[xvt.K])
        for goff, is_v in ((0, True), (40, False)):
            g3 = gl[:, goff:goff + 40].rearrange("p (h e) -> p h e", e=E)
            glk = gl.k(goff, goff + 40)
            r.op("vector", lambda v, g3=g3: v.reduce_max(m1h[:, 0:H], g3, axis=AX.X),
                 reads=[glk], writes=[m1h.K])
            m1b = m1h[:, 0:H].rearrange("p h -> p h ()").broadcast_to([128, H, E])
            r.op("vector", lambda v, g3=g3, m1b=m1b: v.tensor_tensor(
                glm[:, 0:40].rearrange("p (h e) -> p h e", e=E), g3, m1b, ALU.is_ge),
                reads=[glk, m1h.K], writes=[glm.K])
            r.op("vector", lambda v, goff=goff: v.scalar_tensor_tensor(
                glm[:, 0:40], glm[:, 0:40], -1e30, gl[:, goff:goff + 40], ALU.mult, ALU.add),
                reads=[glm.K, glk], writes=[glm.K])
            r.op("vector", lambda v: v.reduce_max(m2h[:, 0:H], glm[:, 0:40].rearrange("p (h e) -> p h e", e=E), axis=AX.X),
                 reads=[glm.K], writes=[m2h.K])
            m2b = m2h[:, 0:H].rearrange("p h -> p h ()").broadcast_to([128, H, E])
            if is_v:
                r.op("vector", lambda v, g3=g3, m2b=m2b: v.tensor_tensor(
                    maskv[:, 0:40].rearrange("p (h e) -> p h e", e=E), g3, m2b, ALU.is_ge),
                    reads=[glk, m2h.K], writes=[maskv.K])
            else:
                dk = maskd.k(i * 40, (i + 1) * 40)
                r.op("vector", lambda v, g3=g3, m2b=m2b, i=i: v.tensor_tensor(
                    maskd[:, i * 40:(i + 1) * 40].rearrange("p (h e) -> p h e", e=E), g3, m2b, ALU.is_ge),
                    reads=[glk, m2h.K], writes=[dk])
        r.op("scalar", lambda s: s.activation(sgm[:, 0:40], gl[:, 0:40], AF.Exp, bias=0.0, scale=-1.0),
             reads=[gl.K], writes=[sgm.K])
        r.op("vector", lambda v: v.tensor_scalar(sgm[:, 0:40], sgm[:, 0:40], 1.0, None, ALU.add),
             reads=[sgm.K], writes=[sgm.K])
        r.op("vector", lambda v: v.reciprocal(sgm[:, 0:40], sgm[:, 0:40]),
             reads=[sgm.K], writes=[sgm.K])
        r.op("vector", lambda v: v.tensor_tensor(sgm[:, 0:40], sgm[:, 0:40], maskv[:, 0:40], ALU.mult),
             reads=[sgm.K, maskv.K], writes=[sgm.K])
        v3d = vtm[:, i * 520: (i + 1) * 520].rearrange("p (h x) -> p h x", x=65)[:, :, 0:64]
        vk_all = vtm.k(i * 520, (i + 1) * 520)
        for e in range(E):
            xv_b = xvt[:, e * 64:(e + 1) * 64].rearrange("p (o x) -> p o x", o=1).broadcast_to([128, H, 64])
            sg_b = sgm[:, 0:40].rearrange("p (h e) -> p h e", e=E)[:, :, e:e + 1].broadcast_to([128, H, 64])
            if e == 0:
                r.op("vector", lambda v, xv_b=xv_b, sg_b=sg_b, v3d=v3d: v.tensor_tensor(
                    v3d, xv_b, sg_b, ALU.mult),
                    reads=[xvt.K, sgm.K], writes=[vk_all])
            else:
                r.op("vector", lambda v, xv_b=xv_b, sg_b=sg_b: v.tensor_tensor(
                    vmt[:, 0:512].rearrange("p (h x) -> p h x", x=64), xv_b, sg_b, ALU.mult),
                    reads=[xvt.K, sgm.K], writes=[vmt.K])
                r.op("vector", lambda v, v3d=v3d: v.tensor_tensor(
                    v3d, v3d, vmt[:, 0:512].rearrange("p (h x) -> p h x", x=64), ALU.add),
                    reads=[vmt.K, vk_all], writes=[vk_all])

    emit_ln_ufm(ctx, R, ufm, T, D, tile_cb=gate_vmix_tile)
    for p in range(npair):
        for icol in range(T // 512):
            pp = P0 if icol % 2 == 0 else P1
            for c in range(nd):
                r.op("tensor", lambda t, p=p, icol=icol, c=c, pp=pp: t.matmul(
                    pp[:, 0:512], wk_sb[p][:, c * 128:(c + 1) * 128],
                    ufm[:, c * T + icol * 512: c * T + icol * 512 + 512],
                    start=(c == 0), stop=(c == nd - 1)),
                    reads=[wk_sb[p].K, ufm.K], writes=[pp.K])
            r.op("scalar", lambda s, p=p, icol=icol, pp=pp: s.activation(
                kfm[p][:, icol * 512:(icol + 1) * 512], pp[:, 0:512], AF.Copy, bias=0.0, scale=1.0),
                reads=[pp.K], writes=[kfm[p].k(icol * 512, (icol + 1) * 512)])

    A.release()

    NIQ = 512
    A.mark()
    qsb = ctx.sb(NIQ, BF16)
    ptb = [[ctx.sb(NIQ, BF16) for _ in range(2)] for _ in range(2)]
    attm = ctx.sb((NIQ // 128) * 512, BF16)
    oev = ctx.sb(NIQ, BF16)
    zcol = ctx.sb(1, F32)
    zt = ctx.sb(E * 64, F32)
    ztb = ctx.sb(E * 64, BF16)
    zts = ctx.sb(E * 128, BF16)
    PS = [[ctx.pbank(0), ctx.pbank(1)], [ctx.pbank(2), ctx.pbank(3)]]
    PSpair = [ctx.P.at(0, 1024, F32), ctx.P.at(4096, 1024, F32)]
    ptbP = [ctx.sb(1024, BF16), ctx.sb(1024, BF16)]
    PO = [ctx.pbank(4), ctx.pbank(5)]
    PQ = ctx.pbank(0)
    PT = ctx.pbank(6, BF16)
    POut = [ctx.pbank(4), ctx.pbank(5)]

    def st_mm(p, hh, j):
        ps = PS[j % 2][hh]
        r.op("tensor", lambda t, p=p, hh=hh, j=j, ps=ps: t.matmul(
            ps[:, 0:NIQ], kfm[p][hh * 64:(hh + 1) * 64, j * 128:(j + 1) * 128],
            qsb[hh * 64:(hh + 1) * 64, 0:NIQ],
            start=True, stop=True, tile_position=(hh * 64, 0)),
            reads=[kfm[p].K, qsb.K], writes=[ps.K])

    for iq in range(T // NIQ):
        for p in range(npair):
            for c in range(nd):
                r.op("tensor", lambda t, p=p, c=c, iq=iq: t.matmul(
                    PQ[:, 0:NIQ], wq_sb[p][:, c * 128:(c + 1) * 128],
                    ufm[:, c * T + iq * NIQ: c * T + iq * NIQ + NIQ],
                    start=(c == 0), stop=(c == nd - 1)),
                    reads=[wq_sb[p].K, ufm.K], writes=[PQ.K])
            r.op("scalar", lambda s: s.activation(qsb[:, 0:NIQ], PQ[:, 0:NIQ], AF.Copy, bias=0.0, scale=1.0),
                 reads=[PQ.K], writes=[qsb.K])
            st_mm(p, 0, 0)
            st_mm(p, 1, 0)
            for j in range(nt):
                psp = PSpair[j % 2]
                ptp = ptbP[j % 2]
                r.op("scalar", lambda s, psp=psp, ptp=ptp: s.activation(
                    ptp[:, 0:1024], psp[:, 0:1024], AF.Exp, bias=0.0, scale=0.125),
                    reads=[psp.K], writes=[ptp.K])
                if j + 1 < nt:
                    st_mm(p, 0, j + 1)
                    st_mm(p, 1, j + 1)
                for hh in range(2):
                    h = 2 * p + hh
                    po = PO[hh]
                    r.op("tensor", lambda t, h=h, j=j, po=po, ptp=ptp, hh=hh: t.matmul(
                        po[0:65, 0:NIQ], vtm[:, j * 520 + h * 65: j * 520 + (h + 1) * 65],
                        ptp[:, hh * NIQ:(hh + 1) * NIQ], start=(j == 0), stop=(j == nt - 1)),
                        reads=[vtm.K, ptp.K], writes=[po.K])
            for hh in range(2):
                h = 2 * p + hh
                po = PO[hh]
                r.op("scalar", lambda s, po=po: s.activation(oev[0:65, 0:NIQ], po[0:65, 0:NIQ],
                                                             AF.Copy, bias=0.0, scale=1.0),
                     reads=[po.K], writes=[oev.K])
                for ic in range(NIQ // 128):
                    r.op("tensor", lambda t, ic=ic: t.transpose(
                        PT[:, 0:65], oev[0:65, ic * 128:(ic + 1) * 128], ctx.ident[0:65, 0:65]),
                        reads=[oev.K, ctx.ident.K], writes=[PT.k(0, 128)])
                    r.op("vector", lambda v: v.reciprocal(zcol[:, 0:1], PT[:, 64:65]),
                         reads=[PT.k(0, 128)], writes=[zcol.K])
                    ak = attm.k(ic * 512 + h * 64, ic * 512 + (h + 1) * 64)
                    r.op("vector", lambda v, h=h, ic=ic: v.tensor_scalar(
                        attm[:, ic * 512 + h * 64: ic * 512 + (h + 1) * 64],
                        PT[:, 0:64], zcol[:, 0:1], None, ALU.mult),
                        reads=[PT.k(0, 128), zcol.K], writes=[ak])
        for ic in range(NIQ // 128):
            i = iq * (NIQ // 128) + ic
            for e in range(E):
                for h in range(H):
                    zsl = zt[:, e * 64:(e + 1) * 64]
                    zk = zt.k(e * 64, (e + 1) * 64)
                    msl = maskd[:, i * 40 + h * E + e: i * 40 + h * E + e + 1]
                    asl = attm[:, ic * 512 + h * 64: ic * 512 + (h + 1) * 64]
                    if h == 0:
                        r.op("vector", lambda v, zsl=zsl, msl=msl, asl=asl: v.tensor_scalar(
                            zsl, asl, msl, None, ALU.mult),
                            reads=[attm.K, maskd.K], writes=[zk])
                    else:
                        r.op("vector", lambda v, zsl=zsl, msl=msl, asl=asl: v.scalar_tensor_tensor(
                            zsl, asl, msl, zsl, ALU.mult, ALU.add),
                            reads=[attm.K, maskd.K, zk], writes=[zk])
            r.op("vector", lambda v: v.tensor_copy(ztb[:, 0:E * 64], zt[:, 0:E * 64]),
                 reads=[zt.K], writes=[ztb.K])
            for e in range(E):
                r.op("tensor", lambda t, e=e: t.transpose(
                    PT[0:64, e * 128:(e + 1) * 128], ztb[:, e * 64:(e + 1) * 64], ctx.ident[:, 0:128]),
                    reads=[ztb.K, ctx.ident.K], writes=[PT.k(e * 128, (e + 1) * 128)])
                r.op("vector", lambda v, e=e: v.tensor_copy(
                    zts[0:64, e * 128:(e + 1) * 128], PT[0:64, e * 128:(e + 1) * 128]),
                    reads=[PT.k(e * 128, (e + 1) * 128)], writes=[zts.k(e * 128, (e + 1) * 128)])
            for dh in range(2):
                PP = POut[dh]
                for e in range(E):
                    r.op("tensor", lambda t, e=e, dh=dh, PP=PP: t.matmul(
                        PP[:, 0:512], zts[0:64, e * 128:(e + 1) * 128],
                        wo_sb[0:64, e * 1024 + dh * 512: e * 1024 + dh * 512 + 512],
                        start=(e == 0), stop=(e == E - 1)),
                        reads=[zts.K, wo_sb.K], writes=[PP.K])
                r.op("vector", lambda v, i=i, dh=dh, PP=PP: v.tensor_tensor(
                    R[:, i * 1024 + dh * 512: i * 1024 + dh * 512 + 512],
                    R[:, i * 1024 + dh * 512: i * 1024 + dh * 512 + 512], PP[:, 0:512], ALU.add),
                    reads=[PP.K, R.k(i * 1024 + dh * 512, i * 1024 + dh * 512 + 512)],
                    writes=[R.k(i * 1024 + dh * 512, i * 1024 + dh * 512 + 512)])
    A.release()
    A.release()


def _build_transposed(ctx, dst, src_ap_fn, n_in_tiles, cols_in, stages, PTa, PTb,
                      dst_col_fn):
    r = ctx.rec
    np_ = cols_in // 128
    if not isinstance(stages, (list, tuple)):
        stages = [stages, stages]
    for it in range(n_in_tiles):
        stage = stages[it % len(stages)]
        r.op("gpsimd", lambda g, it=it, stage=stage: g.dma_start(
            out=stage[:, 0:cols_in], in_=src_ap_fn(it)),
            dma=True, writes=[stage.k(0, cols_in)])
        for pc in range(np_):
            PT = PTa if pc % 2 == 0 else PTb
            r.op("tensor", lambda t, pc=pc, PT=PT, stage=stage: t.transpose(
                PT[:, 0:128], stage[:, pc * 128:(pc + 1) * 128], ctx.ident[:, 0:128]),
                reads=[stage.k(0, cols_in), ctx.ident.K], writes=[PT.k(0, 128)])
            c0 = dst_col_fn(it, pc)
            r.op("vector", lambda v, c0=c0, PT=PT: v.tensor_copy(
                dst[:, c0:c0 + 128], PT[:, 0:128]),
                reads=[PT.k(0, 128)], writes=[dst.k(c0, c0 + 128)])


def emit_l1_ffn(ctx, R, w, T=2048, D=1024, E=16, HD=512, S=2048, phases=(1, 1, 1), gate_steps=99):
    r = ctx.rec
    nt, nd = T // 128, D // 128
    do_gate, do_shared, do_moe = phases
    A = ctx.A
    A.mark()
    ufm = ctx.sb(nd * T, BF16)
    moeacc = ctx.sb(nt * D, F32)
    disp = ctx.sb(nt * E, F32)

    PTa, PTb = ctx.pbank(6, BF16), ctx.pbank(7, BF16)
    P0, P1 = ctx.pbank(0), ctx.pbank(1)

    A.mark()
    gwT = ctx.sb(nd * E, BF16)
    r.op("gpsimd", lambda g: g.dma_start(
        out=gwT.ap3(slice(None), "p (c e) -> p c e", e=E),
        in_=w["gate_wt"].rearrange("(c p) e -> p c e", p=128)), dma=True,
        writes=[gwT.K])
    gl = ctx.sb(E, F32)
    glm = ctx.sb(E, F32)
    eg = ctx.sb(E, F32)
    msk = ctx.sb(E, F32)
    m1 = ctx.sb(1, F32)
    nm1 = ctx.sb(1, F32)
    m2 = ctx.sb(1, F32)
    e2 = ctx.sb(1, F32)

    def gate_tile(i):
        if not do_gate:
            return
        for c in range(nd):
            r.op("tensor", lambda t, i=i, c=c: t.matmul(
                P0[:, 0:E], ufm[:, c * T + i * 128: c * T + (i + 1) * 128],
                gwT[:, c * E:(c + 1) * E], start=(c == 0), stop=(c == nd - 1)),
                reads=[ufm.k(c * T + i * 128, c * T + (i + 1) * 128), gwT.K], writes=[P0.K])
        r.op("vector", lambda v: v.tensor_copy(gl[:, 0:E], P0[:, 0:E]),
             reads=[P0.K], writes=[gl.K])
        r.op("vector", lambda v: v.reduce_max(m1[:, 0:1], gl[:, 0:E], axis=AX.X),
             reads=[gl.K], writes=[m1.K])
        r.op("vector", lambda v: v.tensor_scalar(glm[:, 0:E], gl[:, 0:E], m1[:, 0:1], -1e30, ALU.is_ge, ALU.mult),
             reads=[gl.K, m1.K], writes=[glm.K])
        r.op("vector", lambda v: v.tensor_tensor(glm[:, 0:E], glm[:, 0:E], gl[:, 0:E], ALU.add),
             reads=[glm.K, gl.K], writes=[glm.K])
        r.op("vector", lambda v: v.reduce_max(m2[:, 0:1], glm[:, 0:E], axis=AX.X),
             reads=[glm.K], writes=[m2.K])
        r.op("vector", lambda v: v.tensor_scalar(msk[:, 0:E], gl[:, 0:E], m2[:, 0:1], None, ALU.is_ge),
             reads=[gl.K, m2.K], writes=[msk.K])
        r.op("vector", lambda v: v.tensor_scalar(nm1[:, 0:1], m1[:, 0:1], -1.0, None, ALU.mult),
             reads=[m1.K], writes=[nm1.K])
        r.op("scalar", lambda s: s.activation(eg[:, 0:E], gl[:, 0:E], AF.Exp, bias=nm1[:, 0:1], scale=1.0),
             reads=[gl.K, nm1.K], writes=[eg.K])
        r.op("scalar", lambda s: s.activation(e2[:, 0:1], m2[:, 0:1], AF.Exp, bias=nm1[:, 0:1], scale=1.0),
             reads=[m2.K, nm1.K], writes=[e2.K])
        r.op("vector", lambda v: v.tensor_scalar(e2[:, 0:1], e2[:, 0:1], 1.0, None, ALU.add),
             reads=[e2.K], writes=[e2.K])
        r.op("vector", lambda v: v.reciprocal(e2[:, 0:1], e2[:, 0:1]),
             reads=[e2.K], writes=[e2.K])
        r.op("vector", lambda v: v.tensor_tensor(eg[:, 0:E], eg[:, 0:E], msk[:, 0:E], ALU.mult),
             reads=[eg.K, msk.K], writes=[eg.K])
        r.op("vector", lambda v, i=i: v.tensor_scalar(disp[:, i * E:(i + 1) * E], eg[:, 0:E], e2[:, 0:1], None, ALU.mult),
             reads=[eg.K, e2.K], writes=[disp.k(i * E, (i + 1) * E)])

    emit_ln_ufm(ctx, R, ufm, T, D, tile_cb=gate_tile)
    A.release()

    if not do_gate:
        pass
    A.mark()
    SLAB = 512
    sfc3T = ctx.sb((SLAB // 128) * D, BF16)
    sfc1T = ctx.sb(8 * SLAB, BF16)
    sfc2T = ctx.sb(8 * SLAB, BF16)
    msb2 = [ctx.sb((SLAB // 128) * 512, BF16) for _ in range(2)]
    msil = ctx.sb(512, BF16)
    PG1, PG2 = ctx.pbank(2), ctx.pbank(3)
    PSH = [ctx.pbank(4), ctx.pbank(5)]
    for sl in (range(S // SLAB) if do_shared else []):
        for nm, dstb in (("sfc1t", sfc1T), ("sfc2t", sfc2T)):
            r.op("gpsimd", lambda g, nm=nm, sl=sl, dstb=dstb: g.dma_start(
                out=dstb.ap3(slice(None), "p (c s) -> p c s", s=SLAB),
                in_=w[nm][:, sl * SLAB:(sl + 1) * SLAB].rearrange("(c p) s -> p c s", p=128)),
                dma=True, writes=[dstb.K])
        r.op("gpsimd", lambda g, sl=sl: g.dma_start(
            out=sfc3T.ap3(slice(None), "p (sc d) -> p sc d", d=D),
            in_=w["sfc3t"][sl * SLAB:(sl + 1) * SLAB, :].rearrange("(sc p) d -> p sc d", p=128)),
            dma=True, writes=[sfc3T.K])
        for tb in range(T // 512):
            msb = msb2[tb % 2]
            for sc in range(SLAB // 128):
                for c in range(nd):
                    r.op("tensor", lambda t, sl=sl, tb=tb, sc=sc, c=c: t.matmul(
                        PG1[:, 0:512], sfc1T[:, c * SLAB + sc * 128: c * SLAB + (sc + 1) * 128],
                        ufm[:, c * T + tb * 512: c * T + tb * 512 + 512],
                        start=(c == 0), stop=(c == nd - 1)),
                        reads=[sfc1T.K, ufm.K], writes=[PG1.K])
                for c in range(nd):
                    r.op("tensor", lambda t, sl=sl, tb=tb, sc=sc, c=c: t.matmul(
                        PG2[:, 0:512], sfc2T[:, c * SLAB + sc * 128: c * SLAB + (sc + 1) * 128],
                        ufm[:, c * T + tb * 512: c * T + tb * 512 + 512],
                        start=(c == 0), stop=(c == nd - 1)),
                        reads=[sfc2T.K, ufm.K], writes=[PG2.K])
                r.op("scalar", lambda s: s.activation(msil[:, 0:512], PG1[:, 0:512], AF.Silu, bias=0.0, scale=1.0),
                     reads=[PG1.K], writes=[msil.K])
                r.op("vector", lambda v, sc=sc, msb=msb: v.tensor_tensor(
                    msb[:, sc * 512:(sc + 1) * 512], msil[:, 0:512], PG2[:, 0:512], ALU.mult),
                    reads=[msil.K, PG2.K], writes=[msb.k(sc * 512, (sc + 1) * 512)])
            for tt in range(4):
                i = tb * 4 + tt
                for dh in range(2):
                    PP = PSH[dh]
                    for sc in range(SLAB // 128):
                        r.op("tensor", lambda t, tt=tt, dh=dh, sc=sc, PP=PP, msb=msb: t.matmul(
                            PP[:, 0:512], msb[:, sc * 512 + tt * 128: sc * 512 + tt * 128 + 128],
                            sfc3T[:, sc * 1024 + dh * 512: sc * 1024 + dh * 512 + 512],
                            start=(sc == 0), stop=(sc == SLAB // 128 - 1)),
                            reads=[msb.k(sc * 512 + tt * 128, sc * 512 + tt * 128 + 128), sfc3T.K],
                            writes=[PP.K])
                    ck = moeacc.k(i * D + dh * 512, i * D + dh * 512 + 512)
                    if sl == 0:
                        r.op("vector", lambda v, i=i, dh=dh, PP=PP: v.tensor_copy(
                            moeacc[:, i * D + dh * 512: i * D + dh * 512 + 512], PP[:, 0:512]),
                            reads=[PP.K], writes=[ck])
                    else:
                        r.op("vector", lambda v, i=i, dh=dh, PP=PP: v.tensor_tensor(
                            moeacc[:, i * D + dh * 512: i * D + dh * 512 + 512],
                            moeacc[:, i * D + dh * 512: i * D + dh * 512 + 512], PP[:, 0:512], ALU.add),
                            reads=[PP.K, ck], writes=[ck])
    A.release()

    A.mark()
    w1T2 = [ctx.sb(8 * 512, BF16) for _ in range(2)]
    w2T2 = [ctx.sb(4 * 1024, BF16) for _ in range(2)]
    hw2 = [ctx.sb(4 * 512, BF16), ctx.sb(4 * 512, BF16)]
    PH = [ctx.pbank(2), ctx.pbank(3)]
    PM = [ctx.pbank(4), ctx.pbank(5)]

    def build_we(e):
        b1, b2 = w1T2[e % 2], w2T2[e % 2]
        r.op("gpsimd", lambda g, e=e, b1=b1: g.dma_start(
            out=b1.ap3(slice(None), "p (c h) -> p c h", h=HD),
            in_=w["w1t"][e].rearrange("(c p) h -> p c h", p=128)),
            dma=True, writes=[b1.K])
        r.op("gpsimd", lambda g, e=e, b2=b2: g.dma_start(
            out=b2.ap3(slice(None), "p (hc d) -> p hc d", d=D),
            in_=w["w2t"][e].rearrange("(hc p) d -> p hc d", p=128)),
            dma=True, writes=[b2.K])

    if do_moe:
        build_we(0)
    for e in (range(E) if do_moe else []):
        w1T = w1T2[e % 2]
        w2T = w2T2[e % 2]
        if e + 1 < E:
            build_we(e + 1)
        for tb in range(T // 512):
            hw = hw2[tb % 2]
            for hc in range(4):
                PHx = PH[hc % 2]
                for c in range(nd):
                    r.op("tensor", lambda t, e=e, tb=tb, hc=hc, c=c, PHx=PHx, w1T=w1T: t.matmul(
                        PHx[:, 0:512], w1T[:, c * 512 + hc * 128: c * 512 + (hc + 1) * 128],
                        ufm[:, c * T + tb * 512: c * T + tb * 512 + 512],
                        start=(c == 0), stop=(c == nd - 1)),
                        reads=[w1T.K, ufm.K], writes=[PHx.K])
                r.op("scalar", lambda s, hc=hc, PHx=PHx, hw=hw: s.activation(
                    hw[:, hc * 512:(hc + 1) * 512], PHx[:, 0:512], AF.Silu, bias=0.0, scale=1.0),
                    reads=[PHx.K], writes=[hw.k(hc * 512, (hc + 1) * 512)])
            for tt in range(4):
                i = tb * 4 + tt
                for dh in range(2):
                    PMx = PM[dh]
                    for hc in range(4):
                        r.op("tensor", lambda t, tt=tt, dh=dh, hc=hc, PMx=PMx, hw=hw, w2T=w2T: t.matmul(
                            PMx[:, 0:512], hw[:, hc * 512 + tt * 128: hc * 512 + tt * 128 + 128],
                            w2T[:, hc * 1024 + dh * 512: hc * 1024 + dh * 512 + 512],
                            start=(hc == 0), stop=(hc == 3)),
                            reads=[hw.k(hc * 512 + tt * 128, hc * 512 + tt * 128 + 128), w2T.K],
                            writes=[PMx.K])
                    ck = moeacc.k(i * D + dh * 512, i * D + dh * 512 + 512)
                    r.op("vector", lambda v, i=i, e=e, dh=dh, PMx=PMx: v.scalar_tensor_tensor(
                        moeacc[:, i * D + dh * 512: i * D + dh * 512 + 512],
                        PMx[:, 0:512], disp[:, i * E + e: i * E + e + 1],
                        moeacc[:, i * D + dh * 512: i * D + dh * 512 + 512],
                        ALU.mult, ALU.add),
                        reads=[PMx.K, disp.K, ck], writes=[ck])
    A.release()
    if not (do_shared or do_moe):
        for i in range(nt):
            r.op("gpsimd", lambda g, i=i: g.memset(moeacc[:, i * D:(i + 1) * D], 0.0),
                 writes=[moeacc.k(i * D, (i + 1) * D)])
    elif not do_shared:
        for i in range(nt):
            r.op("gpsimd", lambda g, i=i: g.memset(moeacc[:, i * D:(i + 1) * D], 0.0),
                 writes=[moeacc.k(i * D, (i + 1) * D)])
    for i in range(nt):
        for dh in range(2):
            rk = R.k(i * D + dh * 512, i * D + dh * 512 + 512)
            r.op("vector", lambda v, i=i, dh=dh: v.tensor_tensor(
                R[:, i * D + dh * 512: i * D + dh * 512 + 512],
                R[:, i * D + dh * 512: i * D + dh * 512 + 512],
                moeacc[:, i * D + dh * 512: i * D + dh * 512 + 512], ALU.add),
                reads=[moeacc.K, rk], writes=[rk])
    A.release()


def emit_l0_attn(ctx, R, w, T=2048, D=1024, H=4, K=256, V=256, M=64):
    r = ctx.rec
    nt, nd = T // 128, D // 128
    A = ctx.A
    A.mark()
    tri_f = ctx.sb(128, F32)
    onesf = ctx.sb(128, F32)
    r.op("gpsimd", lambda g: g.memset(tri_f[:, 0:128], 0.0), writes=[tri_f.K])
    r.op("gpsimd", lambda g: g.affine_select(
        out=tri_f[:, 0:128], in_=tri_f[:, 0:128], compare_op=ALU.is_gt,
        fill=1.0, base=0, pattern=[[-1, 128]], channel_multiplier=1),
        reads=[tri_f.K], writes=[tri_f.K])
    r.op("gpsimd", lambda g: g.memset(onesf[:, 0:128], 1.0), writes=[onesf.K])

    wq_sb = ctx.sb(nd * 1024, BF16)
    wk_sb = ctx.sb(nd * 1024, BF16)
    wv_sb = ctx.sb(nd * 1024, BF16)
    wf_sb = ctx.sb(nd * 256, BF16)
    wo_sb = ctx.sb(nd * 1024, BF16)
    for nm, buf, cols in (("Wq", wq_sb, 1024), ("Wk", wk_sb, 1024), ("Wv", wv_sb, 1024),
                          ("Wf", wf_sb, 256), ("Wo", wo_sb, 1024)):
        for c in range(nd):
            r.op("gpsimd", lambda g, nm=nm, c=c, buf=buf, cols=cols: g.dma_start(
                out=buf[:, c * cols:(c + 1) * cols], in_=w[nm][c * 128:(c + 1) * 128, :]),
                dma=True, writes=[buf.k(c * cols, (c + 1) * cols)])
    gn_row = ctx.sb(256, F32)
    G_sb = ctx.sb(256, F32)
    PB = ctx.pbank(4)
    r.op("sync", lambda s: s.dma_start(out=gn_row[0:1, 0:256],
                                       in_=w["g_norm_w"].rearrange("(o v) -> o v", o=1)),
         dma=True, writes=[gn_row.K])
    r.op("tensor", lambda t: t.matmul(PB[:, 0:256], onesf[0:1, 0:128], gn_row[0:1, 0:256],
                                      start=True, stop=True),
         reads=[onesf.K, gn_row.K], writes=[PB.K])
    r.op("vector", lambda v: v.tensor_copy(G_sb[:, 0:256], PB[:, 0:256]),
         reads=[PB.K], writes=[G_sb.K])

    hk_f = [ctx.sb(512, F32) for _ in range(2)]
    hkb = [ctx.sb(512, BF16) for _ in range(2)]
    hv_f = [ctx.sb(1024, F32) for _ in range(2)]
    hvb = [ctx.sb(1024, BF16) for _ in range(2)]
    for b_ in hk_f + hkb + hv_f + hvb:
        r.op("gpsimd", lambda g, b_=b_: g.memset(b_[:, 0:b_.cols], 0.0), writes=[b_.K])

    GT = 256
    ng = T // GT
    ntg = GT // 128
    ufm_g = ctx.sb(nd * GT, BF16)
    qfm = ctx.sb(8 * GT, BF16)
    kfm2 = ctx.sb(8 * GT, BF16)
    ofm_g = ufm_g
    A_s2 = [ctx.sb(256, F32) for _ in range(2)]
    Am2 = [ctx.sb(256, F32) for _ in range(2)]
    stil2 = [ctx.sb(256, BF16) for _ in range(2)]
    acr2 = [ctx.sb(256, F32) for _ in range(2)]
    acb2 = [ctx.sb(256, F32) for _ in range(2)]
    actc2 = [ctx.sb(2, F32) for _ in range(2)]
    ktm2 = [ctx.sb(8 * 128, BF16) for _ in range(2)]
    Sm2 = [ctx.sb(128, BF16) for _ in range(2)]
    wmt2 = [ctx.sb(128, BF16) for _ in range(2)]
    ptT2 = [ctx.sb(128, BF16) for _ in range(2)]
    stT2 = [ctx.sb(128, BF16) for _ in range(2)]
    lg2 = [ctx.sb(64, F32) for _ in range(2)]
    expo2 = [ctx.sb(64, F32) for _ in range(2)]
    ptil2 = [ctx.sb(64, BF16) for _ in range(2)]
    o_sb2 = [ctx.sb(256, F32) for _ in range(2)]
    obf2 = [ctx.sb(256, BF16) for _ in range(2)]
    zs2 = [ctx.sb(4, F32) for _ in range(2)]
    ztmp2 = [ctx.sb(256, F32) for _ in range(2)]
    osq2 = [ctx.sb(256, F32) for _ in range(2)]
    vtm_c2 = [ctx.sb(1024, BF16) for _ in range(2)]
    f8c2 = [ctx.sb(256, F32) for _ in range(2)]
    P0, P1 = ctx.pbank(0), ctx.pbank(1)
    PLOG2 = [ctx.pbank(0), ctx.pbank(1)]
    PSTW2 = [ctx.pbank(2), ctx.pbank(3)]
    POx2 = [ctx.pbank(4), ctx.pbank(5)]
    PC2 = [ctx.pbank(2), ctx.pbank(3)]
    PB2 = [ctx.pbank(4), ctx.pbank(5)]
    PTa, PTb = ctx.pbank(6, BF16), ctx.pbank(7, BF16)
    POP = [ctx.pbank(2), ctx.pbank(3)]

    def do_head(gi, il, cur, nxt, h, vtm_c, f8c, A_s, Am, stil, acr, acb_sb, actc, ktm, PC, PB):
        Sm = Sm2[h % 2]
        wmt = wmt2[h % 2]
        ptT = ptT2[h % 2]
        stT = stT2[h % 2]
        lg = lg2[h % 2]
        expo = expo2[h % 2]
        ptil = ptil2[h % 2]
        o_sb = o_sb2[h % 2]
        obf = obf2[h % 2]
        zs = zs2[h % 2]
        osq = osq2[h % 2]
        PLOG = PLOG2[h % 2]
        PST = PSTW2[h % 2]
        PW = PST
        POx = POx2[h % 2]
        PD = PLOG
        hm = stil[:, h * 64:(h + 1) * 64]
        hmk = stil.k(h * 64, (h + 1) * 64)
        for kc in range(2):
            oc = 2 * h + kc
            r.op("tensor", lambda t, oc=oc, il=il, kc=kc, cur=cur: t.matmul(
                PLOG[:, 0:64], qfm[:, oc * GT + il * 128: oc * GT + (il + 1) * 128],
                hkb[cur][:, oc * 64:(oc + 1) * 64],
                start=(kc == 0), stop=False),
                reads=[qfm.K, hkb[cur].K], writes=[PLOG.K])
        for kc in range(2):
            oc = 2 * h + kc
            r.op("tensor", lambda t, oc=oc, il=il, kc=kc: t.matmul(
                PST[:, 0:128], kfm2[:, oc * GT + il * 128: oc * GT + (il + 1) * 128],
                qfm[:, oc * GT + il * 128: oc * GT + (il + 1) * 128],
                start=(kc == 0), stop=(kc == 1)),
                reads=[kfm2.K, qfm.K], writes=[PST.K])
        r.op("vector", lambda v: v.tensor_tensor(Sm[:, 0:128], PST[:, 0:128], ctx.tri[:, 0:128], ALU.mult),
             reads=[PST.K, ctx.tri.K], writes=[Sm.K])
        r.op("tensor", lambda t, hm=hm: t.matmul(
            PLOG[:, 0:64], Sm[:, 0:128], hm, start=False, stop=True),
            reads=[Sm.K, hmk], writes=[PLOG.K])
        yield
        r.op("vector", lambda v, h=h: v.tensor_tensor(
            lg[:, 0:64], PLOG[:, 0:64], A_s[:, h * 64:(h + 1) * 64], ALU.mult),
            reads=[PLOG.K, A_s.K], writes=[lg.K])
        r.op("vector", lambda v: v.reduce_max(zs[:, 0:1], lg[:, 0:64], axis=AX.X, negate=True),
             reads=[lg.K], writes=[zs.K])
        r.op("vector", lambda v: v.tensor_scalar(zs[:, 1:2], zs[:, 0:1], 0.0625, None, ALU.mult),
             reads=[zs.K], writes=[zs.K])
        r.op("scalar", lambda s: s.activation(expo[:, 0:64], lg[:, 0:64], AF.Exp,
                                              bias=zs[:, 1:2], scale=0.0625, accum_out=zs[:, 2:3]),
             reads=[lg.K, zs.K], writes=[expo.K, zs.K])
        r.op("vector", lambda v: v.reciprocal(zs[:, 2:3], zs[:, 2:3]),
             reads=[zs.K], writes=[zs.K])
        r.op("vector", lambda v: v.tensor_scalar(expo[:, 0:64], expo[:, 0:64], zs[:, 2:3], None, ALU.mult),
             reads=[expo.K, zs.K], writes=[expo.K])
        r.op("vector", lambda v, h=h: v.tensor_tensor(
            ptil[:, 0:64], expo[:, 0:64], A_s[:, h * 64:(h + 1) * 64], ALU.mult),
            reads=[expo.K, A_s.K], writes=[ptil.K])
        yield
        r.op("tensor", lambda t: t.transpose(PTa[0:64, 0:128], ptil[:, 0:64], ctx.ident[:, 0:128]),
             reads=[ptil.K, ctx.ident.K], writes=[PTa.k(0, 128)])
        r.op("vector", lambda v: v.tensor_copy(ptT[0:64, 0:128], PTa[0:64, 0:128]),
             reads=[PTa.k(0, 128)], writes=[ptT.K])
        r.op("tensor", lambda t, hm=hm: t.transpose(PTb[0:64, 0:128], hm, ctx.ident[:, 0:128]),
             reads=[hmk, ctx.ident.K], writes=[PTb.k(0, 128)])
        r.op("vector", lambda v: v.tensor_copy(stT[0:64, 0:128], PTb[0:64, 0:128]),
             reads=[PTb.k(0, 128)], writes=[stT.K])
        yield
        r.op("tensor", lambda t, h=h, cur=cur: t.matmul(
            POx[:, 0:256], ptT[0:64, 0:128], hvb[cur][0:64, h * 256:(h + 1) * 256],
            start=True, stop=False),
            reads=[ptT.K, hvb[cur].K], writes=[POx.K])
        r.op("tensor", lambda t: t.matmul(
            PW[:, 0:128], stT[0:64, 0:128], ptT[0:64, 0:128], start=True, stop=True),
            reads=[stT.K, ptT.K], writes=[PW.K])
        r.op("vector", lambda v: v.tensor_tensor(wmt[:, 0:128], PW[:, 0:128], ctx.tri[:, 0:128], ALU.mult),
             reads=[PW.K, ctx.tri.K], writes=[wmt.K])
        r.op("tensor", lambda t, il=il, h=h: t.matmul(
            POx[:, 0:256], wmt[:, 0:128],
            vtm_c[:, h * 256:(h + 1) * 256],
            start=False, stop=True),
            reads=[wmt.K, vtm_c.K], writes=[POx.K])
        yield
        for kc in range(2):
            oc = 2 * h + kc
            r.op("tensor", lambda t, oc=oc, kc=kc, hm=hm: t.matmul(
                PD[:, 0:64], ktm[:, oc * 128:(oc + 1) * 128], hm, start=True, stop=True),
                reads=[ktm.K, hmk], writes=[PD.K])
            r.op("vector", lambda v, oc=oc, cur=cur, nxt=nxt: v.tensor_tensor(
                hk_f[nxt][:, oc * 64:(oc + 1) * 64], hk_f[cur][:, oc * 64:(oc + 1) * 64],
                PD[:, 0:64], ALU.add),
                reads=[hk_f[cur].K, PD.K], writes=[hk_f[nxt].k(oc * 64, (oc + 1) * 64)])
            r.op("vector", lambda v, oc=oc, h=h, nxt=nxt: v.tensor_tensor(
                hk_f[nxt][:, oc * 64:(oc + 1) * 64], hk_f[nxt][:, oc * 64:(oc + 1) * 64],
                acb_sb[:, h * 64:(h + 1) * 64], ALU.mult),
                reads=[hk_f[nxt].k(oc * 64, (oc + 1) * 64), acb_sb.K],
                writes=[hk_f[nxt].k(oc * 64, (oc + 1) * 64)])
            r.op("vector", lambda v, oc=oc, nxt=nxt: v.tensor_copy(
                hkb[nxt][:, oc * 64:(oc + 1) * 64], hk_f[nxt][:, oc * 64:(oc + 1) * 64]),
                reads=[hk_f[nxt].k(oc * 64, (oc + 1) * 64)], writes=[hkb[nxt].k(oc * 64, (oc + 1) * 64)])
        r.op("tensor", lambda t, il=il, h=h, hm=hm: t.matmul(
            PD[0:64, 64:320], hm, vtm_c[:, h * 256:(h + 1) * 256],
            start=True, stop=True),
            reads=[hmk, vtm_c.K], writes=[PD.K])
        r.op("vector", lambda v, h=h, cur=cur, nxt=nxt: v.tensor_tensor(
            hv_f[nxt][0:64, h * 256:(h + 1) * 256], hv_f[cur][0:64, h * 256:(h + 1) * 256],
            PD[0:64, 64:320], ALU.add),
            reads=[hv_f[cur].K, PD.K], writes=[hv_f[nxt].k(h * 256, (h + 1) * 256)])
        r.op("vector", lambda v, h=h, nxt=nxt: v.tensor_scalar(
            hv_f[nxt][0:64, h * 256:(h + 1) * 256], hv_f[nxt][0:64, h * 256:(h + 1) * 256],
            actc[(h % 2) * 64:(h % 2) * 64 + 64, h // 2: h // 2 + 1], None, ALU.mult),
            reads=[hv_f[nxt].k(h * 256, (h + 1) * 256), actc.K],
            writes=[hv_f[nxt].k(h * 256, (h + 1) * 256)])
        r.op("vector", lambda v, h=h, nxt=nxt: v.tensor_copy(
            hvb[nxt][0:64, h * 256:(h + 1) * 256], hv_f[nxt][0:64, h * 256:(h + 1) * 256]),
            reads=[hv_f[nxt].k(h * 256, (h + 1) * 256)], writes=[hvb[nxt].k(h * 256, (h + 1) * 256)])
        r.op("vector", lambda v: v.tensor_copy(o_sb[:, 0:256], POx[:, 0:256]),
             reads=[POx.K], writes=[o_sb.K])
        yield
        r.op("vector", lambda v: v.tensor_tensor(osq[:, 0:256], o_sb[:, 0:256], o_sb[:, 0:256], ALU.mult),
             reads=[o_sb.K], writes=[osq.K])
        r.op("vector", lambda v: v.reduce_sum(zs[:, 3:4], osq[:, 0:256], axis=AX.X),
             reads=[osq.K], writes=[zs.K])
        r.op("vector", lambda v: v.tensor_scalar(zs[:, 3:4], zs[:, 3:4], 1.0 / 256, 1e-5, ALU.mult, ALU.add),
             reads=[zs.K], writes=[zs.K])
        r.op("scalar", lambda s: s.activation(zs[:, 3:4], zs[:, 3:4], AF.Ln, bias=0.0, scale=1.0),
             reads=[zs.K], writes=[zs.K])
        r.op("scalar", lambda s: s.activation(zs[:, 3:4], zs[:, 3:4], AF.Exp, bias=0.0, scale=-0.5),
             reads=[zs.K], writes=[zs.K])
        r.op("vector", lambda v: v.scalar_tensor_tensor(
            obf[:, 0:256], o_sb[:, 0:256], zs[:, 3:4], G_sb[:, 0:256], ALU.mult, ALU.mult),
            reads=[o_sb.K, zs.K, G_sb.K], writes=[obf.K])
        for vc in range(2):
            oc = 2 * h + vc
            PT = PTa if vc % 2 == 0 else PTb
            r.op("tensor", lambda t, vc=vc, PT=PT: t.transpose(
                PT[:, 0:128], obf[:, vc * 128:(vc + 1) * 128], ctx.ident[:, 0:128]),
                reads=[obf.K, ctx.ident.K], writes=[PT.k(0, 128)])
            r.op("vector", lambda v, oc=oc, il=il, PT=PT: v.tensor_copy(
                ofm_g[:, oc * GT + il * 128: oc * GT + (il + 1) * 128], PT[:, 0:128]),
                reads=[PT.k(0, 128)], writes=[ofm_g.k(oc * GT + il * 128, oc * GT + (il + 1) * 128)])

    def do_chunk(gi, il, cur):
        nxt = 1 - cur
        vtm_c = vtm_c2[il % 2]
        f8c = f8c2[il % 2]
        A_s = A_s2[il % 2]
        Am = Am2[il % 2]
        ef = ztmp2[il % 2]
        stil = stil2[il % 2]
        acr = acr2[il % 2]
        acb_sb = acb2[il % 2]
        actc = actc2[il % 2]
        ktm = ktm2[il % 2]
        ztmp = ztmp2[il % 2]
        PC = PC2[il % 2]
        PB = PB2[il % 2]
        for half in range(2):
            pp = P0 if half % 2 == 0 else P1
            for c in range(nd):
                r.op("tensor", lambda t, il=il, half=half, c=c, pp=pp: t.matmul(
                    pp[:, 0:512], ufm_g[:, c * GT + il * 128: c * GT + (il + 1) * 128],
                    wv_sb[:, c * 1024 + half * 512: c * 1024 + half * 512 + 512],
                    start=(c == 0), stop=(c == nd - 1)),
                    reads=[ufm_g.k(c * GT + il * 128, c * GT + (il + 1) * 128), wv_sb.K], writes=[pp.K])
            r.op("vector", lambda v, half=half, pp=pp: v.tensor_copy(
                vtm_c[:, half * 512: half * 512 + 512], pp[:, 0:512]),
                reads=[pp.K], writes=[vtm_c.k(half * 512, half * 512 + 512)])
        ppf = P0
        for c in range(nd):
            r.op("tensor", lambda t, il=il, c=c, ppf=ppf: t.matmul(
                ppf[:, 0:256], ufm_g[:, c * GT + il * 128: c * GT + (il + 1) * 128],
                wf_sb[:, c * 256:(c + 1) * 256],
                start=(c == 0), stop=(c == nd - 1)),
                reads=[ufm_g.k(c * GT + il * 128, c * GT + (il + 1) * 128), wf_sb.K], writes=[ppf.K])
        r.op("scalar", lambda s, ppf=ppf: s.activation(ztmp[:, 0:256], ppf[:, 0:256], AF.Abs, bias=0.0, scale=1.0),
             reads=[ppf.K], writes=[ztmp.K])
        r.op("scalar", lambda s: s.activation(ztmp[:, 0:256], ztmp[:, 0:256], AF.Exp, bias=0.0, scale=-1.0),
             reads=[ztmp.K], writes=[ztmp.K])
        r.op("scalar", lambda s: s.activation(ztmp[:, 0:256], ztmp[:, 0:256], AF.Ln,
                                              bias=ctx.ones_colf[:, 0:1], scale=1.0),
             reads=[ztmp.K, ctx.ones_colf.K], writes=[ztmp.K])
        r.op("vector", lambda v, ppf=ppf: v.scalar_tensor_tensor(
            f8c[:, 0:256], ppf[:, 0:256], 0.0, ztmp[:, 0:256],
            ALU.min, ALU.subtract),
            reads=[ppf.K, ztmp.K], writes=[f8c.K])
        fs = f8c[:, 0:256]
        fk = f8c.K
        r.op("tensor", lambda t, fs=fs: t.matmul(PC[:, 0:256], tri_f[:, 0:128], fs,
                                                 start=True, stop=True),
             reads=[tri_f.K, fk], writes=[PC.K])
        r.op("tensor", lambda t, fs=fs: t.matmul(PC[0:1, 256:512], onesf[:, 0:1], fs,
                                                 start=True, stop=True),
             reads=[onesf.K, fk], writes=[PC.K])
        r.op("scalar", lambda s: s.activation(A_s[:, 0:256], PC[:, 0:256], AF.Exp, bias=0.0, scale=0.125),
             reads=[PC.K], writes=[A_s.K])
        r.op("scalar", lambda s: s.activation(Am[:, 0:256], PC[:, 0:256], AF.Exp, bias=0.0, scale=-0.125),
             reads=[PC.K], writes=[Am.K])
        r.op("scalar", lambda s: s.activation(acr[0:1, 0:256], PC[0:1, 256:512], AF.Exp, bias=0.0, scale=0.125),
             reads=[PC.K], writes=[acr.K])
        r.op("scalar", lambda s, fs=fs: s.activation(ef[:, 0:256], fs, AF.Exp, bias=0.0, scale=0.125),
             reads=[fk], writes=[ef.K])
        r.op("vector", lambda v: v.tensor_tensor(ef[:, 0:256], ef[:, 0:256], Am[:, 0:256], ALU.mult),
             reads=[ef.K, Am.K], writes=[ef.K])
        r.op("vector", lambda v: v.tensor_tensor(stil[:, 0:256], Am[:, 0:256], ef[:, 0:256], ALU.subtract),
             reads=[Am.K, ef.K], writes=[stil.K])
        r.op("tensor", lambda t: t.matmul(PB[:, 0:256], onesf[0:1, 0:128], acr[0:1, 0:256],
                                          start=True, stop=True),
             reads=[onesf.K, acr.K], writes=[PB.K])
        r.op("vector", lambda v: v.tensor_copy(acb_sb[:, 0:256], PB[:, 0:256]),
             reads=[PB.K], writes=[acb_sb.K])
        for pc in range(2):
            r.op("tensor", lambda t, pc=pc: t.transpose(
                PB[:, 384 + pc: 385 + pc], acr[0:1, pc * 128:(pc + 1) * 128], onesf[0:1, 0:1]),
                reads=[acr.K, onesf.K], writes=[PB.K])
            r.op("vector", lambda v, pc=pc: v.tensor_copy(actc[:, pc:pc + 1], PB[:, 384 + pc: 385 + pc]),
                 reads=[PB.K], writes=[actc.K])
        for oc in range(8):
            PT = PTa if oc % 2 == 0 else PTb
            r.op("tensor", lambda t, oc=oc, il=il, PT=PT: t.transpose(
                PT[:, 0:128], kfm2[:, oc * GT + il * 128: oc * GT + (il + 1) * 128], ctx.ident[:, 0:128]),
                reads=[kfm2.K, ctx.ident.K], writes=[PT.k(0, 128)])
            r.op("vector", lambda v, oc=oc, PT=PT: v.tensor_copy(
                ktm[:, oc * 128:(oc + 1) * 128], PT[:, 0:128]),
                reads=[PT.k(0, 128)], writes=[ktm.k(oc * 128, (oc + 1) * 128)])
        yield
        for pair_ in ((0, 1), (2, 3)):
            gens = [do_head(gi, il, cur, nxt, h, vtm_c, f8c, A_s, Am, stil, acr, acb_sb, actc, ktm, PC, PB)
                    for h in pair_]
            while gens:
                nxt_gens = []
                for g_ in gens:
                    try:
                        next(g_)
                        nxt_gens.append(g_)
                    except StopIteration:
                        pass
                gens = nxt_gens

    cur = 0
    for gi in range(ng):
        tiles = [gi * ntg + k_ for k_ in range(ntg)]
        emit_ln_ufm(ctx, R, ufm_g, GT, D, tiles=tiles, col_base=gi * ntg)
        for nm_, wsb, dst in (("q", wq_sb, qfm), ("k", wk_sb, kfm2)):
            for oc in range(8):
                pp = P0 if oc % 2 == 0 else P1
                for c in range(nd):
                    r.op("tensor", lambda t, oc=oc, c=c, pp=pp, wsb=wsb, gi=gi: t.matmul(
                        pp[:, 0:GT], wsb[:, c * 1024 + oc * 128: c * 1024 + (oc + 1) * 128],
                        ufm_g[:, c * GT: c * GT + GT],
                        start=(c == 0), stop=(c == nd - 1)),
                        reads=[wsb.K, ufm_g.K], writes=[pp.K])
                r.op("scalar", lambda s, oc=oc, pp=pp, dst=dst: s.activation(
                    dst[:, oc * GT:(oc + 1) * GT], pp[:, 0:GT], AF.Silu, bias=0.0, scale=1.0),
                    reads=[pp.K], writes=[dst.k(oc * GT, (oc + 1) * GT)])
        prev_g = None
        for il in range(ntg):
            g_ = do_chunk(gi, il, cur)
            next(g_)
            if prev_g is not None:
                for _ in prev_g:
                    pass
            prev_g = g_
            cur = 1 - cur
        for _ in prev_g:
            pass
        for il in range(ntg):
            i = gi * ntg + il
            for dh in range(2):
                PP = POP[dh]
                for oc in range(8):
                    r.op("tensor", lambda t, il=il, dh=dh, oc=oc, PP=PP: t.matmul(
                        PP[:, 0:512], ofm_g[:, oc * GT + il * 128: oc * GT + (il + 1) * 128],
                        wo_sb[:, oc * 1024 + dh * 512: oc * 1024 + dh * 512 + 512],
                        start=(oc == 0), stop=(oc == 7)),
                        reads=[ofm_g.K, wo_sb.K], writes=[PP.K])
                rk = R.k(i * D + dh * 512, i * D + dh * 512 + 512)
                r.op("vector", lambda v, i=i, dh=dh, PP=PP: v.tensor_tensor(
                    R[:, i * D + dh * 512: i * D + dh * 512 + 512],
                    R[:, i * D + dh * 512: i * D + dh * 512 + 512], PP[:, 0:512], ALU.add),
                    reads=[PP.K, rk], writes=[rk])
    A.release()


from concourse.bass_utils import run_bass_kernel_spmd

_N_CORES = 8

_L0_ATTN_KEYS = ["L0_attn_Wq", "L0_attn_Wk", "L0_attn_Wv", "L0_attn_Wf",
                 "L0_attn_g_norm_w", "L0_attn_Wo"]
_L0_FFN_KEYS = ["L0_ffn_Wq", "L0_ffn_Wk", "L0_ffn_Ws", "L0_ffn_Wd",
                "L0_ffn_Wv", "L0_ffn_Wo"]
_L1_ATTN_KEYS = ["L1_attn_wq", "L1_attn_bq", "L1_attn_wk", "L1_attn_bk",
                 "L1_attn_wv", "L1_attn_bv", "L1_attn_wo", "L1_attn_bo"]
_L1_FFN_KEYS = ["L1_ffn_gate_w", "L1_ffn_w1", "L1_ffn_w2",
                "L1_ffn_sfc1", "L1_ffn_sfc2", "L1_ffn_sfc3"]

_WSHAPES = {
    "L0_attn_Wq": [1024, 1024], "L0_attn_Wk": [1024, 1024], "L0_attn_Wv": [1024, 1024],
    "L0_attn_Wf": [1024, 256], "L0_attn_g_norm_w": [256], "L0_attn_Wo": [1024, 1024],
    "L0_ffn_Wq": [1024, 512], "L0_ffn_Wk": [1024, 512], "L0_ffn_Ws": [1024, 40],
    "L0_ffn_Wd": [1024, 40], "L0_ffn_Wv": [5, 1024, 64], "L0_ffn_Wo": [5, 64, 1024],
    "L1_attn_wq": [16, 1024, 64], "L1_attn_bq": [16, 1, 64], "L1_attn_wk": [16, 1024, 64],
    "L1_attn_bk": [16, 1, 64], "L1_attn_wv": [16, 1024, 64], "L1_attn_bv": [16, 1, 64],
    "L1_attn_bo": [1024],
    "wot": [1024, 1024],
    "gate_wt": [1024, 16], "w1t": [16, 1024, 512], "w2t": [16, 512, 1024],
    "sfc1t": [1024, 2048], "sfc2t": [1024, 2048], "sfc3t": [2048, 1024],
}
_ALL_KEYS = _L0_ATTN_KEYS + _L0_FFN_KEYS + _L1_ATTN_KEYS + _L1_FFN_KEYS


def build_fused(nseq=2, T=2048, D=1024):
    nc = bass.Bass("TRN2", target_bir_lowering=False)
    x = nc.dram_tensor("x", [nseq, T, D], F32, kind="ExternalInput").ap()
    y = nc.dram_tensor("y", [nseq, T, D], F32, kind="ExternalOutput").ap()
    waps = {n: nc.dram_tensor(n, list(sh), F32, kind="ExternalInput").ap()
            for n, sh in _WSHAPES.items()}
    short = lambda keys, pre: {k[len(pre):]: waps[k] for k in keys if k in waps}
    w0a = short(_L0_ATTN_KEYS, "L0_attn_")
    w0f = short(_L0_FFN_KEYS, "L0_ffn_")
    w1a = short(_L1_ATTN_KEYS, "L1_attn_")
    w1a["wot"] = waps["wot"]
    w1f = {"gate_wt": waps["gate_wt"], "w1t": waps["w1t"], "w2t": waps["w2t"],
           "sfc1t": waps["sfc1t"], "sfc2t": waps["sfc2t"], "sfc3t": waps["sfc3t"]}
    es = ExitStack()
    with es:
        rec = Rec(nc)
        ctx = Ctx(nc, rec, es)
        ctx.setup_constants()
        nt = T // 128
        R = ctx.sb(nt * D, F32)
        for s in range(nseq):
            for i in range(nt):
                rec.op("sync", lambda sy, s=s, i=i: sy.dma_start(
                    out=R[:, i * D:(i + 1) * D], in_=x[s, i * 128:(i + 1) * 128, :]),
                    dma=True, writes=[R.k(i * D, (i + 1) * D)])
            emit_l0_attn(ctx, R, w0a)
            emit_l0_ffn(ctx, R, w0f)
            emit_l1_attn(ctx, R, w1a)
            emit_l1_ffn(ctx, R, w1f)
            for i in range(nt):
                rec.op("sync", lambda sy, s=s, i=i: sy.dma_start(
                    out=y[s, i * 128:(i + 1) * 128, :], in_=R[:, i * D:(i + 1) * D]),
                    dma=True, reads=[R.k(i * D, (i + 1) * D)])
        rec.barrier()
        rec.build(es)
        assert rec.check(), "schedulability check failed"
        with nc.Block() as block:
            rec.emit(block)
    return nc, rec


_fused_cache = {}


def kernel(**inputs):
    x = np.ascontiguousarray(inputs["L0_attn_x"], dtype=np.float32)
    B = x.shape[0]
    per = B // _N_CORES
    if per not in _fused_cache:
        _fused_cache[per] = build_fused(nseq=per)
    nc, rec = _fused_cache[per]
    wmap = {k: np.ascontiguousarray(inputs[k], dtype=np.float32) for k in _ALL_KEYS
            if k in _WSHAPES}
    wmap["wot"] = np.ascontiguousarray(np.float32(inputs["L1_attn_wo"]).T)
    wmap["gate_wt"] = np.ascontiguousarray(np.float32(inputs["L1_ffn_gate_w"]).T)
    wmap["w1t"] = np.ascontiguousarray(np.float32(inputs["L1_ffn_w1"]).transpose(0, 2, 1))
    wmap["w2t"] = np.ascontiguousarray(np.float32(inputs["L1_ffn_w2"]).transpose(0, 2, 1))
    wmap["sfc1t"] = np.ascontiguousarray(np.float32(inputs["L1_ffn_sfc1"]).T)
    wmap["sfc2t"] = np.ascontiguousarray(np.float32(inputs["L1_ffn_sfc2"]).T)
    wmap["sfc3t"] = np.ascontiguousarray(np.float32(inputs["L1_ffn_sfc3"]).T)
    in_maps = [{"x": np.ascontiguousarray(x[c * per:(c + 1) * per]), **wmap}
               for c in range(_N_CORES)]
    res = run_bass_kernel_spmd(nc, in_maps, core_ids=list(range(_N_CORES)))
    return np.concatenate([np.asarray(res.results[c]["y"], dtype=np.float32)
                           for c in range(_N_CORES)], axis=0)


# version 14 (final)
# speedup vs baseline: 1.0242x; 1.0167x over previous
import numpy as np
from contextlib import ExitStack
import concourse.bass as bass
import concourse.mybir as mybir

F32 = mybir.dt.float32
BF16 = mybir.dt.bfloat16
AF = mybir.ActivationFunctionType
ALU = mybir.AluOpType
AX = mybir.AxisListType

ENGINES = ("sync", "tensor", "vector", "scalar", "gpsimd")


def _overlap(t1, t2):
    if t1 == "*" or t2 == "*":
        return True
    r1, r2 = isinstance(t1, tuple), isinstance(t2, tuple)
    if r1 and r2:
        return t1[0] < t2[1] and t2[0] < t1[1]
    if r1 or r2:
        return False
    return t1 == t2


def _contains(t1, t2):
    if t1 == "*":
        return True
    if isinstance(t1, tuple) and isinstance(t2, tuple):
        return t1[0] <= t2[0] and t2[1] <= t1[1]
    return t1 == t2


class Op:
    __slots__ = ("idx", "eng", "fn", "deps", "dma", "tag",
                 "marked", "inc_idx", "dma_idx", "waits", "emit_inc")

    def __init__(self, idx, eng, fn, dma, tag):
        self.idx = idx
        self.eng = eng
        self.fn = fn
        self.dma = dma
        self.tag = tag
        self.deps = set()
        self.marked = False
        self.inc_idx = None
        self.dma_idx = None
        self.waits = []


class _NameState:
    __slots__ = ("w", "r")

    def __init__(self):
        self.w = {}
        self.r = {}


class Rec:
    def __init__(self, nc):
        self.nc = nc
        self.ops = []
        self.names = {}

    def _access(self, o, name, tag, iswrite):
        st = self.names.setdefault(name, _NameState())
        if iswrite:
            for t2, wop in st.w.items():
                if _overlap(tag, t2):
                    o.deps.add(wop)
            for t2, rl in st.r.items():
                if _overlap(tag, t2):
                    for rr in rl:
                        o.deps.add(rr)
            for t2 in [t for t in st.w if _contains(tag, t) and t != tag]:
                del st.w[t2]
            for t2 in [t for t in st.r if _contains(tag, t)]:
                del st.r[t2]
            st.w[tag] = o.idx
        else:
            for t2, wop in st.w.items():
                if _overlap(tag, t2):
                    o.deps.add(wop)
            st.r.setdefault(tag, []).append(o.idx)

    def op(self, eng, fn, reads=(), writes=(), dma=False, tag="", deps=()):
        o = Op(len(self.ops), eng, fn, dma, tag)
        for n, t in reads:
            self._access(o, n, t, False)
        for n, t in writes:
            self._access(o, n, t, True)
        for d in deps:
            o.deps.add(d.idx)
        o.deps.discard(o.idx)
        self.ops.append(o)
        return o

    def barrier(self, tag="barrier"):
        last = {}
        for o in self.ops:
            last[o.eng] = o
        outs = []
        for e in ENGINES:
            if e not in last:
                continue
            o = Op(len(self.ops), e, None, False, tag)
            for e2, lo in last.items():
                o.deps.add(lo.idx)
            self.ops.append(o)
            outs.append(o)
        return outs

    def build(self, es):
        nc = self.nc
        esem = {e: es.enter_context(nc.semaphore(f"S{e}")) for e in ENGINES}
        dsem = {e: es.enter_context(nc.semaphore(f"D{e}")) for e in ENGINES}
        for o in self.ops:
            for d in o.deps:
                a = self.ops[d]
                if a.dma:
                    a.marked = True
                elif a.eng == "tensor" and o.eng == "tensor":
                    pass
                else:
                    a.marked = True
        cnt = {e: 0 for e in ENGINES}
        dcnt = {e: 0 for e in ENGINES}
        for o in self.ops:
            if o.dma:
                o.dma_idx = dcnt[o.eng]
                dcnt[o.eng] += 1
            elif o.marked:
                cnt[o.eng] += 1
                o.inc_idx = cnt[o.eng]
        ep_of = {}
        ep_end = {}
        cur_ep = {e: 0 for e in ENGINES}
        cur_cnt = {e: 0 for e in ENGINES}
        open_ep = {e: False for e in ENGINES}
        fence = {}
        for o in self.ops:
            for d in o.deps:
                a = self.ops[d]
                if a.dma and a.idx in ep_of:
                    q2 = a.eng
                    if open_ep[q2] and ep_of[a.idx] == (q2, cur_ep[q2]):
                        open_ep[q2] = False
            if o.dma:
                q = o.eng
                if not open_ep[q]:
                    cur_ep[q] += 1
                    open_ep[q] = True
                    if cur_cnt[q] > 0:
                        fence[o.idx] = 16 * cur_cnt[q]
                ep_of[o.idx] = (q, cur_ep[q])
                cur_cnt[q] += 1
                ep_end[(q, cur_ep[q])] = cur_cnt[q]
        wm_after = [None] * len(self.ops)
        last_on_engine = {}
        val2op = {}
        for o in self.ops:
            if o.inc_idx is not None:
                val2op[(o.eng, o.inc_idx)] = o.idx
        for o in self.ops:
            wm = dict(last_on_engine.get(o.eng) or {})
            waits = {}
            if o.idx in fence:
                k = ("D", o.eng)
                v = fence[o.idx]
                if wm.get(k, 0) < v:
                    waits[k] = v
            for d in sorted(o.deps):
                a = self.ops[d]
                if a.dma:
                    k = ("D", a.eng)
                    v = 16 * ep_end[ep_of[a.idx]]
                elif a.eng == "tensor" and o.eng == "tensor":
                    continue
                else:
                    k = ("S", a.eng)
                    v = a.inc_idx
                if wm.get(k, 0) < v:
                    waits[k] = max(waits.get(k, 0), v)
            for d in sorted(o.deps):
                aw = wm_after[self.ops[d].idx]
                if aw:
                    for k2, v2 in aw.items():
                        if v2 > wm.get(k2, 0):
                            wm[k2] = v2
            for k, v in waits.items():
                if v > wm.get(k, 0):
                    wm[k] = v
            o.waits = [(k, v) for k, v in waits.items()]
            wm_after[o.idx] = wm
            last_on_engine[o.eng] = wm
        referenced = set()
        for o in self.ops:
            for (kind, eng), v in o.waits:
                if kind == "S":
                    referenced.add(val2op[(eng, v)])
        cnt2 = {e: 0 for e in ENGINES}
        rank = {}
        for o in self.ops:
            if (o.inc_idx is not None) and (o.idx in referenced):
                cnt2[o.eng] += 1
                rank[o.idx] = cnt2[o.eng]
                o.emit_inc = True
            else:
                o.emit_inc = bool(o.dma)
        for o in self.ops:
            nw = []
            for (kind, eng), v in o.waits:
                if kind == "S":
                    nw.append(((kind, eng), rank[val2op[(eng, v)]]))
                else:
                    nw.append(((kind, eng), v))
            o.waits = nw
        self._esem = esem
        self._dsem = dsem
        self.final_counts = dict(cnt2)
        self.final_dma = {e: 16 * dcnt[e] for e in ENGINES}

    def check(self):
        streams = {e: [o for o in self.ops if o.eng == e] for e in ENGINES}
        pos = {e: 0 for e in ENGINES}
        sv = {}
        progress = True
        while progress:
            progress = False
            for e in ENGINES:
                while pos[e] < len(streams[e]):
                    o = streams[e][pos[e]]
                    if all(sv.get(k, 0) >= v for k, v in o.waits):
                        if o.dma:
                            sv[("D", e)] = sv.get(("D", e), 0) + 16
                        elif getattr(o, "emit_inc", False):
                            sv[("S", e)] = sv.get(("S", e), 0) + 1
                        pos[e] += 1
                        progress = True
                    else:
                        break
        stuck = {e: pos[e] < len(streams[e]) for e in ENGINES}
        if any(stuck.values()):
            print("DEADLOCK at build-check:")
            for e in ENGINES:
                if stuck[e]:
                    o = streams[e][pos[e]]
                    bad = [(k, v, sv.get(k, 0)) for k, v in o.waits if sv.get(k, 0) < v]
                    print(f"  {e}[{pos[e]}] op#{o.idx} tag={o.tag!r} dma={o.dma} waits: {bad}")
                    print(f"    deps: {sorted(o.deps)[-8:]}")
            return False
        return True

    def emit(self, block):
        esem, dsem = self._esem, self._dsem
        streams = {e: [o for o in self.ops if o.eng == e] for e in ENGINES}

        def mk(e):
            def body(engobj):
                for o in streams[e]:
                    for (kind, eng2), v in o.waits:
                        engobj.wait_ge(dsem[eng2] if kind == "D" else esem[eng2], v)
                    if o.fn is None:
                        continue
                    inst = o.fn(engobj)
                    if o.dma:
                        inst.then_inc(dsem[e], 16)
                    elif getattr(o, "emit_inc", False):
                        inst.then_inc(esem[e], 1)
            return body

        for e in ENGINES:
            if streams[e]:
                getattr(block, e)(mk(e))

    def stats(self):
        from collections import Counter
        c = Counter(o.eng for o in self.ops)
        m = Counter(o.eng for o in self.ops if o.marked)
        nw = sum(len(o.waits) for o in self.ops)
        return dict(total=len(self.ops), per_engine=dict(c), marked=dict(m), waits=nw)




class SB:
    __slots__ = ("ar", "off", "cols", "dt", "esz")

    def __init__(self, ar, off, cols, dt):
        self.ar = ar
        self.off = off
        self.cols = cols
        self.dt = dt
        self.esz = 2 if dt == BF16 else 4

    def __getitem__(self, idx):
        ps, cs = idx
        h = self.ar.typed[self.dt]
        base = self.off // self.esz
        a, b, stp = cs.indices(self.cols)
        assert stp == 1
        return h[ps, base + a: base + b]

    def ap3(self, ps, pattern, **kw):
        return self[ps, 0:self.cols].rearrange(pattern, **kw)

    def k(self, a=0, b=None):
        if b is None:
            b = self.cols
        lo = self.off + a * self.esz
        hi = self.off + b * self.esz
        if self.ar.is_psum:
            lo = (lo // 2048) * 2048
            hi = ((hi + 2047) // 2048) * 2048
        return (self.ar.name, (lo, hi))

    @property
    def K(self):
        return self.k()


class Arena:
    def __init__(self, nc, es, name, byts, space="sbuf", base_partition=0):
        self.name = name
        self.nbytes = byts
        self.is_psum = (space == "psum")
        if space == "sbuf":
            t = es.enter_context(nc.sbuf_tensor(name, [128, byts // 4], F32))
        else:
            t = es.enter_context(nc.psum_tensor(name, [128, byts // 4], F32))
        self.typed = {F32: t, BF16: t.bitcast(BF16)}
        self.off = 0
        self.marks = []

    def alloc(self, cols, dt=F32):
        esz = 2 if dt == BF16 else 4
        off = (self.off + 3) & ~3
        nb = cols * esz
        assert off + nb <= self.nbytes, f"{self.name} overflow: {off}+{nb} > {self.nbytes}"
        self.off = off + nb
        return SB(self, off, cols, dt)

    def at(self, off, cols, dt=F32):
        return SB(self, off, cols, dt)

    def mark(self):
        self.marks.append(self.off)

    def release(self):
        self.off = self.marks.pop()


class Ctx:
    def __init__(self, nc, rec, es, sbuf_bytes=212824):
        self.nc = nc
        self.rec = rec
        self.es = es
        self.A = Arena(nc, es, "AR", sbuf_bytes, "sbuf")
        self.P = Arena(nc, es, "PS", 16 * 1024, "psum")
        self._uid = 0

    def sb(self, cols, dt=F32):
        return self.A.alloc(cols, dt)

    def pbank(self, i, dt=F32, banks=1):
        return self.P.at(i * 2048, banks * 2048 // (2 if dt == BF16 else 4), dt)

    def setup_constants(self):
        r = self.rec
        self.ident = self.sb(128, BF16)
        self.tri = self.sb(128, BF16)
        self.ones_col = self.sb(1, BF16)
        self.ones_colf = self.sb(1, F32)
        self.eps_col = self.sb(1, F32)
        self.ones_row = self.sb(128, BF16)

        def aps(sbv, dt):
            return sbv[:, 0:128]

        r.op("gpsimd", lambda g: g.memset(self.ident[:, 0:128], 0.0), writes=[self.ident.K])
        r.op("gpsimd", lambda g: g.affine_select(
            out=self.ident[:, 0:128], in_=self.ident[:, 0:128], compare_op=ALU.not_equal,
            fill=1.0, base=0, pattern=[[-1, 128]], channel_multiplier=1),
            reads=[self.ident.K], writes=[self.ident.K])
        r.op("gpsimd", lambda g: g.memset(self.tri[:, 0:128], 0.0), writes=[self.tri.K])
        r.op("gpsimd", lambda g: g.affine_select(
            out=self.tri[:, 0:128], in_=self.tri[:, 0:128], compare_op=ALU.is_gt,
            fill=1.0, base=0, pattern=[[-1, 128]], channel_multiplier=1),
            reads=[self.tri.K], writes=[self.tri.K])
        r.op("gpsimd", lambda g: g.memset(self.ones_col[:, 0:1], 1.0), writes=[self.ones_col.K])
        r.op("gpsimd", lambda g: g.memset(self.ones_colf[:, 0:1], 1.0), writes=[self.ones_colf.K])
        r.op("gpsimd", lambda g: g.memset(self.eps_col[:, 0:1], 1e-5), writes=[self.eps_col.K])
        r.op("gpsimd", lambda g: g.memset(self.ones_row[:, 0:128], 1.0), writes=[self.ones_row.K])


def emit_ln_ufm(ctx, R, ufm, T=2048, D=1024, tiles=None, col_base=0, tile_cb=None):
    r = ctx.rec
    nt = T // 128
    nd = D // 128
    A = ctx.A
    A.mark()
    stats = ctx.sb(8 * nt, F32)
    sq = [ctx.sb(D, F32), ctx.sb(D, F32)]
    utm = [ctx.sb(D, BF16), ctx.sb(D, BF16)]
    pp = [ctx.pbank(6, BF16), ctx.pbank(7, BF16)]
    for i in (tiles if tiles is not None else range(nt)):
        Rk = R.k(i * D, (i + 1) * D)
        Rs = R[:, i * D:(i + 1) * D]
        il = i - col_base
        c0 = 8 * il
        sqt = sq[il % 2]
        utt = utm[il % 2]
        s_sum = stats[:, c0:c0 + 1]
        s_mean = stats[:, c0 + 2:c0 + 3]
        s_var = stats[:, c0 + 3:c0 + 4]
        s_rstd = stats[:, c0 + 5:c0 + 6]
        sk = stats.k(c0, c0 + 8)
        r.op("scalar", lambda s, Rs=Rs, sqt=sqt, o=s_sum: s.activation(
            sqt[:, 0:D], Rs, AF.Copy, bias=0.0, scale=1.0, accum_out=o),
            reads=[Rk], writes=[sqt.K, sk])
        r.op("scalar", lambda s, Rs=Rs, sqt=sqt, o=stats[:, c0 + 1:c0 + 2]: s.activation(
            sqt[:, 0:D], Rs, AF.Square, bias=0.0, scale=1.0, accum_out=o),
            reads=[Rk], writes=[sqt.K, sk])
        r.op("vector", lambda v, i_=s_sum, o=s_mean: v.tensor_scalar(o, i_, 1.0 / D, None, ALU.mult),
             reads=[sk], writes=[sk])
        r.op("vector", lambda v, m=s_mean, o=s_var: v.tensor_tensor(o, m, m, ALU.mult),
             reads=[sk], writes=[sk])
        r.op("vector", lambda v, a=stats[:, c0 + 1:c0 + 2], b=s_var: v.tensor_scalar(
            b, a, 1.0 / D, b, ALU.mult, ALU.subtract),
            reads=[sk], writes=[sk])
        r.op("scalar", lambda s, a=s_var, o=s_rstd: s.activation(
            o, a, AF.Ln, bias=ctx.eps_col[:, 0:1], scale=1.0),
            reads=[sk, ctx.eps_col.K], writes=[sk])
        r.op("scalar", lambda s, o=s_rstd: s.activation(o, o, AF.Exp, bias=0.0, scale=-0.5),
             reads=[sk], writes=[sk])
        r.op("vector", lambda v, Rs=Rs, m=s_mean, rs=s_rstd, utt=utt: v.tensor_scalar(
            utt[:, 0:D], Rs, m, rs, ALU.subtract, ALU.mult),
            reads=[Rk, sk], writes=[utt.K])
        for c in range(nd):
            p_ = pp[c % 2]
            r.op("tensor", lambda t, c=c, p_=p_, utt=utt: t.transpose(
                p_[:, 0:128], utt[:, c * 128:(c + 1) * 128], ctx.ident[:, 0:128]),
                reads=[utt.K, ctx.ident.K], writes=[p_.K])
            if c % 2 == 0:
                r.op("vector", lambda v, c=c, il=il, p_=p_: v.tensor_copy(
                    ufm[:, c * T + il * 128: c * T + (il + 1) * 128], p_[:, 0:128]),
                    reads=[p_.K], writes=[ufm.k(c * T + il * 128, c * T + (il + 1) * 128)])
            else:
                r.op("scalar", lambda s, c=c, il=il, p_=p_: s.copy(
                    ufm[:, c * T + il * 128: c * T + (il + 1) * 128], p_[:, 0:128]),
                    reads=[p_.K], writes=[ufm.k(c * T + il * 128, c * T + (il + 1) * 128)])
        if tile_cb is not None:
            tile_cb(i)
    A.release()


def load_bias_cols(ctx, b_dram, n_heads, cols_out):
    r = ctx.rec
    A = ctx.A
    A.mark()
    row = ctx.sb(n_heads * 64, F32)
    pp = ctx.pbank(6)
    r.op("sync", lambda s: s.dma_start(out=row[0:1, :], in_=b_dram.rearrange("h o e -> o (h e)")),
         dma=True, writes=[row.K])
    for p in range(n_heads // 2):
        r.op("tensor", lambda t, p=p: t.transpose(
            pp[:, 0:1], row[0:1, p * 128:(p + 1) * 128], ctx.ones_colf[0:1, 0:1]),
            reads=[row.K, ctx.ones_colf.K], writes=[pp.k(0, 1)])
        r.op("vector", lambda v, p=p: v.tensor_copy(cols_out[:, p:p + 1], pp[:, 0:1]),
             reads=[pp.k(0, 1)], writes=[cols_out.k(p, p + 1)])
    A.release()


def emit_l1_attn(ctx, R, w, T=2048, D=1024, H=16, DH=64):
    r = ctx.rec
    nt, nd, npair = T // 128, D // 128, H // 2
    A = ctx.A
    A.mark()

    ufm = ctx.sb(nd * T, BF16)
    emit_ln_ufm(ctx, R, ufm, T, D)

    kfm = [ctx.sb(T, BF16) for _ in range(npair)]
    vtm = ctx.sb(nt * 1040, BF16)
    for i_ in range(nt):
        r.op("gpsimd", lambda g, i_=i_: g.memset(
            vtm[:, i_ * 1040: (i_ + 1) * 1040].rearrange("p (h x) -> p h x", x=65)[:, :, 64:65], 1.0),
            writes=[vtm.k(i_ * 1040, (i_ + 1) * 1040)])
    warn = [ctx.sb(nd * 128, BF16) for _ in range(npair)]
    bq_cols = ctx.sb(npair, F32)
    bk_cols = ctx.sb(npair, F32)
    bo_row = ctx.sb(D, BF16)
    woT = ctx.sb(nd * 1024, BF16)
    A.mark()
    bv_row = ctx.sb(D, BF16)

    def load_w(nm, p, hh):
        h = 2 * p + hh
        src = w[nm][h].rearrange("(c p) e -> p c e", p=128)
        dst = warn[p].ap3(slice(None), "p (c x) -> p c x", x=128)[:, :, hh * 64:(hh + 1) * 64]
        r.op("gpsimd", lambda g: g.dma_start(out=dst, in_=src), dma=True,
             writes=[warn[p].K])

    for p in range(npair):
        load_w("wk", p, 0)
        load_w("wk", p, 1)
    wv_sb = woT
    for h in range(H):
        src = w["wv"][h].rearrange("(c p) e -> p c e", p=128)
        dst = wv_sb.ap3(slice(None), "p (c h x) -> p c h x", h=H, x=64)[:, :, h, :]
        r.op("gpsimd", lambda g, s_=src, d_=dst: g.dma_start(out=d_, in_=s_), dma=True,
             writes=[wv_sb.K])
    load_bias_cols(ctx, w["bq"], H, bq_cols)
    load_bias_cols(ctx, w["bk"], H, bk_cols)
    r.op("gpsimd", lambda g: g.dma_start(out=bv_row[0:1, :], in_=w["bv"].rearrange("h o e -> o (h e)")),
         dma=True, writes=[bv_row.K])
    r.op("gpsimd", lambda g: g.dma_start(out=bo_row[0:1, :], in_=w["bo"].rearrange("(o d) -> o d", o=1)),
         dma=True, writes=[bo_row.K])

    pk = [ctx.pbank(0), ctx.pbank(1)]
    for p in range(npair):
        for icol in range(T // 512):
            pp = pk[icol % 2]
            for c in range(nd):
                r.op("tensor", lambda t, p=p, icol=icol, c=c, pp=pp: t.matmul(
                    pp[:, 0:512], warn[p][:, c * 128:(c + 1) * 128],
                    ufm[:, c * T + icol * 512: c * T + icol * 512 + 512],
                    start=(c == 0), stop=(c == nd - 1)),
                    reads=[warn[p].K, ufm.K], writes=[pp.K])
            r.op("vector", lambda v, p=p, icol=icol, pp=pp: v.tensor_scalar(
                kfm[p][:, icol * 512:(icol + 1) * 512], pp[:, 0:512],
                bk_cols[:, p:p + 1], None, ALU.add),
                reads=[pp.K, bk_cols.k(p, p + 1)], writes=[kfm[p].k(icol * 512, (icol + 1) * 512)])
    for i in range(nt):
        for half in range(2):
            pp = pk[half % 2]
            r.op("tensor", lambda t, half=half, pp=pp: t.matmul(
                pp[:, 0:512], ctx.ones_row[0:1, 0:128], bv_row[0:1, half * 512:(half + 1) * 512],
                start=True, stop=False),
                reads=[ctx.ones_row.K, bv_row.K], writes=[pp.K])
            for c in range(nd):
                r.op("tensor", lambda t, i=i, half=half, c=c, pp=pp: t.matmul(
                    pp[:, 0:512], ufm[:, c * T + i * 128: c * T + (i + 1) * 128],
                    wv_sb[:, c * 1024 + half * 512: c * 1024 + half * 512 + 512],
                    start=False, stop=(c == nd - 1)),
                    reads=[ufm.K, wv_sb.K], writes=[pp.K])
            dsts = vtm[:, i * 1040 + half * 520: i * 1040 + half * 520 + 520].rearrange(
                "p (h x) -> p h x", x=65)[:, :, 0:64]
            r.op("vector", lambda v, i=i, half=half, pp=pp, dsts=dsts: v.tensor_copy(
                dsts, pp[:, 0:512].rearrange("p (h x) -> p h x", x=64)),
                reads=[pp.K], writes=[vtm.k(i * 1040 + half * 520, i * 1040 + half * 520 + 520)])
    A.release()
    for p in range(npair):
        load_w("wq", p, 0)
        load_w("wq", p, 1)
    r.op("gpsimd", lambda g: g.dma_start(
        out=woT.ap3(slice(None), "p (r d) -> p r d", d=1024),
        in_=w["wot"].rearrange("(r p) d -> p r d", p=128)),
        dma=True, writes=[woT.K])

    NIQ = 512
    qsb = ctx.sb(NIQ, BF16)
    _pA, _pB, _pC = ctx.sb(NIQ, BF16), ctx.sb(NIQ, BF16), ctx.sb(NIQ, BF16)
    ptb = [[_pA, _pB], [_pC, _pC]]
    att = ctx.sb(npair * NIQ, BF16)
    zbb = ctx.A.at(qsb.off, NIQ, BF16)
    zrow = ctx.A.at(_pC.off, NIQ, BF16)
    PS = [[ctx.pbank(0), ctx.pbank(1)], [ctx.pbank(2), ctx.pbank(3)]]
    PO = [ctx.pbank(4), ctx.pbank(5)]
    PQ = [ctx.pbank(0), ctx.pbank(1)]
    PAux = ctx.pbank(6)
    POut = [ctx.pbank(6), ctx.pbank(7)]

    def st_mm(p, hh, j):
        ps = PS[hh][j % 2]
        r.op("tensor", lambda t, p=p, hh=hh, j=j, ps=ps: t.matmul(
            ps[:, 0:NIQ], kfm[p][hh * 64:(hh + 1) * 64, j * 128:(j + 1) * 128],
            qsb[hh * 64:(hh + 1) * 64, 0:NIQ],
            start=True, stop=True, tile_position=(hh * 64, 0)),
            reads=[kfm[p].K, qsb.K], writes=[ps.K])

    for iq in range(T // NIQ):
        for p in range(npair):
            pq = PQ[0]
            for c in range(nd):
                r.op("tensor", lambda t, p=p, c=c, pq=pq, iq=iq: t.matmul(
                    pq[:, 0:NIQ], warn[p][:, c * 128:(c + 1) * 128],
                    ufm[:, c * T + iq * NIQ: c * T + iq * NIQ + NIQ],
                    start=(c == 0), stop=(c == nd - 1)),
                    reads=[warn[p].K, ufm.K], writes=[pq.K])
            r.op("vector", lambda v, p=p, pq=pq: v.tensor_scalar(
                qsb[:, 0:NIQ], pq[:, 0:NIQ], bq_cols[:, p:p + 1], None, ALU.add),
                reads=[pq.K, bq_cols.k(p, p + 1)], writes=[qsb.K])
            st_mm(p, 0, 0)
            st_mm(p, 1, 0)
            for j in range(nt):
                for hh in range(2):
                    ps = PS[hh][j % 2]
                    pt = ptb[hh][j % 2]
                    r.op("scalar", lambda s, ps=ps, pt=pt: s.activation(
                        pt[:, 0:NIQ], ps[:, 0:NIQ], AF.Exp, bias=0.0, scale=0.125),
                        reads=[ps.K], writes=[pt.K])
                if j + 1 < nt:
                    st_mm(p, 0, j + 1)
                    st_mm(p, 1, j + 1)
                for hh in range(2):
                    h = 2 * p + hh
                    po = PO[hh]
                    pt = ptb[hh][j % 2]
                    r.op("tensor", lambda t, h=h, j=j, po=po, pt=pt: t.matmul(
                        po[0:65, 0:NIQ], vtm[:, j * 1040 + h * 65: j * 1040 + (h + 1) * 65],
                        pt[:, 0:NIQ], start=(j == 0), stop=(j == nt - 1)),
                        reads=[vtm.K, pt.K], writes=[po.K])
            for hh in range(2):
                po = PO[hh]
                def _zrow(s, po=po):
                    with ctx.nc.allow_low_precision(reason="Z row staged bf16"):
                        return s.copy(zrow[0:1, 0:NIQ], po[64:65, 0:NIQ])
                r.op("scalar", _zrow, reads=[po.K], writes=[zrow.K])
                r.op("tensor", lambda t: t.matmul(PAux[0:64, 0:NIQ], ctx.ones_row[0:1, 0:64],
                                                  zrow[0:1, 0:NIQ], start=True, stop=True),
                     reads=[ctx.ones_row.K, zrow.K], writes=[PAux.K])
                def _recip(v):
                    with ctx.nc.allow_low_precision(reason="1/Z stored bf16"):
                        return v.reciprocal(zbb[0:64, 0:NIQ], PAux[0:64, 0:NIQ])
                r.op("vector", _recip, reads=[PAux.K], writes=[zbb.K])
                r.op("vector", lambda v, p=p, hh=hh, po=po: v.tensor_tensor(
                    att[hh * 64:(hh + 1) * 64, p * NIQ:(p + 1) * NIQ], po[0:64, 0:NIQ],
                    zbb[0:64, 0:NIQ], ALU.mult),
                    reads=[po.K, zbb.K],
                    writes=[att.k(p * NIQ, (p + 1) * NIQ)])
        for tt in range(NIQ // 128):
            i = iq * (NIQ // 128) + tt
            for dh in range(2):
                PP = POut[dh]
                r.op("tensor", lambda t, dh=dh, PP=PP: t.matmul(
                    PP[:, 0:512], ctx.ones_row[0:1, 0:128], bo_row[0:1, dh * 512:(dh + 1) * 512],
                    start=True, stop=False),
                    reads=[ctx.ones_row.K, bo_row.K], writes=[PP.K])
                for p in range(npair):
                    r.op("tensor", lambda t, tt=tt, dh=dh, p=p, PP=PP: t.matmul(
                        PP[:, 0:512], att[:, p * NIQ + tt * 128: p * NIQ + (tt + 1) * 128],
                        woT[:, p * 1024 + dh * 512: p * 1024 + dh * 512 + 512],
                        start=False, stop=(p == npair - 1)),
                        reads=[att.K, woT.K], writes=[PP.K])
                r.op("vector", lambda v, i=i, dh=dh, PP=PP: v.tensor_tensor(
                    R[:, i * 1024 + dh * 512: i * 1024 + dh * 512 + 512],
                    R[:, i * 1024 + dh * 512: i * 1024 + dh * 512 + 512], PP[:, 0:512], ALU.add),
                    reads=[PP.K, R.k(i * 1024 + dh * 512, i * 1024 + dh * 512 + 512)],
                    writes=[R.k(i * 1024 + dh * 512, i * 1024 + dh * 512 + 512)])
    A.release()


def build_standalone(emit_fn, wspecs, nseq=2, T=2048, D=1024):
    nc = bass.Bass("TRN2", target_bir_lowering=False)
    x = nc.dram_tensor("x", [nseq, T, D], F32, kind="ExternalInput").ap()
    y = nc.dram_tensor("y", [nseq, T, D], F32, kind="ExternalOutput").ap()
    waps = {n: nc.dram_tensor(n, list(sh), F32, kind="ExternalInput").ap()
            for n, sh in wspecs.items()}
    es = ExitStack()
    with es:
        rec = Rec(nc)
        ctx = Ctx(nc, rec, es)
        ctx.setup_constants()
        nt = T // 128
        R = ctx.sb(nt * D, F32)
        for s in range(nseq):
            for i in range(nt):
                rec.op("sync", lambda sy, s=s, i=i: sy.dma_start(
                    out=R[:, i * D:(i + 1) * D], in_=x[s, i * 128:(i + 1) * 128, :]),
                    dma=True, writes=[R.k(i * D, (i + 1) * D)])
            emit_fn(ctx, R, waps)
            for i in range(nt):
                rec.op("sync", lambda sy, s=s, i=i: sy.dma_start(
                    out=y[s, i * 128:(i + 1) * 128, :], in_=R[:, i * D:(i + 1) * D]),
                    dma=True, reads=[R.k(i * D, (i + 1) * D)])
            rec.barrier()
        rec.build(es)
        assert rec.check(), "schedulability check failed"
        with nc.Block() as block:
            rec.emit(block)
    return nc, rec


def emit_l0_ffn(ctx, R, w, T=2048, D=1024, H=8, DH=64, E=5):
    r = ctx.rec
    nt, nd, npair = T // 128, D // 128, H // 2
    A = ctx.A
    A.mark()
    ufm = ctx.sb(nd * T, BF16)

    wq_sb = [ctx.sb(nd * 128, BF16) for _ in range(npair)]
    wk_sb = [ctx.sb(nd * 128, BF16) for _ in range(npair)]
    wsd = ctx.sb(nd * 80, BF16)
    wv_sb = ctx.sb(nd * E * 64, BF16)
    wo_sb = ctx.sb(E * 1024, BF16)
    for p in range(npair):
        for nm, buf in (("Wq", wq_sb), ("Wk", wk_sb)):
            src = w[nm][:, p * 128:(p + 1) * 128].rearrange("(c p) e -> p c e", p=128)
            dst = buf[p].ap3(slice(None), "p (c x) -> p c x", x=128)
            r.op("gpsimd", lambda g, s_=src, d_=dst: g.dma_start(out=d_, in_=s_), dma=True,
                 writes=[buf[p].K])
    for nm, off in (("Ws", 0), ("Wd", 40)):
        src = w[nm].rearrange("(c p) e -> p c e", p=128)
        dst = wsd.ap3(slice(None), "p (c x) -> p c x", x=80)[:, :, off:off + 40]
        r.op("gpsimd", lambda g, s_=src, d_=dst: g.dma_start(out=d_, in_=s_), dma=True,
             writes=[wsd.K])
    for e in range(E):
        src = w["Wv"][e].rearrange("(c p) x -> p c x", p=128)
        dst = wv_sb.ap3(slice(None), "p (c e x) -> p c e x", e=E, x=64)[:, :, e, :]
        r.op("gpsimd", lambda g, s_=src, d_=dst: g.dma_start(out=d_, in_=s_), dma=True,
             writes=[wv_sb.K])
    for e in range(E):
        r.op("gpsimd", lambda g, e=e: g.dma_start(out=wo_sb[0:64, e * 1024:(e + 1) * 1024],
                                                  in_=w["Wo"][e]), dma=True, writes=[wo_sb.K])

    kfm = [ctx.sb(T, BF16) for _ in range(npair)]
    vtm = ctx.sb(nt * H * 65, BF16)
    for i_ in range(nt):
        r.op("gpsimd", lambda g, i_=i_: g.memset(
            vtm[:, i_ * 520: (i_ + 1) * 520].rearrange("p (h x) -> p h x", x=65)[:, :, 64:65], 1.0),
            writes=[vtm.k(i_ * 520, (i_ + 1) * 520)])
    maskd = ctx.sb(nt * H * E, F32)
    P0 = ctx.pbank(0)
    P1 = ctx.pbank(1)

    A.mark()
    gl = ctx.sb(80, F32)
    m1h = ctx.sb(H, F32)
    m2h = ctx.sb(H, F32)
    glm = ctx.sb(40, F32)
    sgm = ctx.sb(40, F32)
    maskv = ctx.sb(40, F32)
    xvt = ctx.sb(E * 64, BF16)
    vmt = ctx.sb(512, BF16)
    PG = ctx.pbank(2)
    PX = ctx.pbank(3)
    def gate_vmix_tile(i):
        for c in range(nd):
            r.op("tensor", lambda t, i=i, c=c: t.matmul(
                PG[:, 0:80], ufm[:, c * T + i * 128: c * T + (i + 1) * 128],
                wsd[:, c * 80:(c + 1) * 80], start=(c == 0), stop=(c == nd - 1)),
                reads=[ufm.K, wsd.K], writes=[PG.K])
        r.op("vector", lambda v: v.tensor_copy(gl[:, 0:80], PG[:, 0:80]),
             reads=[PG.K], writes=[gl.K])
        for c in range(nd):
            r.op("tensor", lambda t, i=i, c=c: t.matmul(
                PX[:, 0:E * 64], ufm[:, c * T + i * 128: c * T + (i + 1) * 128],
                wv_sb[:, c * E * 64:(c + 1) * E * 64], start=(c == 0), stop=(c == nd - 1)),
                reads=[ufm.K, wv_sb.K], writes=[PX.K])
        r.op("vector", lambda v: v.tensor_copy(xvt[:, 0:E * 64], PX[:, 0:E * 64]),
             reads=[PX.K], writes=[xvt.K])
        for goff, is_v in ((0, True), (40, False)):
            g3 = gl[:, goff:goff + 40].rearrange("p (h e) -> p h e", e=E)
            glk = gl.k(goff, goff + 40)
            r.op("vector", lambda v, g3=g3: v.reduce_max(m1h[:, 0:H], g3, axis=AX.X),
                 reads=[glk], writes=[m1h.K])
            m1b = m1h[:, 0:H].rearrange("p h -> p h ()").broadcast_to([128, H, E])
            r.op("vector", lambda v, g3=g3, m1b=m1b: v.tensor_tensor(
                glm[:, 0:40].rearrange("p (h e) -> p h e", e=E), g3, m1b, ALU.is_ge),
                reads=[glk, m1h.K], writes=[glm.K])
            r.op("vector", lambda v, goff=goff: v.scalar_tensor_tensor(
                glm[:, 0:40], glm[:, 0:40], -1e30, gl[:, goff:goff + 40], ALU.mult, ALU.add),
                reads=[glm.K, glk], writes=[glm.K])
            r.op("vector", lambda v: v.reduce_max(m2h[:, 0:H], glm[:, 0:40].rearrange("p (h e) -> p h e", e=E), axis=AX.X),
                 reads=[glm.K], writes=[m2h.K])
            m2b = m2h[:, 0:H].rearrange("p h -> p h ()").broadcast_to([128, H, E])
            if is_v:
                r.op("vector", lambda v, g3=g3, m2b=m2b: v.tensor_tensor(
                    maskv[:, 0:40].rearrange("p (h e) -> p h e", e=E), g3, m2b, ALU.is_ge),
                    reads=[glk, m2h.K], writes=[maskv.K])
            else:
                dk = maskd.k(i * 40, (i + 1) * 40)
                r.op("vector", lambda v, g3=g3, m2b=m2b, i=i: v.tensor_tensor(
                    maskd[:, i * 40:(i + 1) * 40].rearrange("p (h e) -> p h e", e=E), g3, m2b, ALU.is_ge),
                    reads=[glk, m2h.K], writes=[dk])
        r.op("scalar", lambda s: s.activation(sgm[:, 0:40], gl[:, 0:40], AF.Exp, bias=0.0, scale=-1.0),
             reads=[gl.K], writes=[sgm.K])
        r.op("vector", lambda v: v.tensor_scalar(sgm[:, 0:40], sgm[:, 0:40], 1.0, None, ALU.add),
             reads=[sgm.K], writes=[sgm.K])
        r.op("vector", lambda v: v.reciprocal(sgm[:, 0:40], sgm[:, 0:40]),
             reads=[sgm.K], writes=[sgm.K])
        r.op("vector", lambda v: v.tensor_tensor(sgm[:, 0:40], sgm[:, 0:40], maskv[:, 0:40], ALU.mult),
             reads=[sgm.K, maskv.K], writes=[sgm.K])
        v3d = vtm[:, i * 520: (i + 1) * 520].rearrange("p (h x) -> p h x", x=65)[:, :, 0:64]
        vk_all = vtm.k(i * 520, (i + 1) * 520)
        for e in range(E):
            xv_b = xvt[:, e * 64:(e + 1) * 64].rearrange("p (o x) -> p o x", o=1).broadcast_to([128, H, 64])
            sg_b = sgm[:, 0:40].rearrange("p (h e) -> p h e", e=E)[:, :, e:e + 1].broadcast_to([128, H, 64])
            if e == 0:
                r.op("vector", lambda v, xv_b=xv_b, sg_b=sg_b, v3d=v3d: v.tensor_tensor(
                    v3d, xv_b, sg_b, ALU.mult),
                    reads=[xvt.K, sgm.K], writes=[vk_all])
            else:
                r.op("vector", lambda v, xv_b=xv_b, sg_b=sg_b: v.tensor_tensor(
                    vmt[:, 0:512].rearrange("p (h x) -> p h x", x=64), xv_b, sg_b, ALU.mult),
                    reads=[xvt.K, sgm.K], writes=[vmt.K])
                r.op("vector", lambda v, v3d=v3d: v.tensor_tensor(
                    v3d, v3d, vmt[:, 0:512].rearrange("p (h x) -> p h x", x=64), ALU.add),
                    reads=[vmt.K, vk_all], writes=[vk_all])

    emit_ln_ufm(ctx, R, ufm, T, D, tile_cb=gate_vmix_tile)
    for p in range(npair):
        for icol in range(T // 512):
            pp = P0 if icol % 2 == 0 else P1
            for c in range(nd):
                r.op("tensor", lambda t, p=p, icol=icol, c=c, pp=pp: t.matmul(
                    pp[:, 0:512], wk_sb[p][:, c * 128:(c + 1) * 128],
                    ufm[:, c * T + icol * 512: c * T + icol * 512 + 512],
                    start=(c == 0), stop=(c == nd - 1)),
                    reads=[wk_sb[p].K, ufm.K], writes=[pp.K])
            r.op("scalar", lambda s, p=p, icol=icol, pp=pp: s.activation(
                kfm[p][:, icol * 512:(icol + 1) * 512], pp[:, 0:512], AF.Copy, bias=0.0, scale=1.0),
                reads=[pp.K], writes=[kfm[p].k(icol * 512, (icol + 1) * 512)])

    A.release()

    NIQ = 512
    A.mark()
    qsb = ctx.sb(NIQ, BF16)
    ptb = [[ctx.sb(NIQ, BF16) for _ in range(2)] for _ in range(2)]
    attm = ctx.sb((NIQ // 128) * 512, BF16)
    oev = ctx.sb(NIQ, BF16)
    zcol = ctx.sb(1, F32)
    zt = ctx.sb(E * 64, F32)
    ztb = ctx.sb(E * 64, BF16)
    zts = ctx.sb(E * 128, BF16)
    PS = [[ctx.pbank(0), ctx.pbank(1)], [ctx.pbank(2), ctx.pbank(3)]]
    PSpair = [ctx.P.at(0, 1024, F32), ctx.P.at(4096, 1024, F32)]
    ptbP = [ctx.sb(1024, BF16), ctx.sb(1024, BF16)]
    PO = [ctx.pbank(4), ctx.pbank(5)]
    PQ = ctx.pbank(0)
    PT = ctx.pbank(6, BF16)
    POut = [ctx.pbank(4), ctx.pbank(5)]

    def st_mm(p, hh, j):
        ps = PS[j % 2][hh]
        r.op("tensor", lambda t, p=p, hh=hh, j=j, ps=ps: t.matmul(
            ps[:, 0:NIQ], kfm[p][hh * 64:(hh + 1) * 64, j * 128:(j + 1) * 128],
            qsb[hh * 64:(hh + 1) * 64, 0:NIQ],
            start=True, stop=True, tile_position=(hh * 64, 0)),
            reads=[kfm[p].K, qsb.K], writes=[ps.K])

    for iq in range(T // NIQ):
        for p in range(npair):
            for c in range(nd):
                r.op("tensor", lambda t, p=p, c=c, iq=iq: t.matmul(
                    PQ[:, 0:NIQ], wq_sb[p][:, c * 128:(c + 1) * 128],
                    ufm[:, c * T + iq * NIQ: c * T + iq * NIQ + NIQ],
                    start=(c == 0), stop=(c == nd - 1)),
                    reads=[wq_sb[p].K, ufm.K], writes=[PQ.K])
            r.op("scalar", lambda s: s.activation(qsb[:, 0:NIQ], PQ[:, 0:NIQ], AF.Copy, bias=0.0, scale=1.0),
                 reads=[PQ.K], writes=[qsb.K])
            st_mm(p, 0, 0)
            st_mm(p, 1, 0)
            for j in range(nt):
                psp = PSpair[j % 2]
                ptp = ptbP[j % 2]
                r.op("scalar", lambda s, psp=psp, ptp=ptp: s.activation(
                    ptp[:, 0:1024], psp[:, 0:1024], AF.Exp, bias=0.0, scale=0.125),
                    reads=[psp.K], writes=[ptp.K])
                if j + 1 < nt:
                    st_mm(p, 0, j + 1)
                    st_mm(p, 1, j + 1)
                for hh in range(2):
                    h = 2 * p + hh
                    po = PO[hh]
                    r.op("tensor", lambda t, h=h, j=j, po=po, ptp=ptp, hh=hh: t.matmul(
                        po[0:65, 0:NIQ], vtm[:, j * 520 + h * 65: j * 520 + (h + 1) * 65],
                        ptp[:, hh * NIQ:(hh + 1) * NIQ], start=(j == 0), stop=(j == nt - 1)),
                        reads=[vtm.K, ptp.K], writes=[po.K])
            for hh in range(2):
                h = 2 * p + hh
                po = PO[hh]
                r.op("scalar", lambda s, po=po: s.activation(oev[0:65, 0:NIQ], po[0:65, 0:NIQ],
                                                             AF.Copy, bias=0.0, scale=1.0),
                     reads=[po.K], writes=[oev.K])
                for ic in range(NIQ // 128):
                    r.op("tensor", lambda t, ic=ic: t.transpose(
                        PT[:, 0:65], oev[0:65, ic * 128:(ic + 1) * 128], ctx.ident[0:65, 0:65]),
                        reads=[oev.K, ctx.ident.K], writes=[PT.k(0, 128)])
                    r.op("vector", lambda v: v.reciprocal(zcol[:, 0:1], PT[:, 64:65]),
                         reads=[PT.k(0, 128)], writes=[zcol.K])
                    ak = attm.k(ic * 512 + h * 64, ic * 512 + (h + 1) * 64)
                    r.op("vector", lambda v, h=h, ic=ic: v.tensor_scalar(
                        attm[:, ic * 512 + h * 64: ic * 512 + (h + 1) * 64],
                        PT[:, 0:64], zcol[:, 0:1], None, ALU.mult),
                        reads=[PT.k(0, 128), zcol.K], writes=[ak])
        for ic in range(NIQ // 128):
            i = iq * (NIQ // 128) + ic
            for e in range(E):
                for h in range(H):
                    zsl = zt[:, e * 64:(e + 1) * 64]
                    zk = zt.k(e * 64, (e + 1) * 64)
                    msl = maskd[:, i * 40 + h * E + e: i * 40 + h * E + e + 1]
                    asl = attm[:, ic * 512 + h * 64: ic * 512 + (h + 1) * 64]
                    if h == 0:
                        r.op("vector", lambda v, zsl=zsl, msl=msl, asl=asl: v.tensor_scalar(
                            zsl, asl, msl, None, ALU.mult),
                            reads=[attm.K, maskd.K], writes=[zk])
                    else:
                        r.op("vector", lambda v, zsl=zsl, msl=msl, asl=asl: v.scalar_tensor_tensor(
                            zsl, asl, msl, zsl, ALU.mult, ALU.add),
                            reads=[attm.K, maskd.K, zk], writes=[zk])
            r.op("vector", lambda v: v.tensor_copy(ztb[:, 0:E * 64], zt[:, 0:E * 64]),
                 reads=[zt.K], writes=[ztb.K])
            for e in range(E):
                r.op("tensor", lambda t, e=e: t.transpose(
                    PT[0:64, e * 128:(e + 1) * 128], ztb[:, e * 64:(e + 1) * 64], ctx.ident[:, 0:128]),
                    reads=[ztb.K, ctx.ident.K], writes=[PT.k(e * 128, (e + 1) * 128)])
                r.op("vector", lambda v, e=e: v.tensor_copy(
                    zts[0:64, e * 128:(e + 1) * 128], PT[0:64, e * 128:(e + 1) * 128]),
                    reads=[PT.k(e * 128, (e + 1) * 128)], writes=[zts.k(e * 128, (e + 1) * 128)])
            for dh in range(2):
                PP = POut[dh]
                for e in range(E):
                    r.op("tensor", lambda t, e=e, dh=dh, PP=PP: t.matmul(
                        PP[:, 0:512], zts[0:64, e * 128:(e + 1) * 128],
                        wo_sb[0:64, e * 1024 + dh * 512: e * 1024 + dh * 512 + 512],
                        start=(e == 0), stop=(e == E - 1)),
                        reads=[zts.K, wo_sb.K], writes=[PP.K])
                r.op("vector", lambda v, i=i, dh=dh, PP=PP: v.tensor_tensor(
                    R[:, i * 1024 + dh * 512: i * 1024 + dh * 512 + 512],
                    R[:, i * 1024 + dh * 512: i * 1024 + dh * 512 + 512], PP[:, 0:512], ALU.add),
                    reads=[PP.K, R.k(i * 1024 + dh * 512, i * 1024 + dh * 512 + 512)],
                    writes=[R.k(i * 1024 + dh * 512, i * 1024 + dh * 512 + 512)])
    A.release()
    A.release()


def _build_transposed(ctx, dst, src_ap_fn, n_in_tiles, cols_in, stages, PTa, PTb,
                      dst_col_fn):
    r = ctx.rec
    np_ = cols_in // 128
    if not isinstance(stages, (list, tuple)):
        stages = [stages, stages]
    for it in range(n_in_tiles):
        stage = stages[it % len(stages)]
        r.op("gpsimd", lambda g, it=it, stage=stage: g.dma_start(
            out=stage[:, 0:cols_in], in_=src_ap_fn(it)),
            dma=True, writes=[stage.k(0, cols_in)])
        for pc in range(np_):
            PT = PTa if pc % 2 == 0 else PTb
            r.op("tensor", lambda t, pc=pc, PT=PT, stage=stage: t.transpose(
                PT[:, 0:128], stage[:, pc * 128:(pc + 1) * 128], ctx.ident[:, 0:128]),
                reads=[stage.k(0, cols_in), ctx.ident.K], writes=[PT.k(0, 128)])
            c0 = dst_col_fn(it, pc)
            r.op("vector", lambda v, c0=c0, PT=PT: v.tensor_copy(
                dst[:, c0:c0 + 128], PT[:, 0:128]),
                reads=[PT.k(0, 128)], writes=[dst.k(c0, c0 + 128)])


def emit_l1_ffn(ctx, R, w, T=2048, D=1024, E=16, HD=512, S=2048, phases=(1, 1, 1), gate_steps=99):
    r = ctx.rec
    nt, nd = T // 128, D // 128
    do_gate, do_shared, do_moe = phases
    A = ctx.A
    A.mark()
    ufm = ctx.sb(nd * T, BF16)
    moeacc = ctx.sb(nt * D, F32)
    disp = ctx.sb(nt * E, F32)

    PTa, PTb = ctx.pbank(6, BF16), ctx.pbank(7, BF16)
    P0, P1 = ctx.pbank(0), ctx.pbank(1)

    A.mark()
    gwT = ctx.sb(nd * E, BF16)
    r.op("gpsimd", lambda g: g.dma_start(
        out=gwT.ap3(slice(None), "p (c e) -> p c e", e=E),
        in_=w["gate_wt"].rearrange("(c p) e -> p c e", p=128)), dma=True,
        writes=[gwT.K])
    gl = ctx.sb(E, F32)
    glm = ctx.sb(E, F32)
    eg = ctx.sb(E, F32)
    msk = ctx.sb(E, F32)
    m1 = ctx.sb(1, F32)
    nm1 = ctx.sb(1, F32)
    m2 = ctx.sb(1, F32)
    e2 = ctx.sb(1, F32)

    def gate_tile(i):
        if not do_gate:
            return
        for c in range(nd):
            r.op("tensor", lambda t, i=i, c=c: t.matmul(
                P0[:, 0:E], ufm[:, c * T + i * 128: c * T + (i + 1) * 128],
                gwT[:, c * E:(c + 1) * E], start=(c == 0), stop=(c == nd - 1)),
                reads=[ufm.k(c * T + i * 128, c * T + (i + 1) * 128), gwT.K], writes=[P0.K])
        r.op("vector", lambda v: v.tensor_copy(gl[:, 0:E], P0[:, 0:E]),
             reads=[P0.K], writes=[gl.K])
        r.op("vector", lambda v: v.reduce_max(m1[:, 0:1], gl[:, 0:E], axis=AX.X),
             reads=[gl.K], writes=[m1.K])
        r.op("vector", lambda v: v.tensor_scalar(glm[:, 0:E], gl[:, 0:E], m1[:, 0:1], -1e30, ALU.is_ge, ALU.mult),
             reads=[gl.K, m1.K], writes=[glm.K])
        r.op("vector", lambda v: v.tensor_tensor(glm[:, 0:E], glm[:, 0:E], gl[:, 0:E], ALU.add),
             reads=[glm.K, gl.K], writes=[glm.K])
        r.op("vector", lambda v: v.reduce_max(m2[:, 0:1], glm[:, 0:E], axis=AX.X),
             reads=[glm.K], writes=[m2.K])
        r.op("vector", lambda v: v.tensor_scalar(msk[:, 0:E], gl[:, 0:E], m2[:, 0:1], None, ALU.is_ge),
             reads=[gl.K, m2.K], writes=[msk.K])
        r.op("vector", lambda v: v.tensor_scalar(nm1[:, 0:1], m1[:, 0:1], -1.0, None, ALU.mult),
             reads=[m1.K], writes=[nm1.K])
        r.op("scalar", lambda s: s.activation(eg[:, 0:E], gl[:, 0:E], AF.Exp, bias=nm1[:, 0:1], scale=1.0),
             reads=[gl.K, nm1.K], writes=[eg.K])
        r.op("scalar", lambda s: s.activation(e2[:, 0:1], m2[:, 0:1], AF.Exp, bias=nm1[:, 0:1], scale=1.0),
             reads=[m2.K, nm1.K], writes=[e2.K])
        r.op("vector", lambda v: v.tensor_scalar(e2[:, 0:1], e2[:, 0:1], 1.0, None, ALU.add),
             reads=[e2.K], writes=[e2.K])
        r.op("vector", lambda v: v.reciprocal(e2[:, 0:1], e2[:, 0:1]),
             reads=[e2.K], writes=[e2.K])
        r.op("vector", lambda v: v.tensor_tensor(eg[:, 0:E], eg[:, 0:E], msk[:, 0:E], ALU.mult),
             reads=[eg.K, msk.K], writes=[eg.K])
        r.op("vector", lambda v, i=i: v.tensor_scalar(disp[:, i * E:(i + 1) * E], eg[:, 0:E], e2[:, 0:1], None, ALU.mult),
             reads=[eg.K, e2.K], writes=[disp.k(i * E, (i + 1) * E)])

    emit_ln_ufm(ctx, R, ufm, T, D, tile_cb=gate_tile)
    A.release()

    if not do_gate:
        pass
    A.mark()
    SLAB = 512
    sfc3T = ctx.sb((SLAB // 128) * D, BF16)
    sfc1T = ctx.sb(8 * SLAB, BF16)
    sfc2T = ctx.sb(8 * SLAB, BF16)
    msb2 = [ctx.sb((SLAB // 128) * 512, BF16) for _ in range(2)]
    msil = ctx.sb(512, BF16)
    PG1, PG2 = ctx.pbank(2), ctx.pbank(3)
    PSH = [ctx.pbank(4), ctx.pbank(5)]
    for sl in (range(S // SLAB) if do_shared else []):
        for nm, dstb in (("sfc1t", sfc1T), ("sfc2t", sfc2T)):
            r.op("gpsimd", lambda g, nm=nm, sl=sl, dstb=dstb: g.dma_start(
                out=dstb.ap3(slice(None), "p (c s) -> p c s", s=SLAB),
                in_=w[nm][:, sl * SLAB:(sl + 1) * SLAB].rearrange("(c p) s -> p c s", p=128)),
                dma=True, writes=[dstb.K])
        r.op("gpsimd", lambda g, sl=sl: g.dma_start(
            out=sfc3T.ap3(slice(None), "p (sc d) -> p sc d", d=D),
            in_=w["sfc3t"][sl * SLAB:(sl + 1) * SLAB, :].rearrange("(sc p) d -> p sc d", p=128)),
            dma=True, writes=[sfc3T.K])
        for tb in range(T // 512):
            msb = msb2[tb % 2]
            for sc in range(SLAB // 128):
                for c in range(nd):
                    r.op("tensor", lambda t, sl=sl, tb=tb, sc=sc, c=c: t.matmul(
                        PG1[:, 0:512], sfc1T[:, c * SLAB + sc * 128: c * SLAB + (sc + 1) * 128],
                        ufm[:, c * T + tb * 512: c * T + tb * 512 + 512],
                        start=(c == 0), stop=(c == nd - 1)),
                        reads=[sfc1T.K, ufm.K], writes=[PG1.K])
                for c in range(nd):
                    r.op("tensor", lambda t, sl=sl, tb=tb, sc=sc, c=c: t.matmul(
                        PG2[:, 0:512], sfc2T[:, c * SLAB + sc * 128: c * SLAB + (sc + 1) * 128],
                        ufm[:, c * T + tb * 512: c * T + tb * 512 + 512],
                        start=(c == 0), stop=(c == nd - 1)),
                        reads=[sfc2T.K, ufm.K], writes=[PG2.K])
                r.op("scalar", lambda s: s.activation(msil[:, 0:512], PG1[:, 0:512], AF.Silu, bias=0.0, scale=1.0),
                     reads=[PG1.K], writes=[msil.K])
                r.op("vector", lambda v, sc=sc, msb=msb: v.tensor_tensor(
                    msb[:, sc * 512:(sc + 1) * 512], msil[:, 0:512], PG2[:, 0:512], ALU.mult),
                    reads=[msil.K, PG2.K], writes=[msb.k(sc * 512, (sc + 1) * 512)])
            for tt in range(4):
                i = tb * 4 + tt
                for dh in range(2):
                    PP = PSH[dh]
                    for sc in range(SLAB // 128):
                        r.op("tensor", lambda t, tt=tt, dh=dh, sc=sc, PP=PP, msb=msb: t.matmul(
                            PP[:, 0:512], msb[:, sc * 512 + tt * 128: sc * 512 + tt * 128 + 128],
                            sfc3T[:, sc * 1024 + dh * 512: sc * 1024 + dh * 512 + 512],
                            start=(sc == 0), stop=(sc == SLAB // 128 - 1)),
                            reads=[msb.k(sc * 512 + tt * 128, sc * 512 + tt * 128 + 128), sfc3T.K],
                            writes=[PP.K])
                    ck = moeacc.k(i * D + dh * 512, i * D + dh * 512 + 512)
                    if sl == 0:
                        r.op("vector", lambda v, i=i, dh=dh, PP=PP: v.tensor_copy(
                            moeacc[:, i * D + dh * 512: i * D + dh * 512 + 512], PP[:, 0:512]),
                            reads=[PP.K], writes=[ck])
                    else:
                        r.op("vector", lambda v, i=i, dh=dh, PP=PP: v.tensor_tensor(
                            moeacc[:, i * D + dh * 512: i * D + dh * 512 + 512],
                            moeacc[:, i * D + dh * 512: i * D + dh * 512 + 512], PP[:, 0:512], ALU.add),
                            reads=[PP.K, ck], writes=[ck])
    A.release()

    A.mark()
    w1T2 = [ctx.sb(8 * 512, BF16) for _ in range(2)]
    w2T2 = [ctx.sb(4 * 1024, BF16) for _ in range(2)]
    hw2 = [ctx.sb(4 * 512, BF16), ctx.sb(4 * 512, BF16)]
    PH = [ctx.pbank(2), ctx.pbank(3)]
    PM = [ctx.pbank(4), ctx.pbank(5)]

    def build_we(e):
        b1, b2 = w1T2[e % 2], w2T2[e % 2]
        r.op("gpsimd", lambda g, e=e, b1=b1: g.dma_start(
            out=b1.ap3(slice(None), "p (c h) -> p c h", h=HD),
            in_=w["w1t"][e].rearrange("(c p) h -> p c h", p=128)),
            dma=True, writes=[b1.K])
        r.op("gpsimd", lambda g, e=e, b2=b2: g.dma_start(
            out=b2.ap3(slice(None), "p (hc d) -> p hc d", d=D),
            in_=w["w2t"][e].rearrange("(hc p) d -> p hc d", p=128)),
            dma=True, writes=[b2.K])

    if do_moe:
        build_we(0)
    for e in (range(E) if do_moe else []):
        w1T = w1T2[e % 2]
        w2T = w2T2[e % 2]
        if e + 1 < E:
            build_we(e + 1)
        for tb in range(T // 512):
            hw = hw2[tb % 2]
            for hc in range(4):
                PHx = PH[hc % 2]
                for c in range(nd):
                    r.op("tensor", lambda t, e=e, tb=tb, hc=hc, c=c, PHx=PHx, w1T=w1T: t.matmul(
                        PHx[:, 0:512], w1T[:, c * 512 + hc * 128: c * 512 + (hc + 1) * 128],
                        ufm[:, c * T + tb * 512: c * T + tb * 512 + 512],
                        start=(c == 0), stop=(c == nd - 1)),
                        reads=[w1T.K, ufm.K], writes=[PHx.K])
                r.op("scalar", lambda s, hc=hc, PHx=PHx, hw=hw: s.activation(
                    hw[:, hc * 512:(hc + 1) * 512], PHx[:, 0:512], AF.Silu, bias=0.0, scale=1.0),
                    reads=[PHx.K], writes=[hw.k(hc * 512, (hc + 1) * 512)])
            for tt in range(4):
                i = tb * 4 + tt
                for dh in range(2):
                    PMx = PM[dh]
                    for hc in range(4):
                        r.op("tensor", lambda t, tt=tt, dh=dh, hc=hc, PMx=PMx, hw=hw, w2T=w2T: t.matmul(
                            PMx[:, 0:512], hw[:, hc * 512 + tt * 128: hc * 512 + tt * 128 + 128],
                            w2T[:, hc * 1024 + dh * 512: hc * 1024 + dh * 512 + 512],
                            start=(hc == 0), stop=(hc == 3)),
                            reads=[hw.k(hc * 512 + tt * 128, hc * 512 + tt * 128 + 128), w2T.K],
                            writes=[PMx.K])
                    ck = moeacc.k(i * D + dh * 512, i * D + dh * 512 + 512)
                    r.op("vector", lambda v, i=i, e=e, dh=dh, PMx=PMx: v.scalar_tensor_tensor(
                        moeacc[:, i * D + dh * 512: i * D + dh * 512 + 512],
                        PMx[:, 0:512], disp[:, i * E + e: i * E + e + 1],
                        moeacc[:, i * D + dh * 512: i * D + dh * 512 + 512],
                        ALU.mult, ALU.add),
                        reads=[PMx.K, disp.K, ck], writes=[ck])
    A.release()
    if not (do_shared or do_moe):
        for i in range(nt):
            r.op("gpsimd", lambda g, i=i: g.memset(moeacc[:, i * D:(i + 1) * D], 0.0),
                 writes=[moeacc.k(i * D, (i + 1) * D)])
    elif not do_shared:
        for i in range(nt):
            r.op("gpsimd", lambda g, i=i: g.memset(moeacc[:, i * D:(i + 1) * D], 0.0),
                 writes=[moeacc.k(i * D, (i + 1) * D)])
    for i in range(nt):
        for dh in range(2):
            rk = R.k(i * D + dh * 512, i * D + dh * 512 + 512)
            r.op("vector", lambda v, i=i, dh=dh: v.tensor_tensor(
                R[:, i * D + dh * 512: i * D + dh * 512 + 512],
                R[:, i * D + dh * 512: i * D + dh * 512 + 512],
                moeacc[:, i * D + dh * 512: i * D + dh * 512 + 512], ALU.add),
                reads=[moeacc.K, rk], writes=[rk])
    A.release()


def emit_l0_attn(ctx, R, w, T=2048, D=1024, H=4, K=256, V=256, M=64):
    r = ctx.rec
    nt, nd = T // 128, D // 128
    A = ctx.A
    A.mark()
    tri_f = ctx.sb(128, F32)
    onesf = ctx.sb(128, F32)
    r.op("gpsimd", lambda g: g.memset(tri_f[:, 0:128], 0.0), writes=[tri_f.K])
    r.op("gpsimd", lambda g: g.affine_select(
        out=tri_f[:, 0:128], in_=tri_f[:, 0:128], compare_op=ALU.is_gt,
        fill=1.0, base=0, pattern=[[-1, 128]], channel_multiplier=1),
        reads=[tri_f.K], writes=[tri_f.K])
    r.op("gpsimd", lambda g: g.memset(onesf[:, 0:128], 1.0), writes=[onesf.K])

    wq_sb = ctx.sb(nd * 1024, BF16)
    wk_sb = ctx.sb(nd * 1024, BF16)
    wv_sb = ctx.sb(nd * 1024, BF16)
    wf_sb = ctx.sb(nd * 256, BF16)
    wo_sb = ctx.sb(nd * 1024, BF16)
    for nm, buf, cols in (("Wq", wq_sb, 1024), ("Wk", wk_sb, 1024), ("Wv", wv_sb, 1024),
                          ("Wf", wf_sb, 256), ("Wo", wo_sb, 1024)):
        for c in range(nd):
            r.op("gpsimd", lambda g, nm=nm, c=c, buf=buf, cols=cols: g.dma_start(
                out=buf[:, c * cols:(c + 1) * cols], in_=w[nm][c * 128:(c + 1) * 128, :]),
                dma=True, writes=[buf.k(c * cols, (c + 1) * cols)])
    gn_row = ctx.sb(256, F32)
    G_sb = ctx.sb(256, F32)
    PB = ctx.pbank(4)
    r.op("sync", lambda s: s.dma_start(out=gn_row[0:1, 0:256],
                                       in_=w["g_norm_w"].rearrange("(o v) -> o v", o=1)),
         dma=True, writes=[gn_row.K])
    r.op("tensor", lambda t: t.matmul(PB[:, 0:256], onesf[0:1, 0:128], gn_row[0:1, 0:256],
                                      start=True, stop=True),
         reads=[onesf.K, gn_row.K], writes=[PB.K])
    r.op("vector", lambda v: v.tensor_copy(G_sb[:, 0:256], PB[:, 0:256]),
         reads=[PB.K], writes=[G_sb.K])

    hk_f = [ctx.sb(512, F32) for _ in range(2)]
    hkb = [ctx.sb(512, BF16) for _ in range(2)]
    hv_f = [ctx.sb(1024, F32) for _ in range(2)]
    hvb = [ctx.sb(1024, BF16) for _ in range(2)]
    for b_ in hk_f + hkb + hv_f + hvb:
        r.op("gpsimd", lambda g, b_=b_: g.memset(b_[:, 0:b_.cols], 0.0), writes=[b_.K])

    GT = 256
    ng = T // GT
    ntg = GT // 128
    ufm_g = ctx.sb(nd * GT, BF16)
    qfm = ctx.sb(8 * GT, BF16)
    kfm2 = ctx.sb(8 * GT, BF16)
    ofm_g = ufm_g
    A_s2 = [ctx.sb(256, F32) for _ in range(2)]
    Am2 = [ctx.sb(256, F32) for _ in range(2)]
    stil2 = [ctx.sb(256, BF16) for _ in range(2)]
    acr2 = [ctx.sb(256, F32) for _ in range(2)]
    acb2 = [ctx.sb(256, F32) for _ in range(2)]
    actc2 = [ctx.sb(2, F32) for _ in range(2)]
    ktm2 = [ctx.sb(8 * 128, BF16) for _ in range(2)]
    Sm2 = [ctx.sb(128, BF16) for _ in range(2)]
    wmt2 = [ctx.sb(128, BF16) for _ in range(2)]
    ptT2 = [ctx.sb(128, BF16) for _ in range(2)]
    stT2 = [ctx.sb(128, BF16) for _ in range(2)]
    lg2 = [ctx.sb(64, F32) for _ in range(2)]
    expo2 = [ctx.sb(64, F32) for _ in range(2)]
    ptil2 = [ctx.sb(64, BF16) for _ in range(2)]
    o_sb2 = [ctx.sb(256, F32) for _ in range(2)]
    obf2 = [ctx.sb(256, BF16) for _ in range(2)]
    zs2 = [ctx.sb(4, F32) for _ in range(2)]
    ztmp2 = [ctx.sb(256, F32) for _ in range(2)]
    osq2 = [ctx.sb(256, F32) for _ in range(2)]
    vtm_c2 = [ctx.sb(1024, BF16) for _ in range(2)]
    f8c2 = [ctx.sb(256, F32) for _ in range(2)]
    P0, P1 = ctx.pbank(0), ctx.pbank(1)
    PLOG2 = [ctx.pbank(0), ctx.pbank(1)]
    PSTW2 = [ctx.pbank(2), ctx.pbank(3)]
    POx2 = [ctx.pbank(4), ctx.pbank(5)]
    PC2 = [ctx.pbank(2), ctx.pbank(3)]
    PB2 = [ctx.pbank(4), ctx.pbank(5)]
    PTa, PTb = ctx.pbank(6, BF16), ctx.pbank(7, BF16)
    POP = [ctx.pbank(2), ctx.pbank(3)]

    def do_head(gi, il, cur, nxt, h, vtm_c, f8c, A_s, Am, stil, acr, acb_sb, actc, ktm, PC, PB):
        Sm = Sm2[h % 2]
        wmt = wmt2[h % 2]
        ptT = ptT2[h % 2]
        stT = stT2[h % 2]
        lg = lg2[h % 2]
        expo = expo2[h % 2]
        ptil = ptil2[h % 2]
        o_sb = o_sb2[h % 2]
        obf = obf2[h % 2]
        zs = zs2[h % 2]
        osq = osq2[h % 2]
        PLOG = PLOG2[h % 2]
        PST = PSTW2[h % 2]
        PW = PST
        POx = POx2[h % 2]
        PD = PLOG
        hm = stil[:, h * 64:(h + 1) * 64]
        hmk = stil.k(h * 64, (h + 1) * 64)
        for kc in range(2):
            oc = 2 * h + kc
            r.op("tensor", lambda t, oc=oc, il=il, kc=kc, cur=cur: t.matmul(
                PLOG[:, 0:64], qfm[:, oc * GT + il * 128: oc * GT + (il + 1) * 128],
                hkb[cur][:, oc * 64:(oc + 1) * 64],
                start=(kc == 0), stop=False),
                reads=[qfm.K, hkb[cur].K], writes=[PLOG.K])
        for kc in range(2):
            oc = 2 * h + kc
            r.op("tensor", lambda t, oc=oc, il=il, kc=kc: t.matmul(
                PST[:, 0:128], kfm2[:, oc * GT + il * 128: oc * GT + (il + 1) * 128],
                qfm[:, oc * GT + il * 128: oc * GT + (il + 1) * 128],
                start=(kc == 0), stop=(kc == 1)),
                reads=[kfm2.K, qfm.K], writes=[PST.K])
        r.op("vector", lambda v: v.tensor_tensor(Sm[:, 0:128], PST[:, 0:128], ctx.tri[:, 0:128], ALU.mult),
             reads=[PST.K, ctx.tri.K], writes=[Sm.K])
        r.op("tensor", lambda t, hm=hm: t.matmul(
            PLOG[:, 0:64], Sm[:, 0:128], hm, start=False, stop=True),
            reads=[Sm.K, hmk], writes=[PLOG.K])
        yield
        r.op("vector", lambda v, h=h: v.tensor_tensor(
            lg[:, 0:64], PLOG[:, 0:64], A_s[:, h * 64:(h + 1) * 64], ALU.mult),
            reads=[PLOG.K, A_s.K], writes=[lg.K])
        r.op("vector", lambda v: v.reduce_max(zs[:, 0:1], lg[:, 0:64], axis=AX.X, negate=True),
             reads=[lg.K], writes=[zs.K])
        r.op("vector", lambda v: v.tensor_scalar(zs[:, 1:2], zs[:, 0:1], 0.0625, None, ALU.mult),
             reads=[zs.K], writes=[zs.K])
        r.op("scalar", lambda s: s.activation(expo[:, 0:64], lg[:, 0:64], AF.Exp,
                                              bias=zs[:, 1:2], scale=0.0625, accum_out=zs[:, 2:3]),
             reads=[lg.K, zs.K], writes=[expo.K, zs.K])
        r.op("vector", lambda v: v.reciprocal(zs[:, 2:3], zs[:, 2:3]),
             reads=[zs.K], writes=[zs.K])
        r.op("vector", lambda v: v.tensor_scalar(expo[:, 0:64], expo[:, 0:64], zs[:, 2:3], None, ALU.mult),
             reads=[expo.K, zs.K], writes=[expo.K])
        r.op("vector", lambda v, h=h: v.tensor_tensor(
            ptil[:, 0:64], expo[:, 0:64], A_s[:, h * 64:(h + 1) * 64], ALU.mult),
            reads=[expo.K, A_s.K], writes=[ptil.K])
        yield
        r.op("tensor", lambda t: t.transpose(PTa[0:64, 0:128], ptil[:, 0:64], ctx.ident[:, 0:128]),
             reads=[ptil.K, ctx.ident.K], writes=[PTa.k(0, 128)])
        r.op("vector", lambda v: v.tensor_copy(ptT[0:64, 0:128], PTa[0:64, 0:128]),
             reads=[PTa.k(0, 128)], writes=[ptT.K])
        r.op("tensor", lambda t, hm=hm: t.transpose(PTb[0:64, 0:128], hm, ctx.ident[:, 0:128]),
             reads=[hmk, ctx.ident.K], writes=[PTb.k(0, 128)])
        r.op("vector", lambda v: v.tensor_copy(stT[0:64, 0:128], PTb[0:64, 0:128]),
             reads=[PTb.k(0, 128)], writes=[stT.K])
        yield
        r.op("tensor", lambda t, h=h, cur=cur: t.matmul(
            POx[:, 0:256], ptT[0:64, 0:128], hvb[cur][0:64, h * 256:(h + 1) * 256],
            start=True, stop=False),
            reads=[ptT.K, hvb[cur].K], writes=[POx.K])
        r.op("tensor", lambda t: t.matmul(
            PW[:, 0:128], stT[0:64, 0:128], ptT[0:64, 0:128], start=True, stop=True),
            reads=[stT.K, ptT.K], writes=[PW.K])
        r.op("vector", lambda v: v.tensor_tensor(wmt[:, 0:128], PW[:, 0:128], ctx.tri[:, 0:128], ALU.mult),
             reads=[PW.K, ctx.tri.K], writes=[wmt.K])
        r.op("tensor", lambda t, il=il, h=h: t.matmul(
            POx[:, 0:256], wmt[:, 0:128],
            vtm_c[:, h * 256:(h + 1) * 256],
            start=False, stop=True),
            reads=[wmt.K, vtm_c.K], writes=[POx.K])
        yield
        for kc in range(2):
            oc = 2 * h + kc
            r.op("tensor", lambda t, oc=oc, kc=kc, hm=hm: t.matmul(
                PD[:, 0:64], ktm[:, oc * 128:(oc + 1) * 128], hm, start=True, stop=True),
                reads=[ktm.K, hmk], writes=[PD.K])
            r.op("vector", lambda v, oc=oc, cur=cur, nxt=nxt: v.tensor_tensor(
                hk_f[nxt][:, oc * 64:(oc + 1) * 64], hk_f[cur][:, oc * 64:(oc + 1) * 64],
                PD[:, 0:64], ALU.add),
                reads=[hk_f[cur].K, PD.K], writes=[hk_f[nxt].k(oc * 64, (oc + 1) * 64)])
            r.op("vector", lambda v, oc=oc, h=h, nxt=nxt: v.tensor_tensor(
                hk_f[nxt][:, oc * 64:(oc + 1) * 64], hk_f[nxt][:, oc * 64:(oc + 1) * 64],
                acb_sb[:, h * 64:(h + 1) * 64], ALU.mult),
                reads=[hk_f[nxt].k(oc * 64, (oc + 1) * 64), acb_sb.K],
                writes=[hk_f[nxt].k(oc * 64, (oc + 1) * 64)])
            r.op("vector", lambda v, oc=oc, nxt=nxt: v.tensor_copy(
                hkb[nxt][:, oc * 64:(oc + 1) * 64], hk_f[nxt][:, oc * 64:(oc + 1) * 64]),
                reads=[hk_f[nxt].k(oc * 64, (oc + 1) * 64)], writes=[hkb[nxt].k(oc * 64, (oc + 1) * 64)])
        r.op("tensor", lambda t, il=il, h=h, hm=hm: t.matmul(
            PD[0:64, 64:320], hm, vtm_c[:, h * 256:(h + 1) * 256],
            start=True, stop=True),
            reads=[hmk, vtm_c.K], writes=[PD.K])
        r.op("vector", lambda v, h=h, cur=cur, nxt=nxt: v.tensor_tensor(
            hv_f[nxt][0:64, h * 256:(h + 1) * 256], hv_f[cur][0:64, h * 256:(h + 1) * 256],
            PD[0:64, 64:320], ALU.add),
            reads=[hv_f[cur].K, PD.K], writes=[hv_f[nxt].k(h * 256, (h + 1) * 256)])
        r.op("vector", lambda v, h=h, nxt=nxt: v.tensor_scalar(
            hv_f[nxt][0:64, h * 256:(h + 1) * 256], hv_f[nxt][0:64, h * 256:(h + 1) * 256],
            actc[(h % 2) * 64:(h % 2) * 64 + 64, h // 2: h // 2 + 1], None, ALU.mult),
            reads=[hv_f[nxt].k(h * 256, (h + 1) * 256), actc.K],
            writes=[hv_f[nxt].k(h * 256, (h + 1) * 256)])
        r.op("vector", lambda v, h=h, nxt=nxt: v.tensor_copy(
            hvb[nxt][0:64, h * 256:(h + 1) * 256], hv_f[nxt][0:64, h * 256:(h + 1) * 256]),
            reads=[hv_f[nxt].k(h * 256, (h + 1) * 256)], writes=[hvb[nxt].k(h * 256, (h + 1) * 256)])
        r.op("vector", lambda v: v.tensor_copy(o_sb[:, 0:256], POx[:, 0:256]),
             reads=[POx.K], writes=[o_sb.K])
        yield
        r.op("vector", lambda v: v.tensor_tensor(osq[:, 0:256], o_sb[:, 0:256], o_sb[:, 0:256], ALU.mult),
             reads=[o_sb.K], writes=[osq.K])
        r.op("vector", lambda v: v.reduce_sum(zs[:, 3:4], osq[:, 0:256], axis=AX.X),
             reads=[osq.K], writes=[zs.K])
        r.op("vector", lambda v: v.tensor_scalar(zs[:, 3:4], zs[:, 3:4], 1.0 / 256, 1e-5, ALU.mult, ALU.add),
             reads=[zs.K], writes=[zs.K])
        r.op("scalar", lambda s: s.activation(zs[:, 3:4], zs[:, 3:4], AF.Ln, bias=0.0, scale=1.0),
             reads=[zs.K], writes=[zs.K])
        r.op("scalar", lambda s: s.activation(zs[:, 3:4], zs[:, 3:4], AF.Exp, bias=0.0, scale=-0.5),
             reads=[zs.K], writes=[zs.K])
        r.op("vector", lambda v: v.scalar_tensor_tensor(
            obf[:, 0:256], o_sb[:, 0:256], zs[:, 3:4], G_sb[:, 0:256], ALU.mult, ALU.mult),
            reads=[o_sb.K, zs.K, G_sb.K], writes=[obf.K])
        for vc in range(2):
            oc = 2 * h + vc
            PT = PTa if vc % 2 == 0 else PTb
            r.op("tensor", lambda t, vc=vc, PT=PT: t.transpose(
                PT[:, 0:128], obf[:, vc * 128:(vc + 1) * 128], ctx.ident[:, 0:128]),
                reads=[obf.K, ctx.ident.K], writes=[PT.k(0, 128)])
            r.op("vector", lambda v, oc=oc, il=il, PT=PT: v.tensor_copy(
                ofm_g[:, oc * GT + il * 128: oc * GT + (il + 1) * 128], PT[:, 0:128]),
                reads=[PT.k(0, 128)], writes=[ofm_g.k(oc * GT + il * 128, oc * GT + (il + 1) * 128)])

    def do_chunk(gi, il, cur):
        nxt = 1 - cur
        vtm_c = vtm_c2[il % 2]
        f8c = f8c2[il % 2]
        A_s = A_s2[il % 2]
        Am = Am2[il % 2]
        ef = ztmp2[il % 2]
        stil = stil2[il % 2]
        acr = acr2[il % 2]
        acb_sb = acb2[il % 2]
        actc = actc2[il % 2]
        ktm = ktm2[il % 2]
        ztmp = ztmp2[il % 2]
        PC = PC2[il % 2]
        PB = PB2[il % 2]
        for half in range(2):
            pp = P0 if half % 2 == 0 else P1
            for c in range(nd):
                r.op("tensor", lambda t, il=il, half=half, c=c, pp=pp: t.matmul(
                    pp[:, 0:512], ufm_g[:, c * GT + il * 128: c * GT + (il + 1) * 128],
                    wv_sb[:, c * 1024 + half * 512: c * 1024 + half * 512 + 512],
                    start=(c == 0), stop=(c == nd - 1)),
                    reads=[ufm_g.k(c * GT + il * 128, c * GT + (il + 1) * 128), wv_sb.K], writes=[pp.K])
            r.op("vector", lambda v, half=half, pp=pp: v.tensor_copy(
                vtm_c[:, half * 512: half * 512 + 512], pp[:, 0:512]),
                reads=[pp.K], writes=[vtm_c.k(half * 512, half * 512 + 512)])
        ppf = P0
        for c in range(nd):
            r.op("tensor", lambda t, il=il, c=c, ppf=ppf: t.matmul(
                ppf[:, 0:256], ufm_g[:, c * GT + il * 128: c * GT + (il + 1) * 128],
                wf_sb[:, c * 256:(c + 1) * 256],
                start=(c == 0), stop=(c == nd - 1)),
                reads=[ufm_g.k(c * GT + il * 128, c * GT + (il + 1) * 128), wf_sb.K], writes=[ppf.K])
        r.op("scalar", lambda s, ppf=ppf: s.activation(ztmp[:, 0:256], ppf[:, 0:256], AF.Abs, bias=0.0, scale=1.0),
             reads=[ppf.K], writes=[ztmp.K])
        r.op("scalar", lambda s: s.activation(ztmp[:, 0:256], ztmp[:, 0:256], AF.Exp, bias=0.0, scale=-1.0),
             reads=[ztmp.K], writes=[ztmp.K])
        r.op("scalar", lambda s: s.activation(ztmp[:, 0:256], ztmp[:, 0:256], AF.Ln,
                                              bias=ctx.ones_colf[:, 0:1], scale=1.0),
             reads=[ztmp.K, ctx.ones_colf.K], writes=[ztmp.K])
        r.op("vector", lambda v, ppf=ppf: v.scalar_tensor_tensor(
            f8c[:, 0:256], ppf[:, 0:256], 0.0, ztmp[:, 0:256],
            ALU.min, ALU.subtract),
            reads=[ppf.K, ztmp.K], writes=[f8c.K])
        fs = f8c[:, 0:256]
        fk = f8c.K
        r.op("tensor", lambda t, fs=fs: t.matmul(PC[:, 0:256], tri_f[:, 0:128], fs,
                                                 start=True, stop=True),
             reads=[tri_f.K, fk], writes=[PC.K])
        r.op("tensor", lambda t, fs=fs: t.matmul(PC[0:1, 256:512], onesf[:, 0:1], fs,
                                                 start=True, stop=True),
             reads=[onesf.K, fk], writes=[PC.K])
        r.op("scalar", lambda s: s.activation(A_s[:, 0:256], PC[:, 0:256], AF.Exp, bias=0.0, scale=0.125),
             reads=[PC.K], writes=[A_s.K])
        r.op("scalar", lambda s: s.activation(Am[:, 0:256], PC[:, 0:256], AF.Exp, bias=0.0, scale=-0.125),
             reads=[PC.K], writes=[Am.K])
        r.op("scalar", lambda s: s.activation(acr[0:1, 0:256], PC[0:1, 256:512], AF.Exp, bias=0.0, scale=0.125),
             reads=[PC.K], writes=[acr.K])
        r.op("scalar", lambda s, fs=fs: s.activation(ef[:, 0:256], fs, AF.Exp, bias=0.0, scale=0.125),
             reads=[fk], writes=[ef.K])
        r.op("vector", lambda v: v.tensor_tensor(ef[:, 0:256], ef[:, 0:256], Am[:, 0:256], ALU.mult),
             reads=[ef.K, Am.K], writes=[ef.K])
        r.op("vector", lambda v: v.tensor_tensor(stil[:, 0:256], Am[:, 0:256], ef[:, 0:256], ALU.subtract),
             reads=[Am.K, ef.K], writes=[stil.K])
        r.op("tensor", lambda t: t.matmul(PB[:, 0:256], onesf[0:1, 0:128], acr[0:1, 0:256],
                                          start=True, stop=True),
             reads=[onesf.K, acr.K], writes=[PB.K])
        r.op("vector", lambda v: v.tensor_copy(acb_sb[:, 0:256], PB[:, 0:256]),
             reads=[PB.K], writes=[acb_sb.K])
        for pc in range(2):
            r.op("tensor", lambda t, pc=pc: t.transpose(
                PB[:, 384 + pc: 385 + pc], acr[0:1, pc * 128:(pc + 1) * 128], onesf[0:1, 0:1]),
                reads=[acr.K, onesf.K], writes=[PB.K])
            r.op("vector", lambda v, pc=pc: v.tensor_copy(actc[:, pc:pc + 1], PB[:, 384 + pc: 385 + pc]),
                 reads=[PB.K], writes=[actc.K])
        for oc in range(8):
            PT = PTa if oc % 2 == 0 else PTb
            r.op("tensor", lambda t, oc=oc, il=il, PT=PT: t.transpose(
                PT[:, 0:128], kfm2[:, oc * GT + il * 128: oc * GT + (il + 1) * 128], ctx.ident[:, 0:128]),
                reads=[kfm2.K, ctx.ident.K], writes=[PT.k(0, 128)])
            r.op("vector", lambda v, oc=oc, PT=PT: v.tensor_copy(
                ktm[:, oc * 128:(oc + 1) * 128], PT[:, 0:128]),
                reads=[PT.k(0, 128)], writes=[ktm.k(oc * 128, (oc + 1) * 128)])
        yield
        for pair_ in ((0, 1), (2, 3)):
            gens = [do_head(gi, il, cur, nxt, h, vtm_c, f8c, A_s, Am, stil, acr, acb_sb, actc, ktm, PC, PB)
                    for h in pair_]
            while gens:
                nxt_gens = []
                for g_ in gens:
                    try:
                        next(g_)
                        nxt_gens.append(g_)
                    except StopIteration:
                        pass
                gens = nxt_gens

    cur = 0
    for gi in range(ng):
        tiles = [gi * ntg + k_ for k_ in range(ntg)]
        emit_ln_ufm(ctx, R, ufm_g, GT, D, tiles=tiles, col_base=gi * ntg)
        for nm_, wsb, dst in (("q", wq_sb, qfm), ("k", wk_sb, kfm2)):
            for oc in range(8):
                pp = P0 if oc % 2 == 0 else P1
                for c in range(nd):
                    r.op("tensor", lambda t, oc=oc, c=c, pp=pp, wsb=wsb, gi=gi: t.matmul(
                        pp[:, 0:GT], wsb[:, c * 1024 + oc * 128: c * 1024 + (oc + 1) * 128],
                        ufm_g[:, c * GT: c * GT + GT],
                        start=(c == 0), stop=(c == nd - 1)),
                        reads=[wsb.K, ufm_g.K], writes=[pp.K])
                r.op("scalar", lambda s, oc=oc, pp=pp, dst=dst: s.activation(
                    dst[:, oc * GT:(oc + 1) * GT], pp[:, 0:GT], AF.Silu, bias=0.0, scale=1.0),
                    reads=[pp.K], writes=[dst.k(oc * GT, (oc + 1) * GT)])
        prev_g = None
        for il in range(ntg):
            g_ = do_chunk(gi, il, cur)
            next(g_)
            if prev_g is not None:
                for _ in prev_g:
                    pass
            prev_g = g_
            cur = 1 - cur
        for _ in prev_g:
            pass
        for il in range(ntg):
            i = gi * ntg + il
            for dh in range(2):
                PP = POP[dh]
                for oc in range(8):
                    r.op("tensor", lambda t, il=il, dh=dh, oc=oc, PP=PP: t.matmul(
                        PP[:, 0:512], ofm_g[:, oc * GT + il * 128: oc * GT + (il + 1) * 128],
                        wo_sb[:, oc * 1024 + dh * 512: oc * 1024 + dh * 512 + 512],
                        start=(oc == 0), stop=(oc == 7)),
                        reads=[ofm_g.K, wo_sb.K], writes=[PP.K])
                rk = R.k(i * D + dh * 512, i * D + dh * 512 + 512)
                r.op("vector", lambda v, i=i, dh=dh, PP=PP: v.tensor_tensor(
                    R[:, i * D + dh * 512: i * D + dh * 512 + 512],
                    R[:, i * D + dh * 512: i * D + dh * 512 + 512], PP[:, 0:512], ALU.add),
                    reads=[PP.K, rk], writes=[rk])
    A.release()


from concourse.bass_utils import run_bass_kernel_spmd

_N_CORES = 8

_L0_ATTN_KEYS = ["L0_attn_Wq", "L0_attn_Wk", "L0_attn_Wv", "L0_attn_Wf",
                 "L0_attn_g_norm_w", "L0_attn_Wo"]
_L0_FFN_KEYS = ["L0_ffn_Wq", "L0_ffn_Wk", "L0_ffn_Ws", "L0_ffn_Wd",
                "L0_ffn_Wv", "L0_ffn_Wo"]
_L1_ATTN_KEYS = ["L1_attn_wq", "L1_attn_bq", "L1_attn_wk", "L1_attn_bk",
                 "L1_attn_wv", "L1_attn_bv", "L1_attn_wo", "L1_attn_bo"]
_L1_FFN_KEYS = ["L1_ffn_gate_w", "L1_ffn_w1", "L1_ffn_w2",
                "L1_ffn_sfc1", "L1_ffn_sfc2", "L1_ffn_sfc3"]

_WSHAPES = {
    "L0_attn_Wq": [1024, 1024], "L0_attn_Wk": [1024, 1024], "L0_attn_Wv": [1024, 1024],
    "L0_attn_Wf": [1024, 256], "L0_attn_g_norm_w": [256], "L0_attn_Wo": [1024, 1024],
    "L0_ffn_Wq": [1024, 512], "L0_ffn_Wk": [1024, 512], "L0_ffn_Ws": [1024, 40],
    "L0_ffn_Wd": [1024, 40], "L0_ffn_Wv": [5, 1024, 64], "L0_ffn_Wo": [5, 64, 1024],
    "L1_attn_wq": [16, 1024, 64], "L1_attn_bq": [16, 1, 64], "L1_attn_wk": [16, 1024, 64],
    "L1_attn_bk": [16, 1, 64], "L1_attn_wv": [16, 1024, 64], "L1_attn_bv": [16, 1, 64],
    "L1_attn_bo": [1024],
    "wot": [1024, 1024],
    "gate_wt": [1024, 16], "w1t": [16, 1024, 512], "w2t": [16, 512, 1024],
    "sfc1t": [1024, 2048], "sfc2t": [1024, 2048], "sfc3t": [2048, 1024],
}
_ALL_KEYS = _L0_ATTN_KEYS + _L0_FFN_KEYS + _L1_ATTN_KEYS + _L1_FFN_KEYS


def build_fused(nseq=2, T=2048, D=1024):
    nc = bass.Bass("TRN2", target_bir_lowering=False)
    x = nc.dram_tensor("x", [nseq, T, D], F32, kind="ExternalInput").ap()
    y = nc.dram_tensor("y", [nseq, T, D], F32, kind="ExternalOutput").ap()
    waps = {n: nc.dram_tensor(n, list(sh), F32, kind="ExternalInput").ap()
            for n, sh in _WSHAPES.items()}
    short = lambda keys, pre: {k[len(pre):]: waps[k] for k in keys if k in waps}
    w0a = short(_L0_ATTN_KEYS, "L0_attn_")
    w0f = short(_L0_FFN_KEYS, "L0_ffn_")
    w1a = short(_L1_ATTN_KEYS, "L1_attn_")
    w1a["wot"] = waps["wot"]
    w1f = {"gate_wt": waps["gate_wt"], "w1t": waps["w1t"], "w2t": waps["w2t"],
           "sfc1t": waps["sfc1t"], "sfc2t": waps["sfc2t"], "sfc3t": waps["sfc3t"]}
    es = ExitStack()
    with es:
        rec = Rec(nc)
        ctx = Ctx(nc, rec, es)
        ctx.setup_constants()
        nt = T // 128
        R = ctx.sb(nt * D, F32)
        for s in range(nseq):
            for i in range(nt):
                rec.op("sync", lambda sy, s=s, i=i: sy.dma_start(
                    out=R[:, i * D:(i + 1) * D], in_=x[s, i * 128:(i + 1) * 128, :]),
                    dma=True, writes=[R.k(i * D, (i + 1) * D)])
            emit_l0_attn(ctx, R, w0a)
            emit_l0_ffn(ctx, R, w0f)
            emit_l1_attn(ctx, R, w1a)
            emit_l1_ffn(ctx, R, w1f)
            for i in range(nt):
                rec.op("sync", lambda sy, s=s, i=i: sy.dma_start(
                    out=y[s, i * 128:(i + 1) * 128, :], in_=R[:, i * D:(i + 1) * D]),
                    dma=True, reads=[R.k(i * D, (i + 1) * D)])
        rec.barrier()
        rec.build(es)
        assert rec.check(), "schedulability check failed"
        with nc.Block() as block:
            rec.emit(block)
    return nc, rec


_fused_cache = {}


def kernel(**inputs):
    x = np.ascontiguousarray(inputs["L0_attn_x"], dtype=np.float32)
    B = x.shape[0]
    per = B // _N_CORES
    if per not in _fused_cache:
        _fused_cache[per] = build_fused(nseq=per)
    nc, rec = _fused_cache[per]
    wmap = {k: np.ascontiguousarray(inputs[k], dtype=np.float32) for k in _ALL_KEYS
            if k in _WSHAPES}
    wmap["wot"] = np.ascontiguousarray(np.float32(inputs["L1_attn_wo"]).T)
    wmap["gate_wt"] = np.ascontiguousarray(np.float32(inputs["L1_ffn_gate_w"]).T)
    wmap["w1t"] = np.ascontiguousarray(np.float32(inputs["L1_ffn_w1"]).transpose(0, 2, 1))
    wmap["w2t"] = np.ascontiguousarray(np.float32(inputs["L1_ffn_w2"]).transpose(0, 2, 1))
    wmap["sfc1t"] = np.ascontiguousarray(np.float32(inputs["L1_ffn_sfc1"]).T)
    wmap["sfc2t"] = np.ascontiguousarray(np.float32(inputs["L1_ffn_sfc2"]).T)
    wmap["sfc3t"] = np.ascontiguousarray(np.float32(inputs["L1_ffn_sfc3"]).T)
    in_maps = [{"x": np.ascontiguousarray(x[c * per:(c + 1) * per]), **wmap}
               for c in range(_N_CORES)]
    res = run_bass_kernel_spmd(nc, in_maps, core_ids=list(range(_N_CORES)))
    return np.concatenate([np.asarray(res.results[c]["y"], dtype=np.float32)
                           for c in range(_N_CORES)], axis=0)
